# version 03
# baseline (speedup 1.0000x reference)
import sys

sys.path.insert(0, "/opt/trn_rl_repo")
from contextlib import ExitStack

import numpy as np
import ml_dtypes
import concourse.bass as bass
import concourse.mybir as mybir
from concourse.bass_utils import run_bass_kernel_spmd

F32 = mybir.dt.float32
BF16 = mybir.dt.bfloat16
AF = mybir.ActivationFunctionType
ALU = mybir.AluOpType
BF = ml_dtypes.bfloat16

B, T, D = 2, 8192, 1024
S = B * T
H, DK, DV = 16, 64, 128
NCORE = 8
Q = T // 4
CH = 128
NCH = T // CH

ENGINES = ("sync", "scalar", "vector", "tensor", "gpsimd")


def f32(a):
    return np.ascontiguousarray(a, dtype=np.float32)


def cbf(a):
    return np.ascontiguousarray(np.asarray(a).astype(BF))


class Prog:

    N_DMA_SEMS = 12

    def __init__(self, nc, ctx):
        self.nc = nc
        self.items = {e: [] for e in ENGINES}
        E = ctx.enter_context
        self.eng_sem = {e: E(nc.semaphore(f"seq_{e}")) for e in ENGINES}
        self.eng_cnt = {e: 0 for e in ENGINES}
        self.dma_sems = [E(nc.semaphore(f"dma{i}")) for i in range(self.N_DMA_SEMS)]
        self.dma_cnt = [0] * self.N_DMA_SEMS
        self.dma_rr = 0

    def wait(self, eng, tok):
        if tok is not None:
            self.items[eng].append(("w", tok))

    def waits(self, eng, toks):
        for t in toks:
            self.wait(eng, t)

    def op(self, eng, method, *args, waits=(), inc=None, **kwargs):
        for t in waits:
            self.wait(eng, t)
        self.eng_cnt[eng] += 1
        tok = (self.eng_sem[eng], self.eng_cnt[eng])
        incs = [(self.eng_sem[eng], 1)]
        if inc:
            incs.append(inc)
        self.items[eng].append(("c", method, args, kwargs, incs))
        return tok

    def dma(self, eng, out, in_, waits=(), **kwargs):
        slot = self.dma_rr
        self.dma_rr = (self.dma_rr + 1) % self.N_DMA_SEMS
        sem = self.dma_sems[slot]
        if self.dma_cnt[slot] > 0:
            self.wait(eng, (sem, 16 * self.dma_cnt[slot]))
        for t in waits:
            self.wait(eng, t)
        self.dma_cnt[slot] += 1
        tok = (sem, 16 * self.dma_cnt[slot])
        self.items[eng].append(("c", "dma_start", (), {"out": out, "in_": in_, **kwargs},
                               [(sem, 16)]))
        return tok

    def barrier_tokens(self):
        toks = [(self.eng_sem[e], self.eng_cnt[e]) for e in ENGINES if self.eng_cnt[e]]
        toks += [(self.dma_sems[i], 16 * self.dma_cnt[i])
                 for i in range(self.N_DMA_SEMS) if self.dma_cnt[i]]
        return toks

    def finalize(self):
        nc = self.nc
        with nc.Block() as block:
            def mk(eng_name):
                def body(eng):
                    for it in self.items[eng_name]:
                        if it[0] == "w":
                            eng.wait_ge(it[1][0], it[1][1])
                        else:
                            _, method, args, kwargs, incs = it
                            instr = getattr(eng, method)(*args, **kwargs)
                            for (sem, n) in incs:
                                instr.then_inc(sem, n)
                return body
            block.sync(mk("sync"))
            block.scalar(mk("scalar"))
            block.vector(mk("vector"))
            block.tensor(mk("tensor"))
            block.gpsimd(mk("gpsimd"))




class PsumPool:

    def __init__(self, nc, ctx, n_f32=7, bf16_banks=1):
        E = ctx.enter_context
        self.banks = []
        for i in range(n_f32):
            t = E(nc.psum_tensor(f"ps{i}", [128, 512], F32))
            self.banks.append({"t": t, "dtype": F32, "readers": [], "busy_until": None})
        for i in range(bf16_banks):
            t = E(nc.psum_tensor(f"psb{i}", [128, 1024], BF16))
            self.banks.append({"t": t, "dtype": BF16, "readers": [], "busy_until": None})
        self.rr = 0

    def acquire(self, dtype=F32, pin=False):
        n = len(self.banks)
        for k in range(n):
            i = (self.rr + k) % n
            if self.banks[i]["dtype"] == dtype and not self.banks[i].get("pinned"):
                self.rr = (i + 1) % n
                b = self.banks[i]
                waits = list(b["readers"])
                if b["busy_until"] is not None:
                    waits.append(b["busy_until"])
                b["readers"] = []
                b["busy_until"] = None
                b["pinned"] = pin
                self._last = i
                return b["t"], waits, i

    def set_writer(self, bank, tok):
        self.banks[bank]["busy_until"] = tok

    def add_reader(self, bank, tok):
        self.banks[bank]["readers"].append(tok)

    def unpin(self, bank):
        self.banks[bank]["pinned"] = False


def new_nc():
    return bass.Bass("TRN2", target_bir_lowering=False, debug=False,
                     detect_race_conditions=False)



def build_launch_A():
    nc = new_nc()
    x = nc.dram_tensor("x", [Q, D], F32, kind="ExternalInput").ap()
    xn = nc.dram_tensor("xn", [Q, D], F32, kind="ExternalOutput").ap()
    xnb = nc.dram_tensor("xnb", [Q, D], BF16, kind="ExternalOutput").ap()
    NT = Q // 128
    with ExitStack() as ctx:
        E = ctx.enter_context
        p = Prog(nc, ctx)
        xt = [E(nc.sbuf_tensor(f"xt{i}", [128, D], F32)) for i in range(2)]
        tr_s = E(nc.sbuf_tensor("tr_s", [128, D], BF16))
        tr_v = E(nc.sbuf_tensor("tr_v", [128, D], BF16))
        xno = [E(nc.sbuf_tensor(f"xno{i}", [128, D], F32)) for i in range(2)]
        xnob = [E(nc.sbuf_tensor(f"xnob{i}", [128, D], BF16)) for i in range(2)]
        ssq = E(nc.sbuf_tensor("ssq", [128, 1], F32))
        ssum = E(nc.sbuf_tensor("ssum", [128, 1], F32))
        mu = E(nc.sbuf_tensor("mu", [128, 1], F32))
        var = E(nc.sbuf_tensor("var", [128, 1], F32))
        sstd = E(nc.sbuf_tensor("sstd", [128, 1], F32))
        rstd = E(nc.sbuf_tensor("rstd", [128, 1], F32))

        t_done = [[], []]
        t_stat = None
        for i in range(NT):
            sl = i % 2
            t_ld = p.dma("sync", xt[sl][:], x[i * 128:(i + 1) * 128, :], waits=t_done[sl])
            w2 = [t_ld] + ([t_stat] if t_stat is not None else [])
            t_sq = p.op("scalar", "activation", waits=w2, out=tr_s[:], in_=xt[sl][:],
                        func=AF.Square, accum_out=ssq[:])
            t_sm = p.op("vector", "tensor_scalar", waits=w2, out=tr_v[:], in0=xt[sl][:],
                        scalar1=1.0, scalar2=None, op0=ALU.mult, op1=ALU.add,
                        accum_out=ssum[:])
            t_mu = p.op("vector", "tensor_scalar", waits=[t_sm], out=mu[:], in0=ssum[:],
                        scalar1=1.0 / D, scalar2=None, op0=ALU.mult)
            t_v1 = p.op("vector", "scalar_tensor_tensor", waits=[t_mu], out=var[:],
                        in0=mu[:], scalar=-1.0, in1=mu[:], op0=ALU.mult, op1=ALU.mult)
            t_v2 = p.op("vector", "scalar_tensor_tensor", waits=[t_v1, t_sq], out=var[:],
                        in0=ssq[:], scalar=1.0 / D, in1=var[:], op0=ALU.mult, op1=ALU.add)
            t_v3 = p.op("vector", "tensor_scalar", waits=[t_v2], out=var[:], in0=var[:],
                        scalar1=1e-5, scalar2=None, op0=ALU.add)
            t_sd = p.op("scalar", "activation", waits=[t_v3], out=sstd[:], in_=var[:],
                        func=AF.Sqrt)
            t_rs = p.op("vector", "reciprocal", waits=[t_sd], out=rstd[:], in_=sstd[:])
            t_xn = p.op("vector", "tensor_scalar", waits=[t_rs], out=xno[sl][:],
                        in0=xt[sl][:], scalar1=mu[:], scalar2=rstd[:],
                        op0=ALU.subtract, op1=ALU.mult)
            t_cb = p.op("vector", "tensor_copy", waits=[t_xn], out=xnob[sl][:],
                        in_=xno[sl][:])
            t_stat = t_xn
            t_s1 = p.dma("gpsimd", xn[i * 128:(i + 1) * 128, :], xno[sl][:], waits=[t_xn])
            t_s2 = p.dma("gpsimd", xnb[i * 128:(i + 1) * 128, :], xnob[sl][:], waits=[t_cb])
            t_done[sl] = [t_s1, t_s2]
        if B_DUMP:
            dumps_b = [("qn0", qn[0]), ("qn1", qn[1]), ("kn0", kn[0]), ("kn1", kn[1]),
                       ("v_tm", v_tm), ("k_tm", k_tm), ("o_blk", o_blk),
                       ("qkraw0", qkraw[0]), ("ggraw0", ggraw[0])]
            off = 0
            for nm, buf in dumps_b:
                w = buf.shape[1]
                p.dma("gpsimd", dbgb[:, off:off + w], buf[:, 0:w], waits=ww(nm) + rw(nm))
                off += w
            dumps_f = [("ab_tm", ab_tm, 64), ("g_part", g_part, 32), ("beta_p", beta_p, 32),
                       ("c_all", c_all, 32), ("A_all", A_all, 32), ("dec_all", dec_all, 32),
                       ("gam_all", gam_all, 32), ("negbA", negbA, 32)]
            off = 0
            for nm, buf, w in dumps_f:
                p.dma("gpsimd", dbgf[:, off:off + w], buf[:, 0:w], waits=ww(nm) + rw(nm))
                off += w
        p.waits("gpsimd", p.barrier_tokens())
        p.finalize()
    return nc





BLOCK = 1024
NBLK = T // BLOCK
BCH = BLOCK // CH
NEG = -1.0e30

B_NBLK = NBLK
B_PH = 9
B_DUMP = False
B_SUB = 5
B_BCHL = BCH


def build_launch_B():
    nc = new_nc()
    ap = lambda n, s, d, k: nc.dram_tensor(n, s, d, kind=k).ap()
    xnb = ap("xnb", [D, T], BF16, "ExternalInput")
    wqk = ap("wqk", [D, 512], F32, "ExternalInput")
    wv4 = ap("wv4", [D, 512], F32, "ExternalInput")
    wg4 = ap("wg4", [D, 512], F32, "ExternalInput")
    wab = ap("wab", [D, 8], F32, "ExternalInput")
    convqk = ap("convqk", [512, 4], F32, "ExternalInput")
    convv = ap("convv", [512, 4], F32, "ExternalInput")
    alogb = ap("alogb", [128, 4], F32, "ExternalInput")
    dtbb = ap("dtbb", [128, 4], F32, "ExternalInput")
    normw = ap("normw", [128, 1], F32, "ExternalInput")
    identb_i = ap("identb", [128, 128], BF16, "ExternalInput")
    identf_i = ap("identf", [128, 128], F32, "ExternalInput")
    triu_i = ap("triu", [128, 128], F32, "ExternalInput")
    maskL2_i = ap("maskL2", [128, 256], BF16, "ExternalInput")
    maskU2_i = ap("maskU2", [128, 256], BF16, "ExternalInput")
    maskUi2_i = ap("maskUi2", [128, 256], BF16, "ExternalInput")
    ones1_i = ap("ones1", [1, 128], F32, "ExternalInput")
    sel_lo_i = ap("sel_lo", [1, 128], F32, "ExternalInput")
    sel_hi_i = ap("sel_hi", [1, 128], F32, "ExternalInput")
    blk2_i = ap("blk2", [128, 2], BF16, "ExternalInput")
    blk2T_i = ap("blk2T", [2, 128], F32, "ExternalInput")
    ones_c_i = ap("ones_c", [128, 1], BF16, "ExternalInput")
    csts_i = ap("csts", [128, 4], F32, "ExternalInput")
    yout = ap("y", [512, T], BF16, "ExternalOutput")
    rows_dram = nc.dram_tensor("rows_dram", [4, 32 * 128], F32).ap()
    if B_DUMP:
        dbgb = ap("dbgb", [128, 18432], BF16, "ExternalOutput")
        dbgf = ap("dbgf", [128, 512], F32, "ExternalOutput")

    with ExitStack() as ctx:
        E = ctx.enter_context
        p = Prog(nc, ctx)
        pool = PsumPool(nc, ctx)
        sb = lambda n, s, d: E(nc.sbuf_tensor(n, s, d))

        xn_sb = [sb(f"xn{i}", [128, 8192], BF16) for i in range(2)]
        wqk_b = sb("wqk_b", [128, 4096], BF16)
        wv4_b = sb("wv4_b", [128, 4096], BF16)
        wg4_b = sb("wg4_b", [128, 4096], BF16)
        wab_b = sb("wab_b", [128, 64], BF16)
        wst = [sb(f"wst{i}", [128, 512], F32) for i in range(2)]
        qkraw = [sb(f"qkraw{i}", [128, BLOCK + 3], BF16) for i in range(4)]
        vraw = [sb(f"vraw{i}", [128, BLOCK + 3], BF16) for i in range(4)]
        qn = [sb(f"qn{i}", [128, BLOCK], BF16) for i in range(2)]
        kn = [sb(f"kn{i}", [128, BLOCK], BF16) for i in range(2)]
        vpost = [sb(f"vpost{i}", [128, BLOCK], BF16) for i in range(4)]
        ggraw = [sb(f"ggraw{i}", [128, BLOCK], BF16) for i in range(4)]
        k_tm = sb("k_tm", [128, 2048], BF16)
        v_tm = sb("v_tm", [128, 4096], BF16)
        o_blk = sb("o_blk", [128, 4096], BF16)
        diag = [sb(f"diag{i}", [128, 128], BF16) for i in range(32)]
        convs = [sb(f"convs{i}", [128, 4], F32) for i in range(8)]
        identb = sb("identb_s", [128, 128], BF16)
        identf = sb("identf_s", [128, 128], F32)
        triu = sb("triu_s", [128, 128], F32)
        maskL2 = sb("maskL2_s", [128, 256], BF16)
        maskU2 = sb("maskU2_s", [128, 256], BF16)
        maskUi2 = sb("maskUi2_s", [128, 256], BF16)
        ones1 = sb("ones1_s", [1, 128], F32)
        sel_lo = sb("sel_lo_s", [1, 128], F32)
        sel_hi = sb("sel_hi_s", [1, 128], F32)
        blk2 = sb("blk2_s", [128, 2], BF16)
        blk2T = sb("blk2T_s", [2, 128], F32)
        ones_c = sb("ones_c_s", [128, 1], BF16)
        csts = sb("csts_s", [128, 4], F32)
        normw_s = sb("normw_s", [128, 1], F32)
        dtb_p = sb("dtb_p", [128, 4], F32)
        gm_p = sb("gm_p", [128, 4], F32)
        dtb_bc = sb("dtb_bc", [128, 32], F32)
        gm_bc = sb("gm_bc", [128, 32], F32)
        ab_tm = sb("ab_tm", [128, 64], F32)
        za = sb("za", [128, 32], F32)
        zt1 = sb("zt1", [128, 32], F32)
        zt2 = sb("zt2", [128, 32], F32)
        zt3 = sb("zt3", [128, 32], F32)
        g_part = sb("g_part", [128, 32], F32)
        beta_p = sb("beta_p", [128, 32], F32)
        lnb_p = sb("lnb_p", [128, 32], F32)
        c_all = sb("c_all", [128, 32], F32)
        cb_all = sb("cb_all", [128, 32], F32)
        negc_all = sb("negc_all", [128, 32], F32)
        A_all = sb("A_all", [128, 32], F32)
        negbA = sb("negbA", [128, 32], F32)
        dec_all = sb("dec_all", [128, 32], F32)
        gam_all = sb("gam_all", [128, 32], F32)
        c127 = sb("c127", [1, 32], F32)
        A127 = sb("A127", [1, 32], F32)
        darg = sb("darg", [128, 32], F32)
        stg = [sb(f"stg{i}", [1, 2048], F32) for i in range(2)]
        rowst = sb("rowst", [32, 128], F32)
        Ub = [sb(f"Ub{i}", [128, 256], BF16) for i in range(2)]
        Lb = [sb(f"Lb{i}", [128, 256], BF16) for i in range(2)]
        Db = [sb(f"Db{i}", [128, 256], BF16) for i in range(2)]
        expL = sb("expL", [128, 256], BF16)
        expU = sb("expU", [128, 256], BF16)
        expP = sb("expP", [128, 256], BF16)
        PT = sb("PT", [128, 256], BF16)
        bV = sb("bV", [128, 256], BF16)
        Qs = sb("Qs", [128, 128], BF16)
        Kp = sb("Kp", [128, 128], BF16)
        Sf = [sb(f"Sf{i}", [128, 128], F32) for i in range(2)]
        Sbf = [sb(f"Sbf{i}", [128, 128], BF16) for i in range(2)]
        sq_t = sb("sq_t", [128, 512], BF16)
        ssqe = sb("ssqe", [2, 512], F32)
        lr2 = sb("lr2", [2, 512], F32)
        rr2 = sb("rr2", [2, 512], F32)
        sqb = sb("sqb", [128, 1024], BF16)
        silb = sb("silb", [128, 1024], BF16)
        ssn = sb("ssn", [1, 512], F32)
        lnn = sb("lnn", [1, 512], F32)
        rsn = sb("rsn", [1, 512], F32)
        ytmp = sb("ytmp", [128, 512], BF16)

        st_w, st_r = {}, {}

        def _merge(dct, tok):
            sem, val = tok
            key = id(sem)
            if key not in dct or dct[key][1] < val:
                dct[key] = (sem, val)

        def setw(name, tok, keep=False):
            if not keep or name not in st_w:
                st_w[name] = {}
            if not keep:
                st_r[name] = {}
            _merge(st_w[name], tok)

        def addw(name, tok):
            setw(name, tok, keep=True)

        def addr(name, tok):
            d = st_r.setdefault(name, {})
            _merge(d, tok)

        def rw(name):
            return list(st_w.get(name, {}).values())

        def ww(name):
            return list(st_r.get(name, {}).values()) + rw(name)

        evac_tog = [0]

        def evac(dst_ap, src_ap, waits, bank=None, eng=None):
            if eng is None:
                eng = ("vector", "scalar")[evac_tog[0]]
                evac_tog[0] ^= 1
            if eng == "vector":
                t = p.op("vector", "tensor_copy", waits=waits, out=dst_ap, in_=src_ap)
            else:
                t = p.op("scalar", "copy", waits=waits, out=dst_ap, in_=src_ap)
            if bank is not None:
                pool.add_reader(bank, t)
            return t

        tc = []
        for apx, buf in ((identb_i, identb), (identf_i, identf), (triu_i, triu),
                         (maskL2_i, maskL2), (maskU2_i, maskU2), (maskUi2_i, maskUi2),
                         (ones1_i, ones1), (sel_lo_i, sel_lo), (sel_hi_i, sel_hi),
                         (blk2_i, blk2), (blk2T_i, blk2T), (ones_c_i, ones_c),
                         (normw, normw_s), (csts_i, csts), (dtbb, dtb_p), (alogb, gm_p)):
            tc.append(p.dma("sync", buf[:], apx[:, :] if len(apx.shape) == 2 else apx[:]))
        t_identb = tc[0]
        for (wsrc, wdst) in ((wqk, wqk_b), (wv4, wv4_b), (wg4, wg4_b)):
            for f in range(8):
                sl = f % 2
                t_l = p.dma("sync", wst[sl][:], wsrc[f * 128:(f + 1) * 128, :], waits=ww(f"wst{sl}"))
                setw(f"wst{sl}", t_l)
                t_c = p.op("vector", "tensor_copy", waits=[t_l],
                           out=wdst[:, f * 512:(f + 1) * 512], in_=wst[sl][:])
                addr(f"wst{sl}", t_c)
                tc.append(t_c)
        for f in range(8):
            sl = f % 2
            t_l = p.dma("sync", wst[sl][:, 0:8], wab[f * 128:(f + 1) * 128, :], waits=ww(f"wst{sl}"))
            setw(f"wst{sl}", t_l)
            t_c = p.op("vector", "tensor_copy", waits=[t_l],
                       out=wab_b[:, f * 8:(f + 1) * 8], in_=wst[sl][:, 0:8])
            addr(f"wst{sl}", t_c)
            tc.append(t_c)
        for g in range(8):
            srcw = convqk if g < 4 else convv
            row0 = (g % 4) * 128
            t_l = p.dma("sync", convs[g][:], srcw[row0:row0 + 128, :])
            for tap in range(4):
                t_d = p.op("vector", "tensor_scalar", waits=[t_l, t_identb],
                           out=diag[g * 4 + tap][:], in0=identb[:],
                           scalar1=convs[g][:, tap:tap + 1], scalar2=None, op0=ALU.mult)
                tc.append(t_d)
        t_gm1 = p.op("scalar", "activation", waits=[tc[15]], out=gm_p[:], in_=gm_p[:], func=AF.Exp)
        t_gm2 = p.op("vector", "tensor_scalar", waits=[t_gm1], out=gm_p[:], in0=gm_p[:],
                     scalar1=-1.0, scalar2=None, op0=ALU.mult)
        for c in range(8):
            t1 = p.op("vector", "tensor_copy", waits=[tc[14]], out=dtb_bc[:, c:32:8], in_=dtb_p[:])
            t2 = p.op("vector", "tensor_copy", waits=[t_gm2], out=gm_bc[:, c:32:8], in_=gm_p[:])
            setw("dtb_bc", t1, keep=(c > 0))
            setw("gm_bc", t2, keep=(c > 0))
        for i in range(2):
            t = p.op("vector", "memset", Sf[i][:], 0.0)
            setw(f"Sf{i}", t)
            t = p.op("vector", "memset", Sbf[i][:], 0.0)
            setw(f"Sbf{i}", t)
        for i in range(4):
            t = p.op("vector", "memset", qkraw[i][:, 0:3], 0.0)
            setw(f"qkraw{i}", t)
            t = p.op("vector", "memset", vraw[i][:, 0:3], 0.0)
            setw(f"vraw{i}", t)
        w_all_consts = list(tc) + [t_gm2] + rw("dtb_bc") + rw("gm_bc")

        for bi in range(B_NBLK):
            t0 = bi * BLOCK
            xs = xn_sb[bi % 2]
            xname = f"xn{bi % 2}"
            src3 = xnb.rearrange("(f p) t -> p f t", p=128)
            xw = ww(xname)
            t_x0 = p.dma("sync", xs[:].rearrange("p (f t) -> p f t", f=8)[:, :, 0:512],
                         src3[:, :, t0:t0 + 512], waits=xw)
            t_x1 = p.dma("sync", xs[:].rearrange("p (f t) -> p f t", f=8)[:, :, 512:1024],
                         src3[:, :, t0 + 512:t0 + 1024], waits=list(xw))
            setw(xname, t_x0)
            addw(xname, t_x1)
            t_xloads = [t_x0, t_x1]

            for tt in range(2):
                toff = tt * 512
                for grp in range(12):
                    wsel, gi = (wqk_b, grp) if grp < 4 else \
                        ((wv4_b, grp - 4) if grp < 8 else (wg4_b, grp - 8))
                    if grp < 4:
                        dname, dst = f"qkraw{gi}", qkraw[gi][:, 3 + toff:3 + toff + 512]
                    elif grp < 8:
                        dname, dst = f"vraw{gi}", vraw[gi][:, 3 + toff:3 + toff + 512]
                    else:
                        dname, dst = f"ggraw{gi}", ggraw[gi][:, toff:toff + 512]
                    bank_t, wts, bk = pool.acquire()
                    mw = wts + [t_xloads[tt]] + (w_all_consts if bi == 0 and tt == 0 and grp == 0 else [])
                    tm = None
                    for f in range(8):
                        tm = p.op("tensor", "matmul", waits=(mw if f == 0 else []),
                                  out=bank_t[:],
                                  lhsT=wsel[:, f * 512 + gi * 128: f * 512 + gi * 128 + 128],
                                  rhs=xs[:, f * 1024 + toff:f * 1024 + toff + 512],
                                  start=(f == 0), stop=(f == 7))
                    pool.set_writer(bk, tm)
                    te = evac(dst, bank_t[:], [tm] + ww(dname), bank=bk)
                    setw(dname, te, keep=(tt == 1))
                bank_t, wts, bk = pool.acquire()
                tm = None
                for sub in range(4):
                    for f in range(8):
                        tm = p.op("tensor", "matmul",
                                  waits=(wts + [t_xloads[tt]] if (sub == 0 and f == 0) else []),
                                  out=bank_t[:, sub * 8:(sub + 1) * 8],
                                  lhsT=xs[:, f * 1024 + toff + sub * 128: f * 1024 + toff + sub * 128 + 128],
                                  rhs=wab_b[:, f * 8:(f + 1) * 8],
                                  start=(f == 0), stop=(f == 7))
                pool.set_writer(bk, tm)
                abw = ww("ab_tm")
                for sub in range(4):
                    sidx = tt * 4 + sub
                    te = evac(ab_tm[:, sidx:64:8], bank_t[:, sub * 8:(sub + 1) * 8],
                              [tm] + abw, bank=bk, eng="vector")
                    setw("ab_tm", te, keep=(sidx > 0))

            for g in range(8 if B_PH >= 2 else 0):
                rawb = qkraw[g] if g < 4 else vraw[g - 4]
                rname = f"qkraw{g}" if g < 4 else f"vraw{g - 4}"
                for tt in range(2):
                    toff = tt * 512
                    bank_t, wts, bk = pool.acquire()
                    tm = None
                    for tap in range(4):
                        tm = p.op("tensor", "matmul", waits=(wts + rw(rname) if tap == 0 else []),
                                  out=bank_t[:], lhsT=diag[g * 4 + tap][:],
                                  rhs=rawb[:, tap + toff: tap + toff + 512],
                                  start=(tap == 0), stop=(tap == 3))
                    pool.set_writer(bk, tm)
                    if g < 4:
                        dst, dname = (qn, "qn") if g < 2 else (kn, "kn")
                        gi = g % 2
                        te = p.op("scalar", "activation", waits=[tm] + ww(f"{dname}{gi}"),
                                  out=dst[gi][:, toff:toff + 512], in_=bank_t[:], func=AF.Silu)
                        setw(f"{dname}{gi}", te, keep=(tt == 1))
                    else:
                        gi = g - 4
                        te = p.op("scalar", "activation", waits=[tm] + ww(f"vpost{gi}"),
                                  out=vpost[gi][:, toff:toff + 512], in_=bank_t[:], func=AF.Silu)
                        setw(f"vpost{gi}", te, keep=(tt == 1))
                    pool.add_reader(bk, te)
                    addr(rname, te)

            for g in range(4 if B_PH >= 3 else 0):
                buf, bname = (qn[g], f"qn{g}") if g < 2 else (kn[g - 2], f"kn{g - 2}")
                for tt in range(2):
                    toff = tt * 512
                    t_sq = p.op("scalar", "activation", waits=rw(bname) + ww("sq_t"),
                                out=sq_t[:], in_=buf[:, toff:toff + 512], func=AF.Square)
                    setw("sq_t", t_sq)
                    bank_t, wts, bk = pool.acquire()
                    tm = p.op("tensor", "matmul", waits=wts + [t_sq], out=bank_t[0:2, :],
                              lhsT=blk2[:], rhs=sq_t[:], start=True, stop=True)
                    pool.set_writer(bk, tm)
                    addr("sq_t", tm)
                    t_eps = p.op("vector", "tensor_scalar", waits=[tm] + ww("ssqe"),
                                 out=ssqe[:], in0=bank_t[0:2, :], scalar1=1e-6,
                                 scalar2=None, op0=ALU.add)
                    pool.add_reader(bk, t_eps)
                    setw("ssqe", t_eps)
                    t_ln = p.op("scalar", "activation", waits=[t_eps] + ww("lr2"),
                                out=lr2[:], in_=ssqe[:], func=AF.Ln)
                    setw("lr2", t_ln)
                    addr("ssqe", t_ln)
                    t_r = p.op("scalar", "activation", waits=[t_ln] + ww("rr2"),
                               out=rr2[:], in_=lr2[:], func=AF.Exp,
                               scale=csts[0:2, 0:1],
                               bias=(csts[0:2, 1:2] if g < 2 else 0.0))
                    setw("rr2", t_r)
                    addr("lr2", t_r)
                    bank_t2, wts2, bk2 = pool.acquire()
                    tm2 = p.op("tensor", "matmul", waits=wts2 + [t_r], out=bank_t2[:],
                               lhsT=blk2T[:], rhs=rr2[:], start=True, stop=True)
                    pool.set_writer(bk2, tm2)
                    addr("rr2", tm2)
                    t_fin = p.op("vector", "tensor_tensor", waits=[tm2] + rw(bname),
                                 out=buf[:, toff:toff + 512], in0=buf[:, toff:toff + 512],
                                 in1=bank_t2[:], op=ALU.mult)
                    pool.add_reader(bk2, t_fin)
                    addw(bname, t_fin)

            ktw = ww("k_tm")
            vtw = ww("v_tm")
            for c in range(BCH if B_PH >= 4 else 0):
                for pr in range(2):
                    bank_t, wts, bk = pool.acquire(BF16)
                    tt1 = p.op("tensor", "transpose", waits=wts + rw(f"kn{pr}"),
                               out=bank_t[:, 0:128], in_=kn[pr][:, c * 128:(c + 1) * 128],
                               identity=identb[:])
                    pool.set_writer(bk, tt1)
                    addr(f"kn{pr}", tt1)
                    dsl = k_tm[:, (c * 2 + pr) * 128:(c * 2 + pr) * 128 + 128]
                    te = evac(dsl, bank_t[:, 0:128], [tt1] + ktw, bank=bk)
                    setw("k_tm", te, keep=(c > 0 or pr > 0))
                for g in range(4):
                    bank_t, wts, bk = pool.acquire(BF16)
                    tt1 = p.op("tensor", "transpose", waits=wts + rw(f"vpost{g}"),
                               out=bank_t[:, 0:128], in_=vpost[g][:, c * 128:(c + 1) * 128],
                               identity=identb[:])
                    pool.set_writer(bk, tt1)
                    addr(f"vpost{g}", tt1)
                    dsl = v_tm[:, (c * 4 + g) * 128:(c * 4 + g) * 128 + 128]
                    te = evac(dsl, bank_t[:, 0:128], [tt1] + vtw, bank=bk)
                    setw("v_tm", te, keep=(c > 0 or g > 0))

            if B_PH < 5:
                continue
            t1 = p.op("vector", "tensor_tensor", waits=rw("ab_tm") + rw("dtb_bc") + ww("za"),
                      out=za[:], in0=ab_tm[:, 0:32], in1=dtb_bc[:], op=ALU.add)
            setw("za", t1)
            addr("ab_tm", t1)
            t2 = p.op("scalar", "activation", waits=[t1] + ww("zt1"), out=zt1[:], in_=za[:], func=AF.Abs)
            setw("zt1", t2)
            t3 = p.op("vector", "tensor_scalar", waits=[t2], out=zt1[:], in0=zt1[:],
                      scalar1=-1.0, scalar2=None, op0=ALU.mult)
            setw("zt1", t3)
            t4 = p.op("scalar", "activation", waits=[t3] + ww("zt2"), out=zt2[:], in_=zt1[:], func=AF.Exp)
            setw("zt2", t4)
            addr("zt1", t4)
            t5 = p.op("vector", "tensor_scalar", waits=[t4], out=zt2[:], in0=zt2[:],
                      scalar1=1.0, scalar2=None, op0=ALU.add)
            setw("zt2", t5)
            t6 = p.op("scalar", "activation", waits=[t5], out=zt2[:], in_=zt2[:], func=AF.Ln)
            setw("zt2", t6)
            t7 = p.op("scalar", "activation", waits=[t1] + ww("zt3"), out=zt3[:], in_=za[:], func=AF.Relu)
            setw("zt3", t7)
            addr("za", t7)
            t8 = p.op("vector", "tensor_tensor", waits=[t6, t7], out=zt3[:], in0=zt3[:],
                      in1=zt2[:], op=ALU.add)
            setw("zt3", t8)
            addr("zt2", t8)
            t9 = p.op("vector", "tensor_tensor", waits=[t8] + rw("gm_bc") + ww("g_part"),
                      out=g_part[:], in0=zt3[:], in1=gm_bc[:], op=ALU.mult)
            setw("g_part", t9)
            addr("zt3", t9)
            tb1 = p.op("scalar", "activation", waits=rw("ab_tm") + ww("beta_p"),
                       out=beta_p[:], in_=ab_tm[:, 32:64], func=AF.Sigmoid)
            addr("ab_tm", tb1)
            tb2 = p.op("vector", "tensor_scalar", waits=[tb1], out=beta_p[:], in0=beta_p[:],
                       scalar1=2.0, scalar2=None, op0=ALU.mult)
            setw("beta_p", tb2)
            tb3 = p.op("scalar", "activation", waits=[tb2] + ww("lnb_p"), out=lnb_p[:],
                       in_=beta_p[:], func=AF.Ln)
            setw("lnb_p", tb3)
            addr("beta_p", tb3)

            if B_PH < 6:
                continue
            bank_t, wts, bk = pool.acquire()
            tm = p.op("tensor", "matmul", waits=wts + rw("g_part"), out=bank_t[:, 0:32],
                      lhsT=triu[:], rhs=g_part[:], start=True, stop=True)
            pool.set_writer(bk, tm)
            addr("g_part", tm)
            t_c = evac(c_all[:], bank_t[:, 0:32], [tm] + ww("c_all"), bank=bk)
            setw("c_all", t_c)
            t_cb = p.op("vector", "tensor_tensor", waits=rw("c_all") + rw("lnb_p") + ww("cb_all"),
                        out=cb_all[:], in0=c_all[:], in1=lnb_p[:], op=ALU.add)
            setw("cb_all", t_cb)
            addr("c_all", t_cb)
            addr("lnb_p", t_cb)
            t_nc = p.op("vector", "tensor_scalar", waits=rw("c_all") + ww("negc_all"),
                        out=negc_all[:], in0=c_all[:], scalar1=-1.0, scalar2=None, op0=ALU.mult)
            setw("negc_all", t_nc)
            addr("c_all", t_nc)
            t_A = p.op("scalar", "activation", waits=rw("c_all") + ww("A_all"),
                       out=A_all[:], in_=c_all[:], func=AF.Exp)
            setw("A_all", t_A)
            addr("c_all", t_A)
            t_nbA = p.op("vector", "scalar_tensor_tensor", waits=[t_A] + rw("beta_p") + ww("negbA"),
                         out=negbA[:], in0=A_all[:], scalar=-1.0, in1=beta_p[:],
                         op0=ALU.mult, op1=ALU.mult)
            setw("negbA", t_nbA)
            addr("A_all", t_nbA)
            addr("beta_p", t_nbA)
            t_c127 = p.dma("gpsimd", c127[:], c_all[127:128, :], waits=rw("c_all") + ww("c127"))
            setw("c127", t_c127)
            addr("c_all", t_c127)
            t_A127 = p.op("scalar", "activation", waits=[t_c127] + ww("A127"), out=A127[:],
                          in_=c127[:], func=AF.Exp)
            setw("A127", t_A127)
            addr("c127", t_A127)
            bank_t, wts, bk = pool.acquire()
            tm = p.op("tensor", "matmul", waits=wts + [t_A127], out=bank_t[:, 0:32],
                      lhsT=ones1[:], rhs=A127[:], start=True, stop=True)
            pool.set_writer(bk, tm)
            addr("A127", tm)
            t_gam = evac(gam_all[:], bank_t[:, 0:32], [tm] + ww("gam_all"), bank=bk)
            setw("gam_all", t_gam)
            bank_t, wts, bk = pool.acquire()
            tm = p.op("tensor", "matmul", waits=wts + rw("c127"), out=bank_t[:, 0:32],
                      lhsT=ones1[:], rhs=c127[:], start=True, stop=True)
            pool.set_writer(bk, tm)
            addr("c127", tm)
            t_da = p.op("vector", "scalar_tensor_tensor", waits=[tm] + rw("c_all") + ww("darg"),
                        out=darg[:], in0=c_all[:], scalar=-1.0, in1=bank_t[:, 0:32],
                        op0=ALU.mult, op1=ALU.add)
            pool.add_reader(bk, t_da)
            setw("darg", t_da)
            addr("c_all", t_da)
            t_dec = p.op("scalar", "activation", waits=[t_da] + ww("dec_all"), out=dec_all[:],
                         in_=darg[:], func=AF.Exp)
            setw("dec_all", t_dec)
            addr("darg", t_dec)
            for idx, (nm, srcb, srcn) in enumerate((("c", c_all, "c_all"),
                                                    ("negc", negc_all, "negc_all"),
                                                    ("cb", cb_all, "cb_all"),
                                                    ("A", A_all, "A_all"))):
                bank_t, wts, bk = pool.acquire()
                ttr = p.op("tensor", "transpose", waits=wts + rw(srcn),
                           out=bank_t[0:32, 0:128], in_=srcb[:], identity=identf[:])
                pool.set_writer(bk, ttr)
                addr(srcn, ttr)
                te = evac(rowst[:], bank_t[0:32, 0:128], [ttr] + ww("rowst"), bank=bk)
                setw("rowst", te)
                t_st = p.dma("gpsimd", rows_dram[idx:idx + 1, :].rearrange("o (a b) -> (o a) b", a=32),
                             rowst[:], waits=[te] + ww("rows_dram" + nm))
                addr("rowst", t_st)
                setw("rows_dram" + nm, t_st)

            rows_view = rows_dram.rearrange("a (h c t) -> c a h t", h=4, c=8)
            for c in range(min(BCH, B_BCHL) if B_PH >= 7 else 0):
                ssl = c % 2
                t_stg = p.dma("gpsimd", stg[ssl][0:1, :].rearrange("o (a h t) -> o a h t", a=4, h=4),
                              rows_view[c][None],
                              waits=ww(f"stg{ssl}") + [t for nm2 in ("c", "negc", "cb", "A")
                                                      for t in rw("rows_dram" + nm2)])
                setw(f"stg{ssl}", t_stg)
                for nm2 in ("c", "negc", "cb", "A"):
                    addr("rows_dram" + nm2, t_stg)

                def stgs(arr_idx, head):
                    return stg[ssl][0:1, arr_idx * 512 + head * 128: arr_idx * 512 + head * 128 + 128]

                for pr in range(2):
                    colh = [(pr * 2 + hh) * 8 + c for hh in range(2)]
                    kpair, kname = kn[pr], f"kn{pr}"
                    qpair, qname = qn[pr], f"qn{pr}"
                    spr, sname = Sbf[pr], f"Sbf{pr}"
                    tsl = slice(c * 128, (c + 1) * 128)
                    hb_t, hb, mm_done = [], [], []
                    for hh in range(2):
                        rsl = slice(hh * 64, (hh + 1) * 64)
                        bt, wts, bk = pool.acquire()
                        mm1 = p.op("tensor", "matmul",
                                   waits=wts + rw(kname) + rw(qname) + rw(sname),
                                   out=bt[:, 0:128], lhsT=kpair[rsl, tsl], rhs=kpair[rsl, tsl],
                                   start=True, stop=True)
                        mm2 = p.op("tensor", "matmul", out=bt[:, 128:256],
                                   lhsT=kpair[rsl, tsl], rhs=qpair[rsl, tsl],
                                   start=True, stop=True)
                        mm3 = p.op("tensor", "matmul", out=bt[:, 256:384],
                                   lhsT=kpair[rsl, tsl], rhs=spr[rsl, :],
                                   start=True, stop=True)
                        pool.set_writer(bk, mm3)
                        addr(kname, mm3)
                        addr(qname, mm3)
                        addr(sname, mm3)
                        hb_t.append(bt)
                        hb.append(bk)
                        mm_done.append(mm3)
                    bkr_t, wts, bkr = pool.acquire()
                    r1 = p.op("tensor", "matmul", waits=wts + rw(f"stg{ssl}"),
                              out=bkr_t[:, 0:256], lhsT=identb[:], rhs=maskL2[:],
                              start=True, stop=False)
                    r1 = p.op("tensor", "matmul", out=bkr_t[:, 0:256],
                              lhsT=ones1[:], rhs=stg[ssl][0:1, 512 + pr * 256: 768 + pr * 256],
                              start=False, stop=True)
                    r2 = p.op("tensor", "matmul", out=bkr_t[:, 256:512], lhsT=identb[:],
                              rhs=maskU2[:], start=True, stop=False)
                    r2 = p.op("tensor", "matmul", out=bkr_t[:, 256:512],
                              lhsT=ones1[:], rhs=stg[ssl][0:1, 1024 + pr * 256: 1280 + pr * 256],
                              start=False, stop=True)
                    pool.set_writer(bkr, r2)
                    addr(f"stg{ssl}", r2)
                    bk3_t, wts, bk3 = pool.acquire()
                    r3 = p.op("tensor", "matmul", waits=wts + rw(f"stg{ssl}"),
                              out=bk3_t[:, 0:256], lhsT=identb[:], rhs=maskUi2[:],
                              start=True, stop=False)
                    r3 = p.op("tensor", "matmul", out=bk3_t[:, 0:256],
                              lhsT=ones1[:], rhs=stg[ssl][0:1, pr * 256: pr * 256 + 256],
                              start=False, stop=True)
                    ab1 = p.op("tensor", "matmul", out=bk3_t[:, 256:384], lhsT=sel_lo[:],
                               rhs=stgs(3, pr * 2), start=True, stop=False)
                    ab2 = p.op("tensor", "matmul", out=bk3_t[:, 256:384], lhsT=sel_hi[:],
                               rhs=stgs(3, pr * 2 + 1), start=False, stop=True)
                    pool.set_writer(bk3, ab2)
                    addr(f"stg{ssl}", ab2)
                    ew = rw("cb_all") + rw("negc_all")
                    e1 = p.op("scalar", "activation", waits=ew + [r2] + ww("expL"),
                              out=expL[:, 0:128], in_=bkr_t[:, 0:128], func=AF.Exp,
                              bias=cb_all[:, colh[0]:colh[0] + 1])
                    e1b = p.op("scalar", "activation", waits=[e1], out=expL[:, 128:256],
                               in_=bkr_t[:, 128:256], func=AF.Exp,
                               bias=cb_all[:, colh[1]:colh[1] + 1])
                    e2 = p.op("scalar", "activation", waits=[e1b] + ww("expU"),
                              out=expU[:, 0:128], in_=bkr_t[:, 256:384], func=AF.Exp,
                              bias=negc_all[:, colh[0]:colh[0] + 1])
                    e2b = p.op("scalar", "activation", waits=[e2], out=expU[:, 128:256],
                               in_=bkr_t[:, 384:512], func=AF.Exp,
                               bias=negc_all[:, colh[1]:colh[1] + 1])
                    e3 = p.op("scalar", "activation", waits=[ab2] + ww("expP"),
                              out=expP[:, 0:128], in_=bk3_t[:, 0:128], func=AF.Exp,
                              bias=negc_all[:, colh[0]:colh[0] + 1])
                    e3b = p.op("scalar", "activation", waits=[e3], out=expP[:, 128:256],
                               in_=bk3_t[:, 128:256], func=AF.Exp,
                               bias=negc_all[:, colh[1]:colh[1] + 1])
                    setw("expL", e1b)
                    setw("expU", e2b)
                    setw("expP", e3b)
                    addr("cb_all", e1b)
                    addr("negc_all", e3b)
                    pool.add_reader(bkr, e2b)
                    pool.add_reader(bk3, e3b)
                    tbv = None
                    for hh in range(2):
                        hsl = slice(hh * 128, (hh + 1) * 128)
                        tL = p.op("vector", "scalar_tensor_tensor",
                                  waits=[mm_done[hh], e1b] + (ww("Lb0") if hh == 0 else []),
                                  out=Lb[0][:, hsl], in0=hb_t[hh][:, 0:128], scalar=-1.0,
                                  in1=expL[:, hsl], op0=ALU.mult, op1=ALU.mult)
                        setw("Lb0", tL, keep=(hh == 1))
                        tU = p.op("vector", "scalar_tensor_tensor",
                                  waits=[e2b] + (ww("Ub0") if hh == 0 else []),
                                  out=Ub[0][:, hsl], in0=hb_t[hh][:, 0:128], scalar=-1.0,
                                  in1=expU[:, hsl], op0=ALU.mult, op1=ALU.mult)
                        setw("Ub0", tU, keep=(hh == 1))
                        tP = p.op("vector", "tensor_tensor",
                                  waits=[e3b] + (ww("PT") if hh == 0 else []),
                                  out=PT[:, hsl], in0=hb_t[hh][:, 128:256], in1=expP[:, hsl],
                                  op=ALU.mult)
                        setw("PT", tP, keep=(hh == 1))
                        vb = (c * 4 + pr * 2 + hh) * 128
                        tbv = p.op("vector", "tensor_scalar",
                                   waits=rw("v_tm") + rw("beta_p") + (ww("bV") if hh == 0 else []),
                                   out=bV[:, hsl], in0=v_tm[:, vb:vb + 128],
                                   scalar1=beta_p[:, colh[hh]:colh[hh] + 1], scalar2=None,
                                   op0=ALU.mult)
                        setw("bV", tbv, keep=(hh == 1))
                        td = p.op("vector", "scalar_tensor_tensor",
                                  waits=[tbv] + rw("negbA") + (ww("Db0") if hh == 0 else []),
                                  out=Db[0][:, hsl], in0=hb_t[hh][:, 256:384],
                                  scalar=negbA[:, colh[hh]:colh[hh] + 1], in1=bV[:, hsl],
                                  op0=ALU.mult, op1=ALU.add)
                        setw("Db0", td, keep=(hh == 1))
                        pool.add_reader(hb[hh], td)
                    for t_ in rw("Lb0"):
                        addr("expL", t_)
                    addr("v_tm", tbv)
                    addr("beta_p", tbv)
                    addr("negbA", td)
                    tok_U = rw("Ub0")
                    tok_L = rw("Lb0")
                    tok_P = rw("PT")
                    tok_D = rw("Db0")
                    for t_ in tok_U:
                        addr("expU", t_)
                    for t_ in tok_P:
                        addr("expP", t_)
                    if B_SUB < 3:
                        continue
                    cur, nxt = 0, 1
                    for j in range(7):
                        bkd_t, wts, bkd = pool.acquire()
                        md = p.op("tensor", "matmul", waits=wts + tok_D + tok_U,
                                  out=bkd_t[:, 0:256], lhsT=identb[:], rhs=Db[cur][:],
                                  start=True, stop=False)
                        md = p.op("tensor", "matmul", out=bkd_t[:, 0:128],
                                  lhsT=Ub[cur][:, 0:128], rhs=Db[cur][:, 0:128],
                                  start=False, stop=False)
                        md = p.op("tensor", "matmul", out=bkd_t[:, 128:256],
                                  lhsT=Ub[cur][:, 128:256], rhs=Db[cur][:, 128:256],
                                  start=False, stop=True)
                        pool.set_writer(bkd, md)
                        addr(f"Db{cur}", md)
                        addr(f"Ub{cur}", md)
                        te = evac(Db[nxt][:], bkd_t[:, 0:256], [md] + ww(f"Db{nxt}"), bank=bkd)
                        setw(f"Db{nxt}", te)
                        tok_D = [te]
                        if j < 6:
                            bkq_t, wts2, bkq = pool.acquire()
                            q1 = p.op("tensor", "matmul", waits=wts2 + tok_U + tok_L,
                                      out=bkq_t[:, 0:128], lhsT=Lb[cur][:, 0:128],
                                      rhs=Ub[cur][:, 0:128], start=True, stop=True)
                            q2 = p.op("tensor", "matmul", out=bkq_t[:, 128:256],
                                      lhsT=Lb[cur][:, 128:256], rhs=Ub[cur][:, 128:256],
                                      start=True, stop=True)
                            q3 = p.op("tensor", "matmul", out=bkq_t[:, 256:384],
                                      lhsT=Ub[cur][:, 0:128], rhs=Lb[cur][:, 0:128],
                                      start=True, stop=True)
                            q4 = p.op("tensor", "matmul", out=bkq_t[:, 384:512],
                                      lhsT=Ub[cur][:, 128:256], rhs=Lb[cur][:, 128:256],
                                      start=True, stop=True)
                            pool.set_writer(bkq, q4)
                            addr(f"Ub{cur}", q4)
                            addr(f"Lb{cur}", q4)
                            teU = evac(Ub[nxt][:], bkq_t[:, 0:256], [q4] + ww(f"Ub{nxt}"), bank=bkq, eng="scalar")
                            teL = evac(Lb[nxt][:], bkq_t[:, 256:512], [q4] + ww(f"Lb{nxt}"), bank=bkq, eng="scalar")
                            setw(f"Ub{nxt}", teU)
                            setw(f"Lb{nxt}", teL)
                            tok_U, tok_L = [teU], [teL]
                        cur, nxt = nxt, cur
                    if B_SUB < 4:
                        continue
                    bka_t, wts, bka = pool.acquire()
                    a1 = p.op("tensor", "matmul", waits=wts + rw(f"stg{ssl}"),
                              out=bka_t[:, 0:128], lhsT=sel_lo[:],
                              rhs=stgs(3, pr * 2), start=True, stop=False)
                    a2 = p.op("tensor", "matmul", out=bka_t[:, 0:128], lhsT=sel_hi[:],
                              rhs=stgs(3, pr * 2 + 1), start=False, stop=True)
                    pool.set_writer(bka, a2)
                    addr(f"stg{ssl}", a2)
                    tqs = p.op("vector", "tensor_tensor", waits=[a2] + rw(qname) + ww("Qs"),
                               out=Qs[:], in0=qpair[:, tsl], in1=bka_t[:, 0:128], op=ALU.mult)
                    pool.add_reader(bka, tqs)
                    setw("Qs", tqs)
                    addr(qname, tqs)
                    ob_t, ob_bk, ob_tok = [], [], []
                    for hh in range(2):
                        hsl = slice(hh * 128, (hh + 1) * 128)
                        rsl = slice(hh * 64, (hh + 1) * 64)
                        bt, wts, bk = pool.acquire()
                        oo = p.op("tensor", "matmul",
                                  waits=wts + tok_D + tok_P + [tqs] + rw(sname),
                                  out=bt[:, 0:128], lhsT=Db[cur][:, hsl], rhs=PT[:, hsl],
                                  start=True, stop=False)
                        oo = p.op("tensor", "matmul", out=bt[:, 0:128],
                                  lhsT=Sbf[pr][rsl, :], rhs=Qs[rsl, :], start=False, stop=True)
                        pool.set_writer(bk, oo)
                        addr("PT", oo)
                        addr("Qs", oo)
                        addr(sname, oo)
                        addr(f"Db{cur}", oo)
                        ob_t.append(bt)
                        ob_bk.append(bk)
                        ob_tok.append(oo)
                    oview = o_blk[:].rearrange("p (hh t) -> p hh t", hh=4)
                    for hh in range(2):
                        te_o = p.op("vector", "tensor_copy",
                                    waits=[ob_tok[hh]] + (ww("o_blk") if (c == 0 and pr == 0 and hh == 0) else []),
                                    out=oview[:, pr * 2 + hh, c * 128:(c + 1) * 128],
                                    in_=ob_t[hh][:, 0:128])
                        pool.add_reader(ob_bk[hh], te_o)
                        setw("o_blk", te_o, keep=not (c == 0 and pr == 0 and hh == 0))
                    if B_SUB < 5:
                        continue
                    kt0 = (c * 2 + pr) * 128
                    tk0 = p.op("vector", "tensor_scalar", waits=rw("k_tm") + rw("dec_all") + ww("Kp"),
                               out=Kp[:, 0:64], in0=k_tm[:, kt0:kt0 + 64],
                               scalar1=dec_all[:, colh[0]:colh[0] + 1], scalar2=None, op0=ALU.mult)
                    tk1 = p.op("vector", "tensor_scalar", waits=[tk0],
                               out=Kp[:, 64:128], in0=k_tm[:, kt0 + 64:kt0 + 128],
                               scalar1=dec_all[:, colh[1]:colh[1] + 1], scalar2=None, op0=ALU.mult)
                    setw("Kp", tk1)
                    addr("k_tm", tk1)
                    addr("dec_all", tk1)
                    bkst_t, wts, bkst = pool.acquire()
                    sm0 = p.op("tensor", "matmul", waits=wts + [tk1] + tok_D,
                               out=bkst_t[0:64, 0:128], lhsT=Kp[:, 0:64], rhs=Db[cur][:, 0:128],
                               start=True, stop=True)
                    sm1 = p.op("tensor", "matmul", out=bkst_t[64:128, 0:128],
                               lhsT=Kp[:, 64:128], rhs=Db[cur][:, 128:256],
                               start=True, stop=True, tile_position=(0, 64))
                    pool.set_writer(bkst, sm1)
                    addr("Kp", sm1)
                    addr(f"Db{cur}", sm1)
                    su0 = p.op("vector", "scalar_tensor_tensor",
                               waits=[sm1] + ww(f"Sf{pr}") + rw("gam_all"),
                               out=Sf[pr][0:64, :], in0=Sf[pr][0:64, :],
                               scalar=gam_all[0:64, colh[0]:colh[0] + 1], in1=bkst_t[0:64, 0:128],
                               op0=ALU.mult, op1=ALU.add)
                    su1 = p.op("vector", "scalar_tensor_tensor", waits=[su0],
                               out=Sf[pr][64:128, :], in0=Sf[pr][64:128, :],
                               scalar=gam_all[64:128, colh[1]:colh[1] + 1], in1=bkst_t[64:128, 0:128],
                               op0=ALU.mult, op1=ALU.add)
                    setw(f"Sf{pr}", su1)
                    addr("gam_all", su1)
                    pool.add_reader(bkst, su1)
                    tsb = p.op("vector", "tensor_copy", waits=[su1] + ww(f"Sbf{pr}"),
                               out=Sbf[pr][:], in_=Sf[pr][:])
                    setw(f"Sbf{pr}", tsb)
                    addr(f"Sf{pr}", tsb)

            for h in range(4 if B_PH >= 8 else 0):
                ho = h * 1024
                t_sq = p.op("scalar", "activation", waits=rw("o_blk") + ww("sqb"),
                            out=sqb[:], in_=o_blk[:, ho:ho + 1024], func=AF.Square)
                setw("sqb", t_sq)
                addr("o_blk", t_sq)
                t_sil = p.op("scalar", "activation", waits=rw(f"ggraw{h}") + ww("silb"),
                             out=silb[:], in_=ggraw[h][:, 0:1024], func=AF.Silu)
                setw("silb", t_sil)
                addr(f"ggraw{h}", t_sil)
                for half in range(2):
                    hsl = slice(half * 512, (half + 1) * 512)
                    bank_t, wts, bk = pool.acquire()
                    n1 = p.op("tensor", "matmul", waits=wts + [t_sq], out=bank_t[0:1, 0:512],
                              lhsT=ones_c[:], rhs=sqb[:, hsl],
                              start=True, stop=True)
                    pool.set_writer(bk, n1)
                    t_ss = p.op("vector", "tensor_scalar", waits=[n1] + ww("ssn"),
                                out=ssn[:], in0=bank_t[0:1, 0:512],
                                scalar1=1.0 / 128, scalar2=1e-5, op0=ALU.mult, op1=ALU.add)
                    pool.add_reader(bk, t_ss)
                    setw("ssn", t_ss)
                    t_ln = p.op("scalar", "activation", waits=[t_ss] + ww("lnn"),
                                out=lnn[:], in_=ssn[:], func=AF.Ln)
                    setw("lnn", t_ln)
                    addr("ssn", t_ln)
                    t_r = p.op("scalar", "activation", waits=[t_ln] + ww("rsn"),
                               out=rsn[:], in_=lnn[:], func=AF.Exp, scale=csts[0:1, 0:1])
                    setw("rsn", t_r)
                    addr("lnn", t_r)
                    bank_t2, wts2, bk2 = pool.acquire()
                    bb = p.op("tensor", "matmul", waits=wts2 + [t_r], out=bank_t2[:],
                              lhsT=ones1[:], rhs=rsn[0:1, :], start=True, stop=True)
                    pool.set_writer(bk2, bb)
                    addr("rsn", bb)
                    t_y1 = p.op("vector", "scalar_tensor_tensor",
                                waits=[bb] + rw("o_blk") + ww("ytmp"),
                                out=ytmp[:], in0=o_blk[:, ho + half * 512:ho + (half + 1) * 512],
                                scalar=normw_s[:], in1=bank_t2[:], op0=ALU.mult, op1=ALU.mult)
                    pool.add_reader(bk2, t_y1)
                    setw("ytmp", t_y1)
                    addr("o_blk", t_y1)
                    t_y2 = p.op("vector", "tensor_tensor", waits=[t_y1, t_sil],
                                out=ytmp[:], in0=ytmp[:], in1=silb[:, hsl], op=ALU.mult)
                    setw("ytmp", t_y2)
                    addr("silb", t_y2)
                    t_st = p.dma("gpsimd",
                                 yout[h * 128:(h + 1) * 128, t0 + half * 512:t0 + (half + 1) * 512],
                                 ytmp[:], waits=[t_y2])
                    addr("ytmp", t_st)

            if bi < NBLK - 1:
                for i in range(4):
                    t_h = p.op("vector", "tensor_copy", waits=ww(f"qkraw{i}"),
                               out=qkraw[i][:, 0:3], in_=qkraw[i][:, BLOCK:BLOCK + 3])
                    setw(f"qkraw{i}", t_h)
                    t_h2 = p.op("vector", "tensor_copy", waits=ww(f"vraw{i}"),
                                out=vraw[i][:, 0:3], in_=vraw[i][:, BLOCK:BLOCK + 3])
                    setw(f"vraw{i}", t_h2)

        if B_DUMP:
            dumps_b = [("qn0", qn[0]), ("qn1", qn[1]), ("kn0", kn[0]), ("kn1", kn[1]),
                       ("v_tm", v_tm), ("k_tm", k_tm), ("o_blk", o_blk),
                       ("qkraw0", qkraw[0]), ("ggraw0", ggraw[0])]
            off = 0
            for nm, buf in dumps_b:
                w = buf.shape[1]
                p.dma("gpsimd", dbgb[:, off:off + w], buf[:, 0:w], waits=ww(nm) + rw(nm))
                off += w
            dumps_f = [("ab_tm", ab_tm, 64), ("g_part", g_part, 32), ("beta_p", beta_p, 32),
                       ("c_all", c_all, 32), ("A_all", A_all, 32), ("dec_all", dec_all, 32),
                       ("gam_all", gam_all, 32), ("negbA", negbA, 32)]
            off = 0
            for nm, buf, w in dumps_f:
                p.dma("gpsimd", dbgf[:, off:off + w], buf[:, 0:w], waits=ww(nm) + rw(nm))
                off += w
        p.waits("gpsimd", p.barrier_tokens())
        p.finalize()
    return nc


_launches = {}
_last_res = None


def _get(name, builder):
    if name not in _launches:
        _launches[name] = builder()
    return _launches[name]


def run_A(x0):
    nc = _get("A", build_launch_A)
    in_maps = []
    for c in range(NCORE):
        b, q = divmod(c, 4)
        in_maps.append({"x": f32(x0[b, q * Q:(q + 1) * Q])})
    res = run_bass_kernel_spmd(nc, in_maps, core_ids=list(range(NCORE)))
    xn = np.empty((B, T, D), np.float32)
    xnb = np.empty((B, T, D), BF)
    for c in range(NCORE):
        b, q = divmod(c, 4)
        xn[b, q * Q:(q + 1) * Q] = res.results[c]["xn"]
        xnb[b, q * Q:(q + 1) * Q] = res.results[c]["xnb"]
    return xn, xnb




def _consts_B():
    eye = np.eye(128, dtype=np.float32)
    r = np.arange(128)
    mL = np.where(r[None, :] < r[:, None], 0.0, NEG).astype(np.float32)
    mU = np.where(r[:, None] < r[None, :], 0.0, NEG).astype(np.float32)
    mUi = np.where(r[:, None] <= r[None, :], 0.0, NEG).astype(np.float32)
    csts = np.zeros((128, 4), np.float32)
    csts[:, 0] = -0.5
    csts[:, 1] = np.log(0.125)
    return {
        "identb": cbf(eye), "identf": f32(eye),
        "triu": f32(np.where(r[:, None] <= r[None, :], 1.0, 0.0)),
        "maskL2": cbf(np.concatenate([mL, mL], 1)),
        "maskU2": cbf(np.concatenate([mU, mU], 1)),
        "maskUi2": cbf(np.concatenate([mUi, mUi], 1)),
        "ones1": f32(np.ones((1, 128))),
        "sel_lo": f32((r[None, :] < 64).astype(np.float32)),
        "sel_hi": f32((r[None, :] >= 64).astype(np.float32)),
        "blk2": cbf(np.stack([(r < 64), (r >= 64)], 1).astype(np.float32)),
        "blk2T": f32(np.stack([(r < 64), (r >= 64)], 0).astype(np.float32)[:2]),
        "ones_c": cbf(np.ones((128, 1))),
        "csts": csts,
    }


def run_B(xn0b, params):
    nc = _get("B", build_launch_B)
    consts = _consts_B()
    xnb_fm = [np.ascontiguousarray(xn0b[b].T) for b in range(B)]
    in_maps = []
    for c in range(NCORE):
        b = c // 4
        hg = (c % 4) * 4
        wq, wk = params["L0_attn_w_q"], params["L0_attn_w_k"]
        im = {
            "xnb": xnb_fm[b],
            "wqk": f32(np.concatenate([wq[:, hg * 64:(hg + 4) * 64],
                                       wk[:, hg * 64:(hg + 4) * 64]], 1)),
            "wv4": f32(params["L0_attn_w_v"][:, hg * 128:(hg + 4) * 128]),
            "wg4": f32(params["L0_attn_w_g"][:, hg * 128:(hg + 4) * 128]),
            "wab": f32(np.concatenate([params["L0_attn_w_a"][:, hg:hg + 4],
                                       params["L0_attn_w_b"][:, hg:hg + 4]], 1)),
            "convqk": f32(np.concatenate([params["L0_attn_conv_q"][hg * 64:(hg + 4) * 64],
                                          params["L0_attn_conv_k"][hg * 64:(hg + 4) * 64]], 0)),
            "convv": f32(params["L0_attn_conv_v"][hg * 128:(hg + 4) * 128]),
            "alogb": f32(np.broadcast_to(params["L0_attn_A_log"][hg:hg + 4], (128, 4))),
            "dtbb": f32(np.broadcast_to(params["L0_attn_dt_bias"][hg:hg + 4], (128, 4))),
            "normw": f32(params["L0_attn_norm_w"][:, None]),
        }
        im.update(consts)
        in_maps.append(im)
    res = run_bass_kernel_spmd(nc, in_maps, core_ids=list(range(NCORE)))
    global _last_res
    _last_res = res
    y = np.empty((B, T, H * DV), BF)
    for c in range(NCORE):
        b = c // 4
        hg = (c % 4) * 4
        y[b, :, hg * 128:(hg + 4) * 128] = res.results[c]["y"].T
    return y





def build_launch_C():
    nc = new_nc()
    ap = lambda n, s, d, k: nc.dram_tensor(n, s, d, kind=k).ap()
    x0q = ap("x0q", [Q, D], F32, "ExternalInput")
    xn0q = ap("xn0q", [Q, D], F32, "ExternalInput")
    xnfm = ap("xnfm", [D, Q], BF16, "ExternalInput")
    yfm = ap("yfm", [2048, Q], BF16, "ExternalInput")
    wout = ap("wout", [2048, D], F32, "ExternalInput")
    hwT = ap("hwT", [D, 32], F32, "ExternalInput")
    keysT = ap("keysT", [32, 256], F32, "ExternalInput")
    vdfm = ap("vdfm", [D, 8192], F32, "ExternalInput")
    vuall = ap("vuall", [8192, D], F32, "ExternalInput")
    scaleb = ap("scaleb", [128, 1], F32, "ExternalInput")
    normw1 = ap("normw1", [1, D], F32, "ExternalInput")
    identb_i = ap("identb", [128, 128], BF16, "ExternalInput")
    ones1_i = ap("ones1", [1, 128], F32, "ExternalInput")
    csts_i = ap("csts", [128, 4], F32, "ExternalInput")
    selg_i = ap("selg", [128, 32 * 128], BF16, "ExternalInput")
    x1qo = ap("x1q", [Q, D], F32, "ExternalOutput")
    xn1qo = ap("xn1q", [Q, D], F32, "ExternalOutput")
    xn1qbo = ap("xn1qb", [Q, D], BF16, "ExternalOutput")
    z1qbo = ap("z1qb", [Q, D], BF16, "ExternalOutput")
    vd_bf = nc.dram_tensor("vd_bf", [D, 8192], BF16).ap()
    vu_bf = nc.dram_tensor("vu_bf", [8192, D], BF16).ap()

    N5 = Q // 512
    with ExitStack() as ctx:
        E = ctx.enter_context
        p = Prog(nc, ctx)
        pool = PsumPool(nc, ctx)
        sb = lambda n, s, d: E(nc.sbuf_tensor(n, s, d))
        identb = sb("identb_s", [128, 128], BF16)
        ones1 = sb("ones1_s", [1, 128], F32)
        csts = sb("csts_s", [128, 4], F32)
        scale_c = sb("scale_c", [128, 1], F32)
        selg = sb("selg_s", [128, 32 * 128], BF16)
        woutb = sb("woutb", [128, 16 * 1024], BF16)
        hwTb = sb("hwTb", [128, 8 * 32], BF16)
        keysb = sb("keysb", [32, 256], BF16)
        nw_bc = sb("nw_bc", [128, D], F32)
        wst = [sb(f"wst{i}", [128, 512], F32) for i in range(2)]
        cst = [sb(f"cst{i}", [128, 512], BF16) for i in range(2)]
        yqt = [sb(f"yqt{i}", [128, 2048], BF16) for i in range(2)]
        xnf = sb("xnf", [128, 8 * Q], BF16)
        vd_s = [sb(f"vd{i}", [128, 1024], BF16) for i in range(2)]
        vu_s = sb("vu_s", [128, 8 * 1024], BF16)
        hfm = sb("hfm", [32, Q], BF16)
        sim = sb("sim", [128, 256], F32)
        simw = sb("simw", [128, 256], F32)
        mc = sb("mc", [128, 4], F32)
        dm4 = sb("dm4", [128, 4], F32)
        dm4t = sb("dm4t", [128, 4], F32)
        ge4 = sb("ge4", [128, 256], F32)
        den = sb("den", [128, 1], F32)
        comb = sb("comb", [128, 256], BF16)
        cfb = sb("cfb", [128, 4 * 256], BF16)
        projw = sb("projw", [128, 8 * 512], BF16)
        silp = sb("silp", [128, 512], BF16)
        acc = sb("acc", [128, 4096], F32)
        x0t = sb("x0t", [128, 1024], F32)
        xnt = sb("xnt", [128, 1024], F32)
        x1t = sb("x1t", [128, 1024], F32)
        xn1t = sb("xn1t", [128, 1024], F32)
        xn1bt = sb("xn1bt", [128, 1024], BF16)
        z1bt = sb("z1bt", [128, 1024], BF16)
        ssq = sb("ssq", [128, 1], F32)
        ssum = sb("ssum", [128, 1], F32)
        mu = sb("mu", [128, 1], F32)
        var = sb("var", [128, 1], F32)
        sstd = sb("sstd", [128, 1], F32)
        rstd = sb("rstd", [128, 1], F32)
        trb = sb("trb", [128, 1024], BF16)

        st_w, st_r = {}, {}

        def _merge(dct, tok):
            sem, val = tok
            key = id(sem)
            if key not in dct or dct[key][1] < val:
                dct[key] = (sem, val)

        def setw(name, tok, keep=False):
            if not keep or name not in st_w:
                st_w[name] = {}
            if not keep:
                st_r[name] = {}
            _merge(st_w[name], tok)

        def addr(name, tok):
            _merge(st_r.setdefault(name, {}), tok)

        def rw(name):
            return list(st_w.get(name, {}).values())

        def ww(name):
            return list(st_r.get(name, {}).values()) + rw(name)

        tc = []
        for apx, buf in ((identb_i, identb), (ones1_i, ones1), (csts_i, csts),
                         (scaleb, scale_c), (selg_i, selg)):
            tc.append(p.dma("sync", buf[:], apx[:, :]))
        t_k = p.dma("sync", wst[0][0:32, 0:256], keysT[:, :])
        setw("wst0", t_k)
        t_kc = p.op("vector", "tensor_copy", waits=[t_k], out=keysb[:], in_=wst[0][0:32, 0:256])
        addr("wst0", t_kc)
        tc.append(t_kc)
        for f in range(16):
            for hf in range(2):
                sl = (f * 2 + hf) % 2
                t_l = p.dma("sync", wst[sl][:], wout[f * 128:(f + 1) * 128, hf * 512:(hf + 1) * 512],
                            waits=ww(f"wst{sl}"))
                setw(f"wst{sl}", t_l)
                t_c = p.op("vector", "tensor_copy", waits=[t_l],
                           out=woutb[:, f * 1024 + hf * 512:f * 1024 + hf * 512 + 512],
                           in_=wst[sl][:])
                addr(f"wst{sl}", t_c)
                tc.append(t_c)
        for f in range(8):
            sl = f % 2
            t_l = p.dma("sync", wst[sl][:, 0:32], hwT[f * 128:(f + 1) * 128, :], waits=ww(f"wst{sl}"))
            setw(f"wst{sl}", t_l)
            t_c = p.op("vector", "tensor_copy", waits=[t_l], out=hwTb[:, f * 32:(f + 1) * 32],
                       in_=wst[sl][:, 0:32])
            addr(f"wst{sl}", t_c)
            tc.append(t_c)
        t_x = p.dma("sync", xnf[:].rearrange("p (f t) -> p f t", f=8),
                    xnfm.rearrange("(f p) t -> p f t", p=128))
        tc += [t_x]
        setw("xnf", t_x)
        for half in range(2):
            t_nw = p.dma("sync", wst[0][0:1, :], normw1[:, half * 512:(half + 1) * 512],
                         waits=ww("wst0"))
            setw("wst0", t_nw)
            bank_t, wts, bk = pool.acquire()
            tm = p.op("tensor", "matmul", waits=wts + [t_nw],
                      out=bank_t[:], lhsT=ones1[:], rhs=wst[0][0:1, 0:512],
                      start=True, stop=True)
            pool.set_writer(bk, tm)
            te = p.op("vector", "tensor_copy", waits=[tm], out=nw_bc[:, half * 512:(half + 1) * 512],
                      in_=bank_t[:])
            pool.add_reader(bk, te)
            addr("wst0", te)
            setw("nw_bc", te, keep=(half == 1))
        for (src_ap, dst_ap, nrow, ncol, nm) in ((vdfm, vd_bf, D, 8192, "vdc"),
                                                 (vuall, vu_bf, 8192, D, "vuc")):
            for f in range(nrow // 128):
                for cc in range(ncol // 512):
                    sl = (f * (ncol // 512) + cc) % 2
                    t_l = p.dma("sync", wst[sl][:],
                                src_ap[f * 128:(f + 1) * 128, cc * 512:(cc + 1) * 512],
                                waits=ww(f"wst{sl}"))
                    setw(f"wst{sl}", t_l)
                    t_c = p.op("vector", "tensor_copy", waits=[t_l] + ww(f"cst{sl}"),
                               out=cst[sl][:], in_=wst[sl][:])
                    setw(f"cst{sl}", t_c)
                    addr(f"wst{sl}", t_c)
                    t_s = p.dma("gpsimd", dst_ap[f * 128:(f + 1) * 128, cc * 512:(cc + 1) * 512],
                                cst[sl][:], waits=[t_c])
                    addr(f"cst{sl}", t_s)
                    setw(nm, t_s, keep=not (f == 0 and cc == 0))
        w_init = list(tc) + rw("nw_bc")

        for tt in range(N5):
            bank_t, wts, bk = pool.acquire()
            tm = None
            for f in range(8):
                tm = p.op("tensor", "matmul",
                          waits=(wts + w_init if (tt == 0 and f == 0) else (wts if f == 0 else [])),
                          out=bank_t[0:32, :], lhsT=hwTb[:, f * 32:(f + 1) * 32],
                          rhs=xnf[:, f * Q + tt * 512: f * Q + tt * 512 + 512],
                          start=(f == 0), stop=(f == 7))
            pool.set_writer(bk, tm)
            addr("xnf", tm)
            te = p.op("scalar", "copy", waits=[tm] + ww("hfm"), out=hfm[0:32, tt * 512:(tt + 1) * 512],
                      in_=bank_t[0:32, :])
            pool.add_reader(bk, te)
            setw("hfm", te, keep=(tt > 0))

        for t5 in range(N5):
            q0 = t5 * 512
            for sub in range(4):
                t0 = q0 + sub * 128
                ysl = (t5 * 4 + sub) % 2
                t_yld = p.dma("sync", yqt[ysl][:].rearrange("p (f t) -> p f t", f=16),
                              yfm.rearrange("(f p) t -> p f t", p=128)[:, :, t0:t0 + 128],
                              waits=ww(f"yqt{ysl}"))
                setw(f"yqt{ysl}", t_yld)
                for half in range(2):
                    bank_t, wts, bk = pool.acquire()
                    tm = None
                    for f in range(16):
                        tm = p.op("tensor", "matmul", waits=(wts + [t_yld] if f == 0 else []),
                                  out=bank_t[:], lhsT=yqt[ysl][:, f * 128:(f + 1) * 128],
                                  rhs=woutb[:, f * 1024 + half * 512: f * 1024 + half * 512 + 512],
                                  start=(f == 0), stop=(f == 15))
                    pool.set_writer(bk, tm)
                    addr(f"yqt{ysl}", tm)
                    te = p.op("vector", "tensor_copy",
                              waits=[tm] + (ww("acc") if (sub == 0 and half == 0) else []),
                              out=acc[:, sub * 1024 + half * 512: sub * 1024 + half * 512 + 512],
                              in_=bank_t[:])
                    pool.add_reader(bk, te)
                    setw("acc", te, keep=not (sub == 0 and half == 0))
            for sub in range(4):
                t0 = q0 + sub * 128
                bank_t, wts, bk = pool.acquire()
                tm = p.op("tensor", "matmul", waits=wts + rw("hfm") + [t_kc], out=bank_t[:, 0:256],
                          lhsT=hfm[0:32, t0:t0 + 128], rhs=keysb[:], start=True, stop=True)
                pool.set_writer(bk, tm)
                addr("hfm", tm)
                t_s = p.op("vector", "tensor_copy", waits=[tm] + ww("sim"), out=sim[:],
                           in_=bank_t[:, 0:256])
                pool.add_reader(bk, t_s)
                setw("sim", t_s)
                t_wk = p.op("vector", "tensor_copy", waits=[t_s] + ww("simw"), out=simw[:], in_=sim[:])
                setw("simw", t_wk)
                tprev = t_wk
                for r4 in range(4):
                    t_m = p.op("vector", "reduce_max", waits=[tprev] + (ww("mc") if r4 == 0 else []),
                               out=mc[:, r4:r4 + 1], in_=simw[:], axis=mybir.AxisListType.X)
                    setw("mc", t_m, keep=(r4 > 0))
                    if r4 < 3:
                        t_g = p.op("vector", "tensor_scalar", waits=[t_m] + ww("ge4"), out=ge4[:],
                                   in0=simw[:], scalar1=mc[:, r4:r4 + 1], scalar2=None, op0=ALU.is_ge)
                        setw("ge4", t_g)
                        tprev = p.op("vector", "scalar_tensor_tensor", waits=[t_g], out=simw[:],
                                     in0=ge4[:], scalar=NEG, in1=simw[:], op0=ALU.mult, op1=ALU.add)
                        setw("simw", tprev, keep=True)
                        addr("ge4", tprev)
                t_dm = p.op("vector", "tensor_scalar", waits=rw("mc") + ww("dm4"), out=dm4[:],
                            in0=mc[:], scalar1=mc[:, 0:1], scalar2=None, op0=ALU.subtract)
                setw("dm4", t_dm)
                addr("mc", t_dm)
                t_de = p.op("scalar", "activation", waits=[t_dm], out=dm4[:], in_=dm4[:], func=AF.Exp)
                setw("dm4", t_de)
                t_ds = p.op("vector", "tensor_scalar", waits=[t_de] + ww("den") + ww("dm4t"),
                            out=dm4t[:], in0=dm4[:], scalar1=1.0, scalar2=None,
                            op0=ALU.mult, op1=ALU.add, accum_out=den[:])
                setw("den", t_ds)
                setw("dm4t", t_ds)
                addr("dm4", t_ds)
                t_di = p.op("vector", "reciprocal", waits=[t_ds], out=den[:], in_=den[:])
                setw("den", t_di)
                t_sel = p.op("vector", "tensor_scalar", waits=rw("sim") + rw("mc") + ww("ge4"),
                             out=ge4[:], in0=sim[:], scalar1=mc[:, 3:4], scalar2=None, op0=ALU.is_ge)
                setw("ge4", t_sel)
                addr("sim", t_sel)
                addr("mc", t_sel)
                t_ea = p.op("vector", "tensor_scalar", waits=rw("sim") + rw("mc") + rw("simw"),
                            out=simw[:], in0=sim[:], scalar1=mc[:, 0:1], scalar2=None,
                            op0=ALU.subtract)
                setw("simw", t_ea)
                addr("sim", t_ea)
                t_ex = p.op("scalar", "activation", waits=[t_ea], out=simw[:], in_=simw[:], func=AF.Exp)
                setw("simw", t_ex)
                t_cb = p.op("vector", "scalar_tensor_tensor", waits=[t_ex, t_di, t_sel] + ww("comb"),
                            out=comb[:], in0=simw[:], scalar=den[:], in1=ge4[:],
                            op0=ALU.mult, op1=ALU.mult)
                setw("comb", t_cb)
                addr("simw", t_cb)
                addr("den", t_cb)
                addr("ge4", t_cb)
                for half in range(2):
                    bank_t, wts, bk = pool.acquire(BF16)
                    ttp = p.op("tensor", "transpose", waits=wts + [t_cb],
                               out=bank_t[:, 0:128], in_=comb[:, half * 128:(half + 1) * 128],
                               identity=identb[:])
                    pool.set_writer(bk, ttp)
                    addr("comb", ttp)
                    te = p.op("vector", "tensor_copy",
                              waits=[ttp] + (ww("cfb") if (sub == 0 and half == 0) else []),
                              out=cfb[:, sub * 256 + half * 128: sub * 256 + half * 128 + 128],
                              in_=bank_t[:, 0:128])
                    pool.add_reader(bk, te)
                    setw("cfb", te, keep=not (sub == 0 and half == 0))
            for gb in range(8):
                for gi in range(8):
                    g = gb * 8 + gi
                    sl = g % 2
                    t_vd = p.dma("sync", vd_s[sl][:].rearrange("p (f c) -> p f c", f=8),
                                 vd_bf.rearrange("(f p) c -> p f c", p=128)[:, :, g * 128:(g + 1) * 128],
                                 waits=ww(f"vd{sl}") + (rw("vdc") if t5 == 0 and gb == 0 and gi < 2 else []))
                    setw(f"vd{sl}", t_vd)
                    t_vu = p.dma("sync", vu_s[:, gi * 1024:(gi + 1) * 1024],
                                 vu_bf[g * 128:(g + 1) * 128, :],
                                 waits=ww(f"vuq{gi}") + (rw("vuc") if t5 == 0 and gb == 0 else []))
                    setw(f"vuq{gi}", t_vu)
                    bank_t, wts, bk = pool.acquire()
                    tm = None
                    for f in range(8):
                        tm = p.op("tensor", "matmul", waits=(wts + [t_vd] if f == 0 else []),
                                  out=bank_t[:], lhsT=vd_s[sl][:, f * 128:(f + 1) * 128],
                                  rhs=xnf[:, f * Q + q0: f * Q + q0 + 512],
                                  start=(f == 0), stop=(f == 7))
                    pool.set_writer(bk, tm)
                    addr(f"vd{sl}", tm)
                    t_si = p.op("scalar", "activation", waits=[tm] + ww("silp"),
                                out=silp[:], in_=bank_t[:], func=AF.Silu)
                    pool.add_reader(bk, t_si)
                    setw("silp", t_si)
                    bank_t2, wts2, bk2 = pool.acquire()
                    tw = None
                    for sub in range(4):
                        tw = p.op("tensor", "matmul",
                                  waits=(wts2 + rw("cfb") if sub == 0 else []),
                                  out=bank_t2[:, sub * 128:(sub + 1) * 128],
                                  lhsT=selg[:, (g % 32) * 128:(g % 32) * 128 + 128],
                                  rhs=cfb[:, sub * 256 + (g // 32) * 128: sub * 256 + (g // 32) * 128 + 128],
                                  start=True, stop=True)
                    pool.set_writer(bk2, tw)
                    addr("cfb", tw)
                    t_pw = p.op("vector", "tensor_tensor", waits=[tw, t_si] + ww(f"pw{gi}"),
                                out=projw[:, gi * 512:(gi + 1) * 512], in0=silp[:],
                                in1=bank_t2[:], op=ALU.mult)
                    pool.add_reader(bk2, t_pw)
                    setw(f"pw{gi}", t_pw)
                    addr("silp", t_pw)
                for sub in range(4):
                    for half in range(2):
                        bank_t, wts, bk = pool.acquire()
                        tm = None
                        for gi in range(8):
                            tm = p.op("tensor", "matmul",
                                      waits=(wts + rw(f"pw{gi}") + rw(f"vuq{gi}") if gi == 0 else
                                             rw(f"pw{gi}") + rw(f"vuq{gi}")),
                                      out=bank_t[:],
                                      lhsT=projw[:, gi * 512 + sub * 128: gi * 512 + sub * 128 + 128],
                                      rhs=vu_s[:, gi * 1024 + half * 512: gi * 1024 + half * 512 + 512],
                                      start=(gi == 0), stop=(gi == 7))
                        pool.set_writer(bk, tm)
                        for gi in range(8):
                            addr(f"pw{gi}", tm)
                            addr(f"vuq{gi}", tm)
                        t_ac = p.op("vector", "scalar_tensor_tensor", waits=[tm] + ww("acc"),
                                    out=acc[:, sub * 1024 + half * 512: sub * 1024 + half * 512 + 512],
                                    in0=bank_t[:], scalar=scale_c[:],
                                    in1=acc[:, sub * 1024 + half * 512: sub * 1024 + half * 512 + 512],
                                    op0=ALU.mult, op1=ALU.add)
                        pool.add_reader(bk, t_ac)
                        setw("acc", t_ac, keep=True)
            for sub in range(4):
                t0 = q0 + sub * 128
                asl = slice(sub * 1024, (sub + 1) * 1024)
                t_l0 = p.dma("sync", x0t[:], x0q[t0:t0 + 128, :], waits=ww("x0t"))
                setw("x0t", t_l0)
                t_ln = p.dma("sync", xnt[:], xn0q[t0:t0 + 128, :], waits=ww("xnt"))
                setw("xnt", t_ln)
                t_a1 = p.op("vector", "tensor_tensor", waits=[t_l0] + rw("acc") + ww("x1t"),
                            out=x1t[:], in0=acc[:, asl], in1=x0t[:], op=ALU.add)
                setw("x1t", t_a1)
                addr("acc", t_a1)
                addr("x0t", t_a1)
                t_a2 = p.op("vector", "tensor_tensor", waits=[t_ln, t_a1],
                            out=x1t[:], in0=x1t[:], in1=xnt[:], op=ALU.add)
                setw("x1t", t_a2)
                addr("xnt", t_a2)
                t_s1 = p.dma("gpsimd", x1qo[t0:t0 + 128, :], x1t[:], waits=[t_a2])
                addr("x1t", t_s1)
                t_sq = p.op("scalar", "activation", waits=[t_a2] + ww("trb"), out=trb[:],
                            in_=x1t[:], func=AF.Square, accum_out=ssq[:])
                setw("trb", t_sq)
                setw("ssq", t_sq)
                t_sm = p.op("vector", "tensor_scalar", waits=[t_a2] + ww("z1bt") + ww("ssum"),
                            out=z1bt[:], in0=x1t[:], scalar1=1.0, scalar2=None,
                            op0=ALU.mult, op1=ALU.add, accum_out=ssum[:])
                setw("ssum", t_sm)
                setw("z1bt", t_sm)
                t_mu = p.op("vector", "tensor_scalar", waits=[t_sm] + ww("mu"), out=mu[:],
                            in0=ssum[:], scalar1=1.0 / D, scalar2=None, op0=ALU.mult)
                setw("mu", t_mu)
                addr("ssum", t_mu)
                t_v1 = p.op("vector", "scalar_tensor_tensor", waits=[t_mu] + ww("var"), out=var[:],
                            in0=mu[:], scalar=-1.0, in1=mu[:], op0=ALU.mult, op1=ALU.mult)
                setw("var", t_v1)
                t_v2 = p.op("vector", "scalar_tensor_tensor", waits=[t_v1, t_sq], out=var[:],
                            in0=ssq[:], scalar=1.0 / D, in1=var[:], op0=ALU.mult, op1=ALU.add)
                setw("var", t_v2)
                addr("ssq", t_v2)
                t_v3 = p.op("vector", "tensor_scalar", waits=[t_v2], out=var[:], in0=var[:],
                            scalar1=1e-5, scalar2=None, op0=ALU.add)
                setw("var", t_v3)
                t_sd = p.op("scalar", "activation", waits=[t_v3] + ww("sstd"), out=sstd[:],
                            in_=var[:], func=AF.Sqrt)
                setw("sstd", t_sd)
                addr("var", t_sd)
                t_rs = p.op("vector", "reciprocal", waits=[t_sd] + ww("rstd"), out=rstd[:], in_=sstd[:])
                setw("rstd", t_rs)
                addr("sstd", t_rs)
                t_xn1 = p.op("vector", "tensor_scalar", waits=[t_rs] + rw("x1t") + ww("xn1t"),
                             out=xn1t[:], in0=x1t[:], scalar1=mu[:], scalar2=rstd[:],
                             op0=ALU.subtract, op1=ALU.mult)
                setw("xn1t", t_xn1)
                addr("x1t", t_xn1)
                addr("mu", t_xn1)
                addr("rstd", t_xn1)
                t_cb1 = p.op("vector", "tensor_copy", waits=[t_xn1] + ww("xn1bt"), out=xn1bt[:],
                             in_=xn1t[:])
                setw("xn1bt", t_cb1)
                t_s2 = p.dma("gpsimd", xn1qo[t0:t0 + 128, :], xn1t[:], waits=[t_xn1])
                addr("xn1t", t_s2)
                t_s3 = p.dma("gpsimd", xn1qbo[t0:t0 + 128, :], xn1bt[:], waits=[t_cb1])
                addr("xn1bt", t_s3)
                t_sq2 = p.op("scalar", "activation", waits=[t_xn1] + rw("trb") + ww("ssq"),
                             out=trb[:], in_=xn1t[:], func=AF.Square, accum_out=ssq[:])
                setw("trb", t_sq2)
                setw("ssq", t_sq2)
                t_vv = p.op("vector", "tensor_scalar", waits=[t_sq2] + ww("var"), out=var[:],
                            in0=ssq[:], scalar1=1.0 / D, scalar2=1.1920929e-07,
                            op0=ALU.mult, op1=ALU.add)
                setw("var", t_vv)
                addr("ssq", t_vv)
                t_sd2 = p.op("scalar", "activation", waits=[t_vv] + ww("sstd"), out=sstd[:],
                             in_=var[:], func=AF.Sqrt)
                setw("sstd", t_sd2)
                addr("var", t_sd2)
                t_rs2 = p.op("vector", "reciprocal", waits=[t_sd2] + ww("rstd"), out=rstd[:], in_=sstd[:])
                setw("rstd", t_rs2)
                addr("sstd", t_rs2)
                t_z1 = p.op("vector", "tensor_scalar", waits=[t_rs2] + rw("xn1t") + ww("x1t"),
                            out=x1t[:], in0=xn1t[:], scalar1=rstd[:], scalar2=None, op0=ALU.mult)
                setw("x1t", t_z1)
                addr("xn1t", t_z1)
                addr("rstd", t_z1)
                t_z2 = p.op("vector", "tensor_tensor", waits=[t_z1] + rw("nw_bc") + rw("z1bt"),
                            out=z1bt[:], in0=x1t[:], in1=nw_bc[:], op=ALU.mult)
                setw("z1bt", t_z2)
                addr("x1t", t_z2)
                addr("nw_bc", t_z2)
                t_s4 = p.dma("gpsimd", z1qbo[t0:t0 + 128, :], z1bt[:], waits=[t_z2])
                addr("z1bt", t_s4)
        p.waits("gpsimd", p.barrier_tokens())
        p.finalize()
    return nc


def run_C(x0, xn0, xn0b, yb, params):
    nc = _get("C", build_launch_C)
    r = np.arange(128)
    eye = np.eye(128, dtype=np.float32)
    selg = np.zeros((128, 32 * 128), np.float32)
    for s in range(32):
        for m in range(128):
            selg[s * 4 + m // 32, s * 128 + m] = 1.0
    csts = np.zeros((128, 4), np.float32)
    csts[:, 0] = -0.5
    consts = {
        "identb": cbf(eye), "ones1": f32(np.ones((1, 128))), "csts": csts,
        "selg": cbf(selg),
    }
    vdfm = f32(np.transpose(params["L0_ffn_values_down"], (1, 0, 2)).reshape(D, 8192))
    vuall = f32(params["L0_ffn_values_up"].reshape(8192, D))
    in_maps = []
    for c in range(NCORE):
        b, q = divmod(c, 4)
        tsl = slice(q * Q, (q + 1) * Q)
        im = {
            "x0q": f32(x0[b, tsl]),
            "xn0q": f32(xn0[b, tsl]),
            "xnfm": np.ascontiguousarray(xn0b[b, tsl].T),
            "yfm": np.ascontiguousarray(yb[b, tsl].T),
            "wout": f32(params["L0_attn_w_out"]),
            "hwT": f32(params["L0_ffn_hasher_w"].T),
            "keysT": f32(params["L0_ffn_keys"].T),
            "vdfm": vdfm,
            "vuall": vuall,
            "scaleb": f32(np.broadcast_to(params["L0_ffn_scale"].reshape(1, 1), (128, 1))),
            "normw1": f32(params["L1_attn_norm_w"][None, :]),
        }
        im.update(consts)
        in_maps.append(im)
    res = run_bass_kernel_spmd(nc, in_maps, core_ids=list(range(NCORE)))
    global _last_res
    _last_res = res
    outs = {}
    for nm, dt in (("x1q", np.float32), ("xn1q", np.float32), ("xn1qb", BF), ("z1qb", BF)):
        a = np.empty((B, T, D), dt)
        for c in range(NCORE):
            b, q = divmod(c, 4)
            a[b, q * Q:(q + 1) * Q] = res.results[c][nm]
        outs[nm] = a
    return outs





def build_launch_D():
    nc = new_nc()
    ap = lambda n, s, d, k: nc.dram_tensor(n, s, d, kind=k).ap()
    z1fm = ap("z1fm", [D, T], BF16, "ExternalInput")
    wq4 = ap("wq4", [D, 256], F32, "ExternalInput")
    wk4 = ap("wk4", [D, 256], F32, "ExternalInput")
    wv4 = ap("wv4", [D, 256], F32, "ExternalInput")
    wg4h = ap("wg4h", [D, 4], F32, "ExternalInput")
    identb_i = ap("identb", [128, 128], BF16, "ExternalInput")
    ones164_i = ap("ones164", [1, 64], F32, "ExternalInput")
    maskd4_i = ap("maskd4", [128, 2048], BF16, "ExternalInput")
    csts_i = ap("csts", [128, 4], F32, "ExternalInput")
    ogo = ap("og", [256, T], BF16, "ExternalOutput")
    gates_dram = nc.dram_tensor("gates_dram", [4, T], F32).ap()

    NT5 = T // 512
    with ExitStack() as ctx:
        E = ctx.enter_context
        p = Prog(nc, ctx)
        pool = PsumPool(nc, ctx)
        sb = lambda n, s, d: E(nc.sbuf_tensor(n, s, d))
        identb = sb("identb_s", [128, 128], BF16)
        ones164 = sb("ones164_s", [1, 64], F32)
        maskd4 = sb("maskd4_s", [128, 2048], BF16)
        csts = sb("csts_s", [128, 4], F32)
        wqb = sb("wqb", [128, 8 * 256], BF16)
        wkb = sb("wkb", [128, 8 * 256], BF16)
        wvb = sb("wvb", [128, 8 * 256], BF16)
        wgb = sb("wgb", [128, 8 * 4], BF16)
        wst = [sb(f"wst{i}", [128, 256], F32) for i in range(2)]
        z1s = [sb(f"z1s{i}", [128, 4096], BF16) for i in range(2)]
        Qf = sb("Qf", [128, T], BF16)
        Kf = sb("Kf", [128, T], BF16)
        vaug = sb("vaug", [128, 64 * 130], BF16)
        Pt = [sb(f"Pt{i}", [128, 1024], BF16) for i in range(2)]
        osb = sb("osb", [64, 512], F32)
        grow = sb("grow", [1, 512], F32)
        glr = sb("glr", [1, 512], F32)
        linv = sb("linv", [1, 512], F32)
        ogt = sb("ogt", [64, 512], BF16)
        gt2 = sb("gt2", [2, 512], F32)

        st_w, st_r = {}, {}

        def _merge(dct, tok):
            sem, val = tok
            key = id(sem)
            if key not in dct or dct[key][1] < val:
                dct[key] = (sem, val)

        def setw(name, tok, keep=False):
            if not keep or name not in st_w:
                st_w[name] = {}
            if not keep:
                st_r[name] = {}
            _merge(st_w[name], tok)

        def addr(name, tok):
            _merge(st_r.setdefault(name, {}), tok)

        def rw(name):
            return list(st_w.get(name, {}).values())

        def ww(name):
            return list(st_r.get(name, {}).values()) + rw(name)

        tc = []
        for apx, buf in ((identb_i, identb), (ones164_i, ones164), (maskd4_i, maskd4),
                         (csts_i, csts)):
            tc.append(p.dma("sync", buf[:], apx[:, :]))
        for (wsrc, wdst, ncol) in ((wq4, wqb, 256), (wk4, wkb, 256), (wv4, wvb, 256),
                                   (wg4h, wgb, 4)):
            for f in range(8):
                sl = f % 2
                t_l = p.dma("sync", wst[sl][:, 0:ncol], wsrc[f * 128:(f + 1) * 128, :],
                            waits=ww(f"wst{sl}"))
                setw(f"wst{sl}", t_l)
                t_c = p.op("vector", "tensor_copy", waits=[t_l],
                           out=wdst[:, f * ncol:(f + 1) * ncol], in_=wst[sl][:, 0:ncol])
                addr(f"wst{sl}", t_c)
                tc.append(t_c)
        w_init = list(tc)

        for pr in range(2):
            t_ms = p.op("vector", "memset", vaug[:], 1.0, waits=ww("vaug"))
            setw("vaug", t_ms)
            for tt in range(NT5):
                q0 = tt * 512
                sl = tt % 2
                t_z = p.dma("sync", z1s[sl][:].rearrange("p (f t) -> p f t", f=8),
                            z1fm.rearrange("(f p) t -> p f t", p=128)[:, :, q0:q0 + 512],
                            waits=ww(f"z1s{sl}"))
                setw(f"z1s{sl}", t_z)
                for (wb, dst, dname) in ((wqb, Qf, "Qf"), (wkb, Kf, "Kf")):
                    bank_t, wts, bk = pool.acquire()
                    tm = None
                    for f in range(8):
                        tm = p.op("tensor", "matmul",
                                  waits=(wts + [t_z] + (w_init if (pr == 0 and tt == 0 and f == 0) else []) if f == 0 else []),
                                  out=bank_t[:],
                                  lhsT=wb[:, f * 256 + pr * 128: f * 256 + pr * 128 + 128],
                                  rhs=z1s[sl][:, f * 512:(f + 1) * 512],
                                  start=(f == 0), stop=(f == 7))
                    pool.set_writer(bk, tm)
                    addr(f"z1s{sl}", tm)
                    te = p.op("scalar", "copy", waits=[tm] + (ww(dname) if tt == 0 else []),
                              out=dst[:, q0:q0 + 512], in_=bank_t[:])
                    pool.add_reader(bk, te)
                    setw(dname, te, keep=(tt > 0))
                for sub in range(4):
                    st_i = tt * 4 + sub
                    bank_t, wts, bk = pool.acquire()
                    tm = None
                    for f in range(8):
                        tm = p.op("tensor", "matmul", waits=(wts + [t_z] if f == 0 else []),
                                  out=bank_t[:, 0:128],
                                  lhsT=z1s[sl][:, f * 512 + sub * 128: f * 512 + sub * 128 + 128],
                                  rhs=wvb[:, f * 256 + pr * 128: f * 256 + pr * 128 + 128],
                                  start=(f == 0), stop=(f == 7))
                    pool.set_writer(bk, tm)
                    te = p.op("vector", "tensor_copy", waits=[tm] + rw("vaug"),
                              out=vaug[:, st_i * 130:st_i * 130 + 130].rearrange(
                                  "p (h c) -> p h c", h=2)[:, :, 0:64],
                              in_=bank_t[:, 0:128].rearrange("p (h c) -> p h c", h=2))
                    pool.add_reader(bk, te)
                    setw("vaug", te, keep=True)
                bank_t, wts, bk = pool.acquire()
                tm = None
                for f in range(8):
                    tm = p.op("tensor", "matmul", waits=(wts + [t_z] if f == 0 else []),
                              out=bank_t[0:2, :],
                              lhsT=wgb[:, f * 4 + pr * 2: f * 4 + pr * 2 + 2],
                              rhs=z1s[sl][:, f * 512:(f + 1) * 512],
                              start=(f == 0), stop=(f == 7))
                pool.set_writer(bk, tm)
                te = p.op("vector", "tensor_copy", waits=[tm] + ww("gt2"), out=gt2[:],
                          in_=bank_t[0:2, :])
                pool.add_reader(bk, te)
                setw("gt2", te)
                t_gs = p.dma("gpsimd", gates_dram[pr * 2:pr * 2 + 2, q0:q0 + 512], gt2[:],
                             waits=[te] + (ww("gates_dram") if tt == 0 else []))
                addr("gt2", t_gs)
                setw("gates_dram", t_gs, keep=(tt > 0))
            for hh in range(2):
                rsl = slice(hh * 64, (hh + 1) * 64)
                for qt in range(NT5):
                    q0 = qt * 512
                    t_g = p.dma("sync", grow[:], gates_dram[pr * 2 + hh:pr * 2 + hh + 1, q0:q0 + 512],
                                waits=ww("grow") + rw("gates_dram"))
                    setw("grow", t_g)
                    addr("gates_dram", t_g)
                    t_sg = p.op("scalar", "activation", waits=[t_g], out=grow[:], in_=grow[:],
                                func=AF.Sigmoid)
                    setw("grow", t_sg)
                    acc_t, awts, acc_bk = pool.acquire(pin=True)
                    ng = (q0 // 128 + 4 + 1) // 2
                    first_mm = [True]

                    def emit_scores(gidx):
                        kb0 = gidx * 2
                        sc_t, swts, sc_bk = pool.acquire()
                        sc_t2, swts2, sc_bk2 = pool.acquire()
                        banks = [(sc_t, sc_bk, swts), (sc_t2, sc_bk2, swts2)]
                        exp_waits = []
                        for half in range(2):
                            kb = kb0 + half
                            if kb * 128 >= q0 + 512:
                                continue
                            bt, bkx, bwts = banks[half]
                            tm = p.op("tensor", "matmul",
                                      waits=bwts + (rw("Kf") + rw("Qf") if (gidx == 0 and qt == 0) else []),
                                      out=bt[:], lhsT=Kf[rsl, kb * 128:(kb + 1) * 128],
                                      rhs=Qf[rsl, q0:q0 + 512], start=True,
                                      stop=(kb * 128 + 128 <= q0))
                            if gidx == 0:
                                addr("Kf", tm)
                                addr("Qf", tm)
                            if kb * 128 + 128 > q0:
                                m = kb - q0 // 128
                                tm = p.op("tensor", "matmul", out=bt[:],
                                          lhsT=identb[:], rhs=maskd4[:, m * 512:(m + 1) * 512],
                                          start=False, stop=True)
                            pool.set_writer(bkx, tm)
                            exp_waits.append(tm)
                        return banks, exp_waits

                    def emit_exp(gidx, banks, exp_waits):
                        nm = f"Pt{gidx % 2}"
                        for half in range(2):
                            kb = gidx * 2 + half
                            if kb * 128 >= q0 + 512:
                                continue
                            bt, bkx, _ = banks[half]
                            t_e = p.op("scalar", "activation",
                                       waits=(exp_waits + ww(nm) if half == 0 else []),
                                       out=Pt[gidx % 2][:, half * 512:(half + 1) * 512], in_=bt[:],
                                       func=AF.Exp, scale=csts[:, 0:1])
                            pool.add_reader(bkx, t_e)
                            setw(nm, t_e, keep=(half == 1))

                    def emit_av(gidx):
                        nm = f"Pt{gidx % 2}"
                        tm = None
                        for half in range(2):
                            kb = gidx * 2 + half
                            if kb * 128 >= q0 + 512:
                                continue
                            tm = p.op("tensor", "matmul",
                                      waits=(awts if first_mm[0] else []) + rw(nm) + (rw("vaug") if (qt == 0 and gidx == 0) else []),
                                      out=acc_t[0:65, :],
                                      lhsT=vaug[:, kb * 130 + hh * 65: kb * 130 + hh * 65 + 65],
                                      rhs=Pt[gidx % 2][:, half * 512:(half + 1) * 512],
                                      start=first_mm[0], stop=(kb == q0 // 128 + 3))
                            addr(nm, tm)
                            addr("vaug", tm)
                            first_mm[0] = False
                        return tm

                    pend = None
                    tm = None
                    for gidx in range(ng):
                        banks, exp_waits = emit_scores(gidx)
                        emit_exp(gidx, banks, exp_waits)
                        if pend is not None:
                            tm = emit_av(pend)
                        pend = gidx
                    tm = emit_av(pend)
                    pool.set_writer(acc_bk, tm)
                    pool.unpin(acc_bk)
                    t_o = p.op("scalar", "copy", waits=[tm] + ww("osb"), out=osb[:],
                               in_=acc_t[0:64, :])
                    setw("osb", t_o)
                    t_li = p.op("vector", "reciprocal", waits=[tm] + ww("linv"), out=linv[:],
                                in_=acc_t[64:65, :])
                    setw("linv", t_li)
                    pool.add_reader(acc_bk, t_o)
                    pool.add_reader(acc_bk, t_li)
                    t_gl = p.op("vector", "tensor_tensor", waits=[t_li, t_sg] + ww("glr"),
                                out=glr[:], in0=grow[:], in1=linv[:], op=ALU.mult)
                    setw("glr", t_gl)
                    addr("grow", t_gl)
                    addr("linv", t_gl)
                    bb_t, bwts, bb_bk = pool.acquire()
                    t_bb = p.op("tensor", "matmul", waits=bwts + [t_gl], out=bb_t[0:64, :],
                                lhsT=ones164[:], rhs=glr[:], start=True, stop=True)
                    pool.set_writer(bb_bk, t_bb)
                    addr("glr", t_bb)
                    t_og = p.op("vector", "tensor_tensor", waits=[t_bb, t_o] + ww("ogt"),
                                out=ogt[:], in0=osb[:], in1=bb_t[0:64, :], op=ALU.mult)
                    pool.add_reader(bb_bk, t_og)
                    setw("ogt", t_og)
                    addr("osb", t_og)
                    t_st = p.dma("gpsimd", ogo[(pr * 2 + hh) * 64:(pr * 2 + hh) * 64 + 64, q0:q0 + 512],
                                 ogt[:], waits=[t_og])
                    addr("ogt", t_st)
        p.waits("gpsimd", p.barrier_tokens())
        p.finalize()
    return nc


def run_D(z1b, params):
    nc = _get("D", build_launch_D)
    eye = np.eye(128, dtype=np.float32)
    r = np.arange(128)
    maskd = np.zeros((128, 2048), np.float32)
    for m in range(4):
        for j in range(4):
            blk = maskd[:, m * 512 + j * 128: m * 512 + (j + 1) * 128]
            if j < m:
                blk[:] = NEG
            elif j == m:
                blk[:] = np.where(r[:, None] > r[None, :], NEG, 0.0)
    csts = np.zeros((128, 4), np.float32)
    csts[:, 0] = 0.125
    consts = {
        "identb": cbf(eye), "ones164": f32(np.ones((1, 64))),
        "maskd4": cbf(maskd), "csts": csts,
    }
    z1fm = [np.ascontiguousarray(z1b[b].T) for b in range(B)]
    in_maps = []
    for c in range(NCORE):
        b = c // 4
        hg = (c % 4) * 4
        im = {
            "z1fm": z1fm[b],
            "wq4": f32(params["L1_attn_Wq"][:, hg * 64:(hg + 4) * 64]),
            "wk4": f32(params["L1_attn_Wkv"][:, hg * 64:(hg + 4) * 64]),
            "wv4": f32(params["L1_attn_Wkv"][:, 1024 + hg * 64:1024 + (hg + 4) * 64]),
            "wg4h": f32(params["L1_attn_Wg"][:, hg:hg + 4]),
        }
        im.update(consts)
        in_maps.append(im)
    res = run_bass_kernel_spmd(nc, in_maps, core_ids=list(range(NCORE)))
    global _last_res
    _last_res = res
    og = np.empty((B, T, 1024), BF)
    for c in range(NCORE):
        b = c // 4
        hg = (c % 4) * 4
        og[b, :, hg * 64:(hg + 4) * 64] = res.results[c]["og"].T
    return og





def build_launch_E():
    nc = new_nc()
    ap = lambda n, s, d, k: nc.dram_tensor(n, s, d, kind=k).ap()
    x1q = ap("x1q", [Q, D], F32, "ExternalInput")
    xnfm = ap("xnfm", [D, Q], BF16, "ExternalInput")
    ofm = ap("ofm", [D, Q], BF16, "ExternalInput")
    wout1 = ap("wout1", [D, D], F32, "ExternalInput")
    sgwT = ap("sgwT", [D, 2048], F32, "ExternalInput")
    suwT = ap("suwT", [D, 2048], F32, "ExternalInput")
    sdwT = ap("sdwT", [2048, D], F32, "ExternalInput")
    gwT = ap("gwT", [D, 4096], F32, "ExternalInput")
    uwT = ap("uwT", [D, 4096], F32, "ExternalInput")
    dwT = ap("dwT", [4096, D], F32, "ExternalInput")
    rwT = ap("rwT", [D, 16], F32, "ExternalInput")
    cbias = ap("cbias", [128, 16], F32, "ExternalInput")
    identb_i = ap("identb", [128, 128], BF16, "ExternalInput")
    sel16_i = ap("sel16", [16, 16 * 128], BF16, "ExternalInput")
    x2qo = ap("x2q", [Q, D], F32, "ExternalOutput")
    sg_fm = nc.dram_tensor("sg_fm", [128, 16 * 1024], BF16).ap()
    su_fm = nc.dram_tensor("su_fm", [128, 16 * 1024], BF16).ap()
    gw_fm = nc.dram_tensor("gw_fm", [128, 16 * 2048], BF16).ap()
    uw_fm = nc.dram_tensor("uw_fm", [128, 16 * 2048], BF16).ap()
    dw_bf = nc.dram_tensor("dw_bf", [4096, D], BF16).ap()

    N5 = Q // 512
    with ExitStack() as ctx:
        E = ctx.enter_context
        p = Prog(nc, ctx)
        pool = PsumPool(nc, ctx)
        sb = lambda n, s, d: E(nc.sbuf_tensor(n, s, d))
        identb = sb("identb_s", [128, 128], BF16)
        sel16 = sb("sel16_s", [16, 16 * 128], BF16)
        woutb = sb("woutb", [128, 8 * 1024], BF16)
        rwTb = sb("rwTb", [128, 8 * 16], BF16)
        cb_bc = sb("cb_bc", [128, 16], F32)
        sdb = sb("sdb", [128, 16 * 1024], BF16)
        wst = [sb(f"wst{i}", [128, 512], F32) for i in range(2)]
        cst = [sb(f"cst{i}", [128, 512], BF16) for i in range(2)]
        xnf = sb("xnf", [128, 8 * Q], BF16)
        ot = [sb(f"ot{i}", [128, 1024], BF16) for i in range(2)]
        wsg = [sb(f"wsg{i}", [128, 1024], BF16) for i in range(2)]
        wsu = [sb(f"wsu{i}", [128, 1024], BF16) for i in range(2)]
        wge = [sb(f"wge{i}", [128, 2048], BF16) for i in range(2)]
        wue = [sb(f"wue{i}", [128, 2048], BF16) for i in range(2)]
        wde = [sb(f"wde{i}", [128, 2 * 1024], BF16) for i in range(4)]
        hsh = sb("hsh", [128, 16 * 512], BF16)
        hwe = sb("hwe", [128, 4 * 2 * 512], BF16)
        silp = sb("silp", [128, 512], BF16)
        hxe = sb("hxe", [128, 512], BF16)
        acc = sb("acc", [128, 4096], F32)
        x1t = sb("x1t", [128, 1024], F32)
        x2t = sb("x2t", [128, 1024], F32)
        logt = sb("logt", [128, 16], F32)
        sbt = sb("sbt", [128, 16], F32)
        scor = sb("scor", [128, 16], F32)
        gsc = sb("gsc", [128, 4], F32)
        gm = sb("gm", [128, 1], F32)
        gmask = sb("gmask", [128, 4], F32)
        smask = sb("smask", [128, 16], F32)
        mskd = sb("mskd", [128, 16], F32)
        m4r = sb("m4r", [128, 1], F32)
        selr = sb("selr", [128, 16], F32)
        wr = sb("wr", [128, 16], F32)
        dsum = sb("dsum", [128, 1], F32)
        comb = sb("comb", [128, 16], BF16)
        combf = sb("combf", [16, 512], BF16)
        tr16 = sb("tr16", [128, 16], F32)

        st_w, st_r = {}, {}

        def _merge(dct, tok):
            sem, val = tok
            key = id(sem)
            if key not in dct or dct[key][1] < val:
                dct[key] = (sem, val)

        def setw(name, tok, keep=False):
            if not keep or name not in st_w:
                st_w[name] = {}
            if not keep:
                st_r[name] = {}
            _merge(st_w[name], tok)

        def addr(name, tok):
            _merge(st_r.setdefault(name, {}), tok)

        def rw(name):
            return list(st_w.get(name, {}).values())

        def ww(name):
            return list(st_r.get(name, {}).values()) + rw(name)

        tc = []
        for apx, buf in ((identb_i, identb), (sel16_i, sel16), (cbias, cb_bc)):
            tc.append(p.dma("sync", buf[:], apx[:, :]))
        for f in range(8):
            for hf in range(2):
                sl = (f * 2 + hf) % 2
                t_l = p.dma("sync", wst[sl][:], wout1[f * 128:(f + 1) * 128, hf * 512:(hf + 1) * 512],
                            waits=ww(f"wst{sl}"))
                setw(f"wst{sl}", t_l)
                t_c = p.op("vector", "tensor_copy", waits=[t_l],
                           out=woutb[:, f * 1024 + hf * 512: f * 1024 + hf * 512 + 512],
                           in_=wst[sl][:])
                addr(f"wst{sl}", t_c)
                tc.append(t_c)
        for f in range(8):
            sl = f % 2
            t_l = p.dma("sync", wst[sl][:, 0:16], rwT[f * 128:(f + 1) * 128, :], waits=ww(f"wst{sl}"))
            setw(f"wst{sl}", t_l)
            t_c = p.op("vector", "tensor_copy", waits=[t_l], out=rwTb[:, f * 16:(f + 1) * 16],
                       in_=wst[sl][:, 0:16])
            addr(f"wst{sl}", t_c)
            tc.append(t_c)
        for f in range(16):
            for hf in range(2):
                sl = (f * 2 + hf) % 2
                t_l = p.dma("sync", wst[sl][:], sdwT[f * 128:(f + 1) * 128, hf * 512:(hf + 1) * 512],
                            waits=ww(f"wst{sl}"))
                setw(f"wst{sl}", t_l)
                t_c = p.op("vector", "tensor_copy", waits=[t_l],
                           out=sdb[:, f * 1024 + hf * 512: f * 1024 + hf * 512 + 512],
                           in_=wst[sl][:])
                addr(f"wst{sl}", t_c)
                tc.append(t_c)
        t_x = p.dma("sync", xnf[:].rearrange("p (f t) -> p f t", f=8),
                    xnfm.rearrange("(f p) t -> p f t", p=128))
        tc.append(t_x)
        setw("xnf", t_x)
        for (src_ap, dst_ap, nrow, ncol, nm, kind) in (
                (sgwT, sg_fm, D, 2048, "sgc", "hc"),
                (suwT, su_fm, D, 2048, "suc", "hc"),
                (gwT, gw_fm, D, 4096, "gwc", "ex"),
                (uwT, uw_fm, D, 4096, "uwc", "ex"),
                (dwT, dw_bf, 4096, D, "dwc", "raw")):
            for f in range(nrow // 128):
                for cc in range(ncol // 512):
                    sl = (f * (ncol // 512) + cc) % 2
                    t_l = p.dma("sync", wst[sl][:],
                                src_ap[f * 128:(f + 1) * 128, cc * 512:(cc + 1) * 512],
                                waits=ww(f"wst{sl}"))
                    setw(f"wst{sl}", t_l)
                    t_c = p.op("vector", "tensor_copy", waits=[t_l] + ww(f"cst{sl}"),
                               out=cst[sl][:], in_=wst[sl][:])
                    setw(f"cst{sl}", t_c)
                    addr(f"wst{sl}", t_c)
                    if kind == "raw":
                        t_s = p.dma("gpsimd", dst_ap[f * 128:(f + 1) * 128, cc * 512:(cc + 1) * 512],
                                    cst[sl][:], waits=[t_c])
                        addr(f"cst{sl}", t_s)
                        setw(nm, t_s, keep=not (f == 0 and cc == 0))
                    elif kind == "hc":
                        for s4 in range(4):
                            hcg = cc * 4 + s4
                            t_s = p.dma("gpsimd",
                                        dst_ap[:, hcg * 1024 + f * 128: hcg * 1024 + f * 128 + 128],
                                        cst[sl][:, s4 * 128:(s4 + 1) * 128],
                                        waits=([t_c] if s4 == 0 else []))
                            addr(f"cst{sl}", t_s)
                            setw(nm, t_s, keep=not (f == 0 and cc == 0 and s4 == 0))
                    else:
                        for s2 in range(2):
                            eg2 = cc * 2 + s2
                            t_s = p.dma("gpsimd",
                                        dst_ap[:, eg2 * 2048 + f * 256: eg2 * 2048 + f * 256 + 256],
                                        cst[sl][:, s2 * 256:(s2 + 1) * 256],
                                        waits=([t_c] if s2 == 0 else []))
                            addr(f"cst{sl}", t_s)
                            setw(nm, t_s, keep=not (f == 0 and cc == 0 and s2 == 0))
        w_init = list(tc)

        for t5 in range(N5):
            q0 = t5 * 512
            for sub in range(4):
                t0 = q0 + sub * 128
                osl = (t5 * 4 + sub) % 2
                t_old = p.dma("sync", ot[osl][:].rearrange("p (f t) -> p f t", f=8),
                              ofm.rearrange("(f p) t -> p f t", p=128)[:, :, t0:t0 + 128],
                              waits=ww(f"ot{osl}"))
                setw(f"ot{osl}", t_old)
                for half in range(2):
                    bank_t, wts, bk = pool.acquire()
                    tm = None
                    for f in range(8):
                        tm = p.op("tensor", "matmul",
                                  waits=(wts + [t_old] + (w_init if (t5 == 0 and sub == 0 and half == 0 and f == 0) else []) if f == 0 else []),
                                  out=bank_t[:], lhsT=ot[osl][:, f * 128:(f + 1) * 128],
                                  rhs=woutb[:, f * 1024 + half * 512: f * 1024 + half * 512 + 512],
                                  start=(f == 0), stop=(f == 7))
                    pool.set_writer(bk, tm)
                    addr(f"ot{osl}", tm)
                    te = p.op("vector", "tensor_copy",
                              waits=[tm] + (ww("acc") if (sub == 0 and half == 0) else []),
                              out=acc[:, sub * 1024 + half * 512: sub * 1024 + half * 512 + 512],
                              in_=bank_t[:])
                    pool.add_reader(bk, te)
                    setw("acc", te, keep=not (sub == 0 and half == 0))
            for sub in range(4):
                t0 = q0 + sub * 128
                bank_t, wts, bk = pool.acquire()
                tm = None
                for f in range(8):
                    tm = p.op("tensor", "matmul", waits=(wts + rw("xnf") if f == 0 else []),
                              out=bank_t[:, 0:16],
                              lhsT=xnf[:, f * Q + t0: f * Q + t0 + 128],
                              rhs=rwTb[:, f * 16:(f + 1) * 16],
                              start=(f == 0), stop=(f == 7))
                pool.set_writer(bk, tm)
                addr("xnf", tm)
                t_cp = p.op("vector", "tensor_copy", waits=[tm] + ww("logt"), out=logt[:],
                            in_=bank_t[:, 0:16])
                pool.add_reader(bk, t_cp)
                setw("logt", t_cp)
                t_sc = p.op("scalar", "activation", waits=[t_cp] + ww("scor"), out=scor[:],
                            in_=logt[:], func=AF.Sigmoid)
                setw("scor", t_sc)
                addr("logt", t_sc)
                t_sb = p.op("vector", "tensor_tensor", waits=[t_sc, tc[2]] + ww("sbt"),
                            out=sbt[:], in0=scor[:], in1=cb_bc[:], op=ALU.add)
                setw("sbt", t_sb)
                tprev = t_sb
                for g4 in range(4):
                    gs = slice(g4 * 4, g4 * 4 + 4)
                    t_m1 = p.op("vector", "reduce_max", waits=[tprev] + (ww("gm") if g4 == 0 else []),
                                out=gm[:], in_=sbt[:, gs], axis=mybir.AxisListType.X)
                    setw("gm", t_m1)
                    t_g1 = p.op("vector", "tensor_scalar", waits=[t_m1] + (ww("mskd") if g4 == 0 else []),
                                out=mskd[:, gs], in0=sbt[:, gs], scalar1=gm[:], scalar2=None,
                                op0=ALU.is_ge)
                    setw("mskd", t_g1, keep=(g4 > 0))
                    t_g2 = p.op("vector", "scalar_tensor_tensor", waits=[t_g1],
                                out=mskd[:, gs], in0=mskd[:, gs], scalar=NEG, in1=sbt[:, gs],
                                op0=ALU.mult, op1=ALU.add)
                    setw("mskd", t_g2, keep=True)
                    t_m2 = p.op("vector", "reduce_max", waits=[t_g2] + (ww("gsc") if g4 == 0 else []),
                                out=gsc[:, g4:g4 + 1], in_=mskd[:, gs], axis=mybir.AxisListType.X)
                    setw("gsc", t_m2, keep=(g4 > 0))
                    t_m3 = p.op("vector", "tensor_tensor", waits=[t_m2, t_m1],
                                out=gsc[:, g4:g4 + 1], in0=gsc[:, g4:g4 + 1], in1=gm[:], op=ALU.add)
                    setw("gsc", t_m3, keep=True)
                    addr("gm", t_m3)
                    tprev = t_m3
                t_t1 = p.op("vector", "reduce_max", waits=rw("gsc") + ww("gm"), out=gm[:],
                            in_=gsc[:], axis=mybir.AxisListType.X)
                setw("gm", t_t1)
                t_t2 = p.op("vector", "tensor_scalar", waits=[t_t1] + ww("gmask"), out=gmask[:],
                            in0=gsc[:], scalar1=gm[:], scalar2=None, op0=ALU.is_ge)
                setw("gmask", t_t2)
                t_t3 = p.op("vector", "scalar_tensor_tensor", waits=[t_t2] + ww("tr16"),
                            out=tr16[:, 0:4], in0=gmask[:], scalar=NEG, in1=gsc[:],
                            op0=ALU.mult, op1=ALU.add)
                setw("tr16", t_t3)
                addr("gsc", t_t3)
                addr("gmask", t_t3)
                t_t4 = p.op("vector", "reduce_max", waits=[t_t3] + ww("gm"), out=gm[:],
                            in_=tr16[:, 0:4], axis=mybir.AxisListType.X)
                setw("gm", t_t4)
                addr("tr16", t_t4)
                t_t5x = p.op("vector", "tensor_scalar", waits=[t_t4] + rw("gsc") + ww("gmask"),
                             out=gmask[:], in0=gsc[:], scalar1=gm[:], scalar2=None, op0=ALU.is_ge)
                setw("gmask", t_t5x)
                addr("gm", t_t5x)
                addr("gsc", t_t5x)
                t_sm = None
                for j in range(4):
                    t_sm = p.op("vector", "tensor_copy",
                                waits=[t_t5x] + (ww("smask") if j == 0 else []),
                                out=smask[:].rearrange("p (g j) -> p g j", g=4)[:, :, j],
                                in_=gmask[:])
                    setw("smask", t_sm, keep=(j > 0))
                t_mk = p.op("vector", "tensor_scalar", waits=[t_sm] + ww("mskd"),
                            out=mskd[:], in0=smask[:], scalar1=1.0e30, scalar2=-1.0e30,
                            op0=ALU.mult, op1=ALU.add)
                setw("mskd", t_mk)
                addr("smask", t_mk)
                t_mk2 = p.op("vector", "tensor_tensor", waits=[t_mk] + rw("sbt"), out=mskd[:],
                             in0=mskd[:], in1=sbt[:], op=ALU.add)
                addr("sbt", t_mk2)
                setw("mskd", t_mk2)
                tprev = t_mk2
                for r4 in range(3):
                    t_m = p.op("vector", "reduce_max", waits=[tprev] + ww("m4r"), out=m4r[:],
                               in_=mskd[:], axis=mybir.AxisListType.X)
                    setw("m4r", t_m)
                    t_g = p.op("vector", "tensor_scalar", waits=[t_m] + ww("selr"), out=selr[:],
                               in0=mskd[:], scalar1=m4r[:], scalar2=None, op0=ALU.is_ge)
                    setw("selr", t_g)
                    tprev = p.op("vector", "scalar_tensor_tensor", waits=[t_g], out=mskd[:],
                                 in0=selr[:], scalar=NEG, in1=mskd[:], op0=ALU.mult, op1=ALU.add)
                    setw("mskd", tprev, keep=True)
                    addr("selr", tprev)
                t_m4 = p.op("vector", "reduce_max", waits=[tprev] + ww("m4r"), out=m4r[:],
                            in_=mskd[:], axis=mybir.AxisListType.X)
                setw("m4r", t_m4)
                addr("mskd", t_m4)
                t_mo = p.op("vector", "tensor_scalar",
                            waits=[t_m4] + rw("smask") + ww("tr16"),
                            out=tr16[:], in0=smask[:], scalar1=1.0e30, scalar2=-1.0e30,
                            op0=ALU.mult, op1=ALU.add)
                setw("tr16", t_mo)
                addr("smask", t_mo)
                t_mo2 = p.op("vector", "tensor_tensor", waits=[t_mo] + rw("sbt"), out=tr16[:],
                             in0=tr16[:], in1=sbt[:], op=ALU.add)
                addr("sbt", t_mo2)
                setw("tr16", t_mo2)
                t_se = p.op("vector", "tensor_scalar", waits=[t_mo2] + rw("m4r") + ww("selr"),
                            out=selr[:], in0=tr16[:], scalar1=m4r[:], scalar2=None, op0=ALU.is_ge)
                setw("selr", t_se)
                addr("tr16", t_se)
                addr("m4r", t_se)
                t_wr = p.op("vector", "tensor_tensor", waits=[t_se] + rw("scor") + ww("wr"),
                            out=wr[:], in0=scor[:], in1=selr[:], op=ALU.mult)
                setw("wr", t_wr)
                addr("scor", t_wr)
                addr("selr", t_wr)
                t_dsm = p.op("vector", "tensor_scalar", waits=[t_wr] + ww("dsum") + ww("tr16"),
                             out=tr16[:], in0=wr[:], scalar1=1.0, scalar2=None,
                             op0=ALU.mult, op1=ALU.add, accum_out=dsum[:])
                setw("dsum", t_dsm)
                setw("tr16", t_dsm)
                t_dse = p.op("vector", "tensor_scalar", waits=[t_dsm], out=dsum[:], in0=dsum[:],
                             scalar1=1e-20, scalar2=None, op0=ALU.add)
                setw("dsum", t_dse)
                t_dsi = p.op("vector", "reciprocal", waits=[t_dse], out=dsum[:], in_=dsum[:])
                setw("dsum", t_dsi)
                t_cmb = p.op("vector", "tensor_scalar", waits=[t_dsi] + rw("wr") + ww("comb"),
                             out=comb[:], in0=wr[:], scalar1=dsum[:], scalar2=None, op0=ALU.mult)
                setw("comb", t_cmb)
                addr("wr", t_cmb)
                addr("dsum", t_cmb)
                bank_t, wts, bk = pool.acquire(BF16)
                ttp = p.op("tensor", "transpose", waits=wts + [t_cmb], out=bank_t[0:16, 0:128],
                           in_=comb[:], identity=identb[:])
                pool.set_writer(bk, ttp)
                addr("comb", ttp)
                te = p.op("vector", "tensor_copy",
                          waits=[ttp] + (ww("combf") if sub == 0 else []),
                          out=combf[0:16, sub * 128:(sub + 1) * 128],
                          in_=bank_t[0:16, 0:128])
                pool.add_reader(bk, te)
                setw("combf", te, keep=(sub > 0))
            for hc in range(16):
                sl = hc % 2
                for (wsrc, wbuf, wnm, cnm) in ((sg_fm, wsg, "wsg", "sgc"), (su_fm, wsu, "wsu", "suc")):
                    t_l = p.dma("sync", wbuf[sl][:],
                                wsrc[:, hc * 1024:(hc + 1) * 1024],
                                waits=ww(f"{wnm}{sl}") + (rw(cnm) if t5 == 0 else []))
                    setw(f"{wnm}{sl}", t_l)
                bank_g, wts_g, bk_g = pool.acquire()
                tm_g = None
                for f in range(8):
                    tm_g = p.op("tensor", "matmul",
                                waits=(wts_g + rw(f"wsg{sl}") + rw("xnf") if f == 0 else []),
                                out=bank_g[:], lhsT=wsg[sl][:, f * 128:(f + 1) * 128],
                                rhs=xnf[:, f * Q + q0: f * Q + q0 + 512],
                                start=(f == 0), stop=(f == 7))
                pool.set_writer(bk_g, tm_g)
                addr(f"wsg{sl}", tm_g)
                bank_u, wts_u, bk_u = pool.acquire()
                tm_u = None
                for f in range(8):
                    tm_u = p.op("tensor", "matmul",
                                waits=(wts_u + rw(f"wsu{sl}") if f == 0 else []),
                                out=bank_u[:], lhsT=wsu[sl][:, f * 128:(f + 1) * 128],
                                rhs=xnf[:, f * Q + q0: f * Q + q0 + 512],
                                start=(f == 0), stop=(f == 7))
                pool.set_writer(bk_u, tm_u)
                addr(f"wsu{sl}", tm_u)
                t_si = p.op("scalar", "activation", waits=[tm_g] + ww("silp"), out=silp[:],
                            in_=bank_g[:], func=AF.Silu)
                pool.add_reader(bk_g, t_si)
                setw("silp", t_si)
                t_h = p.op("vector", "tensor_tensor", waits=[t_si, tm_u] + ww(f"hsh{hc}"),
                           out=hsh[:, hc * 512:(hc + 1) * 512], in0=silp[:], in1=bank_u[:],
                           op=ALU.mult)
                pool.add_reader(bk_u, t_h)
                setw(f"hsh{hc}", t_h)
                addr("silp", t_h)
            for sub in range(4):
                for half in range(2):
                    bank_t, wts, bk = pool.acquire()
                    tm = None
                    for hc in range(16):
                        tm = p.op("tensor", "matmul",
                                  waits=(wts + rw(f"hsh{hc}") if hc == 0 else rw(f"hsh{hc}")),
                                  out=bank_t[:],
                                  lhsT=hsh[:, hc * 512 + sub * 128: hc * 512 + sub * 128 + 128],
                                  rhs=sdb[:, hc * 1024 + half * 512: hc * 1024 + half * 512 + 512],
                                  start=(hc == 0), stop=(hc == 15))
                        addr(f"hsh{hc}", tm)
                    pool.set_writer(bk, tm)
                    t_ac = p.op("vector", "tensor_tensor", waits=[tm] + ww("acc"),
                                out=acc[:, sub * 1024 + half * 512: sub * 1024 + half * 512 + 512],
                                in0=bank_t[:],
                                in1=acc[:, sub * 1024 + half * 512: sub * 1024 + half * 512 + 512],
                                op=ALU.add)
                    pool.add_reader(bk, t_ac)
                    setw("acc", t_ac, keep=True)
            for eg in range(4):
                for ei in range(4):
                    e = eg * 4 + ei
                    sl = e % 2
                    t_lg = p.dma("sync", wge[sl][:],
                                 gw_fm[:, e * 2048:(e + 1) * 2048],
                                 waits=ww(f"wge{sl}") + (rw("gwc") if t5 == 0 else []))
                    setw(f"wge{sl}", t_lg)
                    t_lu = p.dma("sync", wue[sl][:],
                                 uw_fm[:, e * 2048:(e + 1) * 2048],
                                 waits=ww(f"wue{sl}") + (rw("uwc") if t5 == 0 else []))
                    setw(f"wue{sl}", t_lu)
                    bank_w, wts_w, bk_w = pool.acquire()
                    t_wb = p.op("tensor", "matmul", waits=wts_w + rw("combf"),
                                out=bank_w[:], lhsT=sel16[:, e * 128:(e + 1) * 128],
                                rhs=combf[0:16, :], start=True, stop=True)
                    pool.set_writer(bk_w, t_wb)
                    addr("combf", t_wb)
                    for c2 in range(2):
                        bank_g, wts_g, bk_g = pool.acquire()
                        tm_g = None
                        for f in range(8):
                            tm_g = p.op("tensor", "matmul",
                                        waits=(wts_g + [t_lg] if f == 0 else []),
                                        out=bank_g[:],
                                        lhsT=wge[sl][:, f * 256 + c2 * 128: f * 256 + c2 * 128 + 128],
                                        rhs=xnf[:, f * Q + q0: f * Q + q0 + 512],
                                        start=(f == 0), stop=(f == 7))
                        pool.set_writer(bk_g, tm_g)
                        addr(f"wge{sl}", tm_g)
                        bank_u, wts_u, bk_u = pool.acquire()
                        tm_u = None
                        for f in range(8):
                            tm_u = p.op("tensor", "matmul",
                                        waits=(wts_u + [t_lu] if f == 0 else []),
                                        out=bank_u[:],
                                        lhsT=wue[sl][:, f * 256 + c2 * 128: f * 256 + c2 * 128 + 128],
                                        rhs=xnf[:, f * Q + q0: f * Q + q0 + 512],
                                        start=(f == 0), stop=(f == 7))
                        pool.set_writer(bk_u, tm_u)
                        addr(f"wue{sl}", tm_u)
                        t_si = p.op("scalar", "activation", waits=[tm_g] + ww("silp"), out=silp[:],
                                    in_=bank_g[:], func=AF.Silu)
                        pool.add_reader(bk_g, t_si)
                        setw("silp", t_si)
                        t_hx = p.op("vector", "tensor_tensor", waits=[t_si, tm_u] + ww("hxe"),
                                    out=hxe[:], in0=silp[:], in1=bank_u[:], op=ALU.mult)
                        pool.add_reader(bk_u, t_hx)
                        setw("hxe", t_hx)
                        addr("silp", t_hx)
                        t_hw = p.op("vector", "tensor_tensor",
                                    waits=[t_hx, t_wb] + ww(f"hwe_{ei}_{c2}"),
                                    out=hwe[:, (ei * 2 + c2) * 512:(ei * 2 + c2 + 1) * 512],
                                    in0=hxe[:], in1=bank_w[:], op=ALU.mult)
                        pool.add_reader(bk_w, t_hw)
                        setw(f"hwe_{ei}_{c2}", t_hw)
                        addr("hxe", t_hw)
                    t_ld = p.dma("sync", wde[ei][:, 0:1024],
                                 dw_bf[e * 256:e * 256 + 128, :],
                                 waits=ww(f"wde{ei}") + (rw("dwc") if t5 == 0 else []))
                    setw(f"wde{ei}", t_ld)
                    t_ld2 = p.dma("sync", wde[ei][:, 1024:2048],
                                  dw_bf[e * 256 + 128:e * 256 + 256, :], waits=[])
                    setw(f"wde{ei}", t_ld2, keep=True)
                for sub in range(4):
                    for half in range(2):
                        bank_t, wts, bk = pool.acquire()
                        tm = None
                        nmm = 0
                        for ei in range(4):
                            for c2 in range(2):
                                tm = p.op("tensor", "matmul",
                                          waits=((wts if nmm == 0 else []) + rw(f"hwe_{ei}_{c2}") +
                                                 rw(f"wde{ei}")),
                                          out=bank_t[:],
                                          lhsT=hwe[:, (ei * 2 + c2) * 512 + sub * 128:
                                                   (ei * 2 + c2) * 512 + sub * 128 + 128],
                                          rhs=wde[ei][:, c2 * 1024 + half * 512:
                                                      c2 * 1024 + half * 512 + 512],
                                          start=(nmm == 0), stop=(nmm == 7))
                                addr(f"hwe_{ei}_{c2}", tm)
                                addr(f"wde{ei}", tm)
                                nmm += 1
                        pool.set_writer(bk, tm)
                        t_ac = p.op("vector", "tensor_tensor", waits=[tm] + ww("acc"),
                                    out=acc[:, sub * 1024 + half * 512: sub * 1024 + half * 512 + 512],
                                    in0=bank_t[:],
                                    in1=acc[:, sub * 1024 + half * 512: sub * 1024 + half * 512 + 512],
                                    op=ALU.add)
                        pool.add_reader(bk, t_ac)
                        setw("acc", t_ac, keep=True)
            for sub in range(4):
                t0 = q0 + sub * 128
                t_l1 = p.dma("sync", x1t[:], x1q[t0:t0 + 128, :], waits=ww("x1t"))
                setw("x1t", t_l1)
                t_x2 = p.op("vector", "tensor_tensor", waits=[t_l1] + rw("acc") + ww("x2t"),
                            out=x2t[:], in0=acc[:, sub * 1024:(sub + 1) * 1024], in1=x1t[:],
                            op=ALU.add)
                setw("x2t", t_x2)
                addr("acc", t_x2)
                addr("x1t", t_x2)
                t_st = p.dma("gpsimd", x2qo[t0:t0 + 128, :], x2t[:], waits=[t_x2])
                addr("x2t", t_st)
        p.waits("gpsimd", p.barrier_tokens())
        p.finalize()
    return nc


def run_E(x1, xn1b, og, params):
    nc = _get("E", build_launch_E)
    eye = np.eye(128, dtype=np.float32)
    sel16 = np.zeros((16, 16 * 128), np.float32)
    for e in range(16):
        sel16[e, e * 128:(e + 1) * 128] = 1.0
    consts = {"identb": cbf(eye), "sel16": cbf(sel16)}
    gwT = f32(np.transpose(params["L1_ffn_gate_w"], (1, 0, 2)).reshape(D, 4096))
    uwT = f32(np.transpose(params["L1_ffn_up_w"], (1, 0, 2)).reshape(D, 4096))
    dwT = f32(params["L1_ffn_down_w"].reshape(4096, D))
    in_maps = []
    for c in range(NCORE):
        b, q = divmod(c, 4)
        tsl = slice(q * Q, (q + 1) * Q)
        im = {
            "x1q": f32(x1[b, tsl]),
            "xnfm": np.ascontiguousarray(xn1b[b, tsl].T),
            "ofm": np.ascontiguousarray(og[b, tsl].T),
            "wout1": f32(params["L1_attn_Wout"]),
            "sgwT": f32(params["L1_ffn_shared_gate_w"].T),
            "suwT": f32(params["L1_ffn_shared_up_w"].T),
            "sdwT": f32(params["L1_ffn_shared_down_w"].T),
            "gwT": gwT, "uwT": uwT, "dwT": dwT,
            "rwT": f32(params["L1_ffn_router_w"].T),
            "cbias": f32(np.broadcast_to(params["L1_ffn_correction_bias"][None, :], (128, 16))),
        }
        im.update(consts)
        in_maps.append(im)
    res = run_bass_kernel_spmd(nc, in_maps, core_ids=list(range(NCORE)))
    global _last_res
    _last_res = res
    x2 = np.empty((B, T, D), np.float32)
    for c in range(NCORE):
        b, q = divmod(c, 4)
        x2[b, q * Q:(q + 1) * Q] = res.results[c]["x2q"]
    return x2


def kernel(**inputs):
    params = inputs
    x0 = inputs["L0_attn_x"]
    xn0, xn0b = run_A(x0)
    yb = run_B(xn0b, params)
    c_outs = run_C(x0, xn0, xn0b, yb, params)
    og = run_D(c_outs["z1qb"], params)
    x2 = run_E(c_outs["x1q"], c_outs["xn1qb"], og, params)
    return x2


if __name__ == "__main__":
    pass


# version 04 (final)
# speedup vs baseline: 1.0380x; 1.0380x over previous
import sys

sys.path.insert(0, "/opt/trn_rl_repo")
from contextlib import ExitStack

import numpy as np
import ml_dtypes
import concourse.bass as bass
import concourse.mybir as mybir
from concourse.bass_utils import run_bass_kernel_spmd

F32 = mybir.dt.float32
BF16 = mybir.dt.bfloat16
AF = mybir.ActivationFunctionType
ALU = mybir.AluOpType
BF = ml_dtypes.bfloat16

B, T, D = 2, 8192, 1024
S = B * T
H, DK, DV = 16, 64, 128
NCORE = 8
Q = T // 4
CH = 128
NCH = T // CH

ENGINES = ("sync", "scalar", "vector", "tensor", "gpsimd")


def f32(a):
    return np.ascontiguousarray(a, dtype=np.float32)


def cbf(a):
    return np.ascontiguousarray(np.asarray(a).astype(BF))


class Prog:

    N_DMA_SEMS = 12

    def __init__(self, nc, ctx):
        self.nc = nc
        self.items = {e: [] for e in ENGINES}
        E = ctx.enter_context
        self.eng_sem = {e: E(nc.semaphore(f"seq_{e}")) for e in ENGINES}
        self.eng_cnt = {e: 0 for e in ENGINES}
        self.dma_sems = [E(nc.semaphore(f"dma{i}")) for i in range(self.N_DMA_SEMS)]
        self.dma_cnt = [0] * self.N_DMA_SEMS
        self.dma_rr = 0

    def wait(self, eng, tok):
        if tok is not None:
            self.items[eng].append(("w", tok))

    def waits(self, eng, toks):
        for t in toks:
            self.wait(eng, t)

    def op(self, eng, method, *args, waits=(), inc=None, **kwargs):
        for t in waits:
            self.wait(eng, t)
        self.eng_cnt[eng] += 1
        tok = (self.eng_sem[eng], self.eng_cnt[eng])
        incs = [(self.eng_sem[eng], 1)]
        if inc:
            incs.append(inc)
        self.items[eng].append(("c", method, args, kwargs, incs))
        return tok

    def dma(self, eng, out, in_, waits=(), **kwargs):
        slot = self.dma_rr
        self.dma_rr = (self.dma_rr + 1) % self.N_DMA_SEMS
        sem = self.dma_sems[slot]
        if self.dma_cnt[slot] > 0:
            self.wait(eng, (sem, 16 * self.dma_cnt[slot]))
        for t in waits:
            self.wait(eng, t)
        self.dma_cnt[slot] += 1
        tok = (sem, 16 * self.dma_cnt[slot])
        self.items[eng].append(("c", "dma_start", (), {"out": out, "in_": in_, **kwargs},
                               [(sem, 16)]))
        return tok

    def barrier_tokens(self):
        toks = [(self.eng_sem[e], self.eng_cnt[e]) for e in ENGINES if self.eng_cnt[e]]
        toks += [(self.dma_sems[i], 16 * self.dma_cnt[i])
                 for i in range(self.N_DMA_SEMS) if self.dma_cnt[i]]
        return toks

    def finalize(self):
        nc = self.nc
        with nc.Block() as block:
            def mk(eng_name):
                def body(eng):
                    for it in self.items[eng_name]:
                        if it[0] == "w":
                            eng.wait_ge(it[1][0], it[1][1])
                        else:
                            _, method, args, kwargs, incs = it
                            instr = getattr(eng, method)(*args, **kwargs)
                            for (sem, n) in incs:
                                instr.then_inc(sem, n)
                return body
            block.sync(mk("sync"))
            block.scalar(mk("scalar"))
            block.vector(mk("vector"))
            block.tensor(mk("tensor"))
            block.gpsimd(mk("gpsimd"))




class PsumPool:

    def __init__(self, nc, ctx, n_f32=7, bf16_banks=1):
        E = ctx.enter_context
        self.banks = []
        for i in range(n_f32):
            t = E(nc.psum_tensor(f"ps{i}", [128, 512], F32))
            self.banks.append({"t": t, "dtype": F32, "readers": [], "busy_until": None})
        for i in range(bf16_banks):
            t = E(nc.psum_tensor(f"psb{i}", [128, 1024], BF16))
            self.banks.append({"t": t, "dtype": BF16, "readers": [], "busy_until": None})
        self.rr = 0

    def acquire(self, dtype=F32, pin=False):
        n = len(self.banks)
        for k in range(n):
            i = (self.rr + k) % n
            if self.banks[i]["dtype"] == dtype and not self.banks[i].get("pinned"):
                self.rr = (i + 1) % n
                b = self.banks[i]
                waits = list(b["readers"])
                if b["busy_until"] is not None:
                    waits.append(b["busy_until"])
                b["readers"] = []
                b["busy_until"] = None
                b["pinned"] = pin
                self._last = i
                return b["t"], waits, i

    def set_writer(self, bank, tok):
        self.banks[bank]["busy_until"] = tok

    def add_reader(self, bank, tok):
        self.banks[bank]["readers"].append(tok)

    def unpin(self, bank):
        self.banks[bank]["pinned"] = False


def new_nc():
    return bass.Bass("TRN2", target_bir_lowering=False, debug=False,
                     detect_race_conditions=False)



def build_launch_A():
    nc = new_nc()
    x = nc.dram_tensor("x", [Q, D], F32, kind="ExternalInput").ap()
    xn = nc.dram_tensor("xn", [Q, D], F32, kind="ExternalOutput").ap()
    xnb = nc.dram_tensor("xnb", [Q, D], BF16, kind="ExternalOutput").ap()
    NT = Q // 128
    with ExitStack() as ctx:
        E = ctx.enter_context
        p = Prog(nc, ctx)
        xt = [E(nc.sbuf_tensor(f"xt{i}", [128, D], F32)) for i in range(2)]
        tr_s = E(nc.sbuf_tensor("tr_s", [128, D], BF16))
        tr_v = E(nc.sbuf_tensor("tr_v", [128, D], BF16))
        xno = [E(nc.sbuf_tensor(f"xno{i}", [128, D], F32)) for i in range(2)]
        xnob = [E(nc.sbuf_tensor(f"xnob{i}", [128, D], BF16)) for i in range(2)]
        ssq = E(nc.sbuf_tensor("ssq", [128, 1], F32))
        ssum = E(nc.sbuf_tensor("ssum", [128, 1], F32))
        mu = E(nc.sbuf_tensor("mu", [128, 1], F32))
        var = E(nc.sbuf_tensor("var", [128, 1], F32))
        sstd = E(nc.sbuf_tensor("sstd", [128, 1], F32))
        rstd = E(nc.sbuf_tensor("rstd", [128, 1], F32))

        t_done = [[], []]
        t_stat = None
        for i in range(NT):
            sl = i % 2
            t_ld = p.dma("sync", xt[sl][:], x[i * 128:(i + 1) * 128, :], waits=t_done[sl])
            w2 = [t_ld] + ([t_stat] if t_stat is not None else [])
            t_sq = p.op("scalar", "activation", waits=w2, out=tr_s[:], in_=xt[sl][:],
                        func=AF.Square, accum_out=ssq[:])
            t_sm = p.op("vector", "tensor_scalar", waits=w2, out=tr_v[:], in0=xt[sl][:],
                        scalar1=1.0, scalar2=None, op0=ALU.mult, op1=ALU.add,
                        accum_out=ssum[:])
            t_mu = p.op("vector", "tensor_scalar", waits=[t_sm], out=mu[:], in0=ssum[:],
                        scalar1=1.0 / D, scalar2=None, op0=ALU.mult)
            t_v1 = p.op("vector", "scalar_tensor_tensor", waits=[t_mu], out=var[:],
                        in0=mu[:], scalar=-1.0, in1=mu[:], op0=ALU.mult, op1=ALU.mult)
            t_v2 = p.op("vector", "scalar_tensor_tensor", waits=[t_v1, t_sq], out=var[:],
                        in0=ssq[:], scalar=1.0 / D, in1=var[:], op0=ALU.mult, op1=ALU.add)
            t_v3 = p.op("vector", "tensor_scalar", waits=[t_v2], out=var[:], in0=var[:],
                        scalar1=1e-5, scalar2=None, op0=ALU.add)
            t_sd = p.op("scalar", "activation", waits=[t_v3], out=sstd[:], in_=var[:],
                        func=AF.Sqrt)
            t_rs = p.op("vector", "reciprocal", waits=[t_sd], out=rstd[:], in_=sstd[:])
            t_xn = p.op("vector", "tensor_scalar", waits=[t_rs], out=xno[sl][:],
                        in0=xt[sl][:], scalar1=mu[:], scalar2=rstd[:],
                        op0=ALU.subtract, op1=ALU.mult)
            t_cb = p.op("vector", "tensor_copy", waits=[t_xn], out=xnob[sl][:],
                        in_=xno[sl][:])
            t_stat = t_xn
            t_s1 = p.dma("gpsimd", xn[i * 128:(i + 1) * 128, :], xno[sl][:], waits=[t_xn])
            t_s2 = p.dma("gpsimd", xnb[i * 128:(i + 1) * 128, :], xnob[sl][:], waits=[t_cb])
            t_done[sl] = [t_s1, t_s2]
        if B_DUMP:
            dumps_b = [("qn0", qn[0]), ("qn1", qn[1]), ("kn0", kn[0]), ("kn1", kn[1]),
                       ("v_tm", v_tm), ("k_tm", k_tm), ("o_blk", o_blk),
                       ("qkraw0", qkraw[0]), ("ggraw0", ggraw[0])]
            off = 0
            for nm, buf in dumps_b:
                w = buf.shape[1]
                p.dma("gpsimd", dbgb[:, off:off + w], buf[:, 0:w], waits=ww(nm) + rw(nm))
                off += w
            dumps_f = [("ab_tm", ab_tm, 64), ("g_part", g_part, 32), ("beta_p", beta_p, 32),
                       ("c_all", c_all, 32), ("A_all", A_all, 32), ("dec_all", dec_all, 32),
                       ("gam_all", gam_all, 32), ("negbA", negbA, 32)]
            off = 0
            for nm, buf, w in dumps_f:
                p.dma("gpsimd", dbgf[:, off:off + w], buf[:, 0:w], waits=ww(nm) + rw(nm))
                off += w
        p.waits("gpsimd", p.barrier_tokens())
        p.finalize()
    return nc





BLOCK = 1024
NBLK = T // BLOCK
BCH = BLOCK // CH
NEG = -1.0e30

B_NBLK = NBLK
B_PH = 9
B_DUMP = False
B_SUB = 5
B_BCHL = BCH


def build_launch_B():
    nc = new_nc()
    ap = lambda n, s, d, k: nc.dram_tensor(n, s, d, kind=k).ap()
    xnb = ap("xnb", [D, T], BF16, "ExternalInput")
    wqk = ap("wqk", [D, 512], F32, "ExternalInput")
    wv4 = ap("wv4", [D, 512], F32, "ExternalInput")
    wg4 = ap("wg4", [D, 512], F32, "ExternalInput")
    wab = ap("wab", [D, 8], F32, "ExternalInput")
    convqk = ap("convqk", [512, 4], F32, "ExternalInput")
    convv = ap("convv", [512, 4], F32, "ExternalInput")
    alogb = ap("alogb", [128, 4], F32, "ExternalInput")
    dtbb = ap("dtbb", [128, 4], F32, "ExternalInput")
    normw = ap("normw", [128, 1], F32, "ExternalInput")
    identb_i = ap("identb", [128, 128], BF16, "ExternalInput")
    identf_i = ap("identf", [128, 128], F32, "ExternalInput")
    triu_i = ap("triu", [128, 128], F32, "ExternalInput")
    maskL2_i = ap("maskL2", [128, 256], BF16, "ExternalInput")
    maskU2_i = ap("maskU2", [128, 256], BF16, "ExternalInput")
    maskUi2_i = ap("maskUi2", [128, 256], BF16, "ExternalInput")
    ones1_i = ap("ones1", [1, 128], F32, "ExternalInput")
    sel_lo_i = ap("sel_lo", [1, 128], F32, "ExternalInput")
    sel_hi_i = ap("sel_hi", [1, 128], F32, "ExternalInput")
    blk2_i = ap("blk2", [128, 2], BF16, "ExternalInput")
    blk2T_i = ap("blk2T", [2, 128], F32, "ExternalInput")
    ones_c_i = ap("ones_c", [128, 1], BF16, "ExternalInput")
    csts_i = ap("csts", [128, 4], F32, "ExternalInput")
    yout = ap("y", [512, T], BF16, "ExternalOutput")
    rows_dram = nc.dram_tensor("rows_dram", [4, 32 * 128], F32).ap()
    if B_DUMP:
        dbgb = ap("dbgb", [128, 18432], BF16, "ExternalOutput")
        dbgf = ap("dbgf", [128, 512], F32, "ExternalOutput")

    with ExitStack() as ctx:
        E = ctx.enter_context
        p = Prog(nc, ctx)
        pool = PsumPool(nc, ctx)
        sb = lambda n, s, d: E(nc.sbuf_tensor(n, s, d))

        xn_sb = [sb(f"xn{i}", [128, 8192], BF16) for i in range(2)]
        wqk_b = sb("wqk_b", [128, 4096], BF16)
        wv4_b = sb("wv4_b", [128, 4096], BF16)
        wg4_b = sb("wg4_b", [128, 4096], BF16)
        wab_b = sb("wab_b", [128, 64], BF16)
        wst = [sb(f"wst{i}", [128, 512], F32) for i in range(2)]
        qkraw = [sb(f"qkraw{i}", [128, BLOCK + 3], BF16) for i in range(4)]
        vraw = [sb(f"vraw{i}", [128, BLOCK + 3], BF16) for i in range(4)]
        qn = [sb(f"qn{i}", [128, BLOCK], BF16) for i in range(2)]
        kn = [sb(f"kn{i}", [128, BLOCK], BF16) for i in range(2)]
        vpost = [sb(f"vpost{i}", [128, BLOCK], BF16) for i in range(4)]
        ggraw = [sb(f"ggraw{i}", [128, BLOCK], BF16) for i in range(4)]
        k_tm = sb("k_tm", [128, 2048], BF16)
        v_tm = sb("v_tm", [128, 4096], BF16)
        o_blk = sb("o_blk", [128, 4096], BF16)
        diag = [sb(f"diag{i}", [128, 128], BF16) for i in range(32)]
        convs = [sb(f"convs{i}", [128, 4], F32) for i in range(8)]
        identb = sb("identb_s", [128, 128], BF16)
        identf = sb("identf_s", [128, 128], F32)
        triu = sb("triu_s", [128, 128], F32)
        maskL2 = sb("maskL2_s", [128, 256], BF16)
        maskU2 = sb("maskU2_s", [128, 256], BF16)
        maskUi2 = sb("maskUi2_s", [128, 256], BF16)
        ones1 = sb("ones1_s", [1, 128], F32)
        sel_lo = sb("sel_lo_s", [1, 128], F32)
        sel_hi = sb("sel_hi_s", [1, 128], F32)
        blk2 = sb("blk2_s", [128, 2], BF16)
        blk2T = sb("blk2T_s", [2, 128], F32)
        ones_c = sb("ones_c_s", [128, 1], BF16)
        csts = sb("csts_s", [128, 4], F32)
        normw_s = sb("normw_s", [128, 1], F32)
        dtb_p = sb("dtb_p", [128, 4], F32)
        gm_p = sb("gm_p", [128, 4], F32)
        dtb_bc = sb("dtb_bc", [128, 32], F32)
        gm_bc = sb("gm_bc", [128, 32], F32)
        ab_tm = sb("ab_tm", [128, 64], F32)
        za = sb("za", [128, 32], F32)
        zt1 = sb("zt1", [128, 32], F32)
        zt2 = sb("zt2", [128, 32], F32)
        zt3 = sb("zt3", [128, 32], F32)
        g_part = sb("g_part", [128, 32], F32)
        beta_p = sb("beta_p", [128, 32], F32)
        lnb_p = sb("lnb_p", [128, 32], F32)
        c_all = sb("c_all", [128, 32], F32)
        cb_all = sb("cb_all", [128, 32], F32)
        negc_all = sb("negc_all", [128, 32], F32)
        A_all = sb("A_all", [128, 32], F32)
        negbA = sb("negbA", [128, 32], F32)
        dec_all = sb("dec_all", [128, 32], F32)
        gam_all = sb("gam_all", [128, 32], F32)
        c127 = sb("c127", [1, 32], F32)
        A127 = sb("A127", [1, 32], F32)
        darg = sb("darg", [128, 32], F32)
        stg = [sb(f"stg{i}", [1, 2048], F32) for i in range(2)]
        rowst = sb("rowst", [32, 128], F32)
        Ub = [sb(f"Ub{i}", [128, 256], BF16) for i in range(2)]
        Lb = [sb(f"Lb{i}", [128, 256], BF16) for i in range(2)]
        Db = [sb(f"Db{i}", [128, 256], BF16) for i in range(2)]
        expL = sb("expL", [128, 256], BF16)
        expU = sb("expU", [128, 256], BF16)
        expP = sb("expP", [128, 256], BF16)
        PT = sb("PT", [128, 256], BF16)
        bV = sb("bV", [128, 256], BF16)
        Qs = sb("Qs", [128, 128], BF16)
        Kp = sb("Kp", [128, 128], BF16)
        Sf = [sb(f"Sf{i}", [128, 128], F32) for i in range(2)]
        Sbf = [sb(f"Sbf{i}", [128, 128], BF16) for i in range(2)]
        sq_t = sb("sq_t", [128, 512], BF16)
        ssqe = sb("ssqe", [2, 512], F32)
        lr2 = sb("lr2", [2, 512], F32)
        rr2 = sb("rr2", [2, 512], F32)
        sqb = sb("sqb", [128, 1024], BF16)
        silb = sb("silb", [128, 1024], BF16)
        ssn = sb("ssn", [1, 512], F32)
        lnn = sb("lnn", [1, 512], F32)
        rsn = sb("rsn", [1, 512], F32)
        ytmp = sb("ytmp", [128, 512], BF16)

        st_w, st_r = {}, {}

        def _merge(dct, tok):
            sem, val = tok
            key = id(sem)
            if key not in dct or dct[key][1] < val:
                dct[key] = (sem, val)

        def setw(name, tok, keep=False):
            if not keep or name not in st_w:
                st_w[name] = {}
            if not keep:
                st_r[name] = {}
            _merge(st_w[name], tok)

        def addw(name, tok):
            setw(name, tok, keep=True)

        def addr(name, tok):
            d = st_r.setdefault(name, {})
            _merge(d, tok)

        def rw(name):
            return list(st_w.get(name, {}).values())

        def ww(name):
            return list(st_r.get(name, {}).values()) + rw(name)

        evac_tog = [0]

        def evac(dst_ap, src_ap, waits, bank=None, eng=None):
            if eng is None:
                eng = ("vector", "scalar")[evac_tog[0]]
                evac_tog[0] ^= 1
            if eng == "vector":
                t = p.op("vector", "tensor_copy", waits=waits, out=dst_ap, in_=src_ap)
            else:
                t = p.op("scalar", "copy", waits=waits, out=dst_ap, in_=src_ap)
            if bank is not None:
                pool.add_reader(bank, t)
            return t

        tc = []
        for apx, buf in ((identb_i, identb), (identf_i, identf), (triu_i, triu),
                         (maskL2_i, maskL2), (maskU2_i, maskU2), (maskUi2_i, maskUi2),
                         (ones1_i, ones1), (sel_lo_i, sel_lo), (sel_hi_i, sel_hi),
                         (blk2_i, blk2), (blk2T_i, blk2T), (ones_c_i, ones_c),
                         (normw, normw_s), (csts_i, csts), (dtbb, dtb_p), (alogb, gm_p)):
            tc.append(p.dma("sync", buf[:], apx[:, :] if len(apx.shape) == 2 else apx[:]))
        t_identb = tc[0]
        for (wsrc, wdst) in ((wqk, wqk_b), (wv4, wv4_b), (wg4, wg4_b)):
            for f in range(8):
                sl = f % 2
                t_l = p.dma("sync", wst[sl][:], wsrc[f * 128:(f + 1) * 128, :], waits=ww(f"wst{sl}"))
                setw(f"wst{sl}", t_l)
                t_c = p.op("vector", "tensor_copy", waits=[t_l],
                           out=wdst[:, f * 512:(f + 1) * 512], in_=wst[sl][:])
                addr(f"wst{sl}", t_c)
                tc.append(t_c)
        for f in range(8):
            sl = f % 2
            t_l = p.dma("sync", wst[sl][:, 0:8], wab[f * 128:(f + 1) * 128, :], waits=ww(f"wst{sl}"))
            setw(f"wst{sl}", t_l)
            t_c = p.op("vector", "tensor_copy", waits=[t_l],
                       out=wab_b[:, f * 8:(f + 1) * 8], in_=wst[sl][:, 0:8])
            addr(f"wst{sl}", t_c)
            tc.append(t_c)
        for g in range(8):
            srcw = convqk if g < 4 else convv
            row0 = (g % 4) * 128
            t_l = p.dma("sync", convs[g][:], srcw[row0:row0 + 128, :])
            for tap in range(4):
                t_d = p.op("vector", "tensor_scalar", waits=[t_l, t_identb],
                           out=diag[g * 4 + tap][:], in0=identb[:],
                           scalar1=convs[g][:, tap:tap + 1], scalar2=None, op0=ALU.mult)
                tc.append(t_d)
        t_gm1 = p.op("scalar", "activation", waits=[tc[15]], out=gm_p[:], in_=gm_p[:], func=AF.Exp)
        t_gm2 = p.op("vector", "tensor_scalar", waits=[t_gm1], out=gm_p[:], in0=gm_p[:],
                     scalar1=-1.0, scalar2=None, op0=ALU.mult)
        for c in range(8):
            t1 = p.op("vector", "tensor_copy", waits=[tc[14]], out=dtb_bc[:, c:32:8], in_=dtb_p[:])
            t2 = p.op("vector", "tensor_copy", waits=[t_gm2], out=gm_bc[:, c:32:8], in_=gm_p[:])
            setw("dtb_bc", t1, keep=(c > 0))
            setw("gm_bc", t2, keep=(c > 0))
        for i in range(2):
            t = p.op("vector", "memset", Sf[i][:], 0.0)
            setw(f"Sf{i}", t)
            t = p.op("vector", "memset", Sbf[i][:], 0.0)
            setw(f"Sbf{i}", t)
        for i in range(4):
            t = p.op("vector", "memset", qkraw[i][:, 0:3], 0.0)
            setw(f"qkraw{i}", t)
            t = p.op("vector", "memset", vraw[i][:, 0:3], 0.0)
            setw(f"vraw{i}", t)
        w_all_consts = list(tc) + [t_gm2] + rw("dtb_bc") + rw("gm_bc")

        for bi in range(B_NBLK):
            t0 = bi * BLOCK
            xs = xn_sb[bi % 2]
            xname = f"xn{bi % 2}"
            src3 = xnb.rearrange("(f p) t -> p f t", p=128)
            xw = ww(xname)
            t_x0 = p.dma("sync", xs[:].rearrange("p (f t) -> p f t", f=8)[:, :, 0:512],
                         src3[:, :, t0:t0 + 512], waits=xw)
            t_x1 = p.dma("sync", xs[:].rearrange("p (f t) -> p f t", f=8)[:, :, 512:1024],
                         src3[:, :, t0 + 512:t0 + 1024], waits=list(xw))
            setw(xname, t_x0)
            addw(xname, t_x1)
            t_xloads = [t_x0, t_x1]

            for tt in range(2):
                toff = tt * 512
                for grp in range(12):
                    wsel, gi = (wqk_b, grp) if grp < 4 else \
                        ((wv4_b, grp - 4) if grp < 8 else (wg4_b, grp - 8))
                    if grp < 4:
                        dname, dst = f"qkraw{gi}", qkraw[gi][:, 3 + toff:3 + toff + 512]
                    elif grp < 8:
                        dname, dst = f"vraw{gi}", vraw[gi][:, 3 + toff:3 + toff + 512]
                    else:
                        dname, dst = f"ggraw{gi}", ggraw[gi][:, toff:toff + 512]
                    bank_t, wts, bk = pool.acquire()
                    mw = wts + [t_xloads[tt]] + (w_all_consts if bi == 0 and tt == 0 and grp == 0 else [])
                    tm = None
                    for f in range(8):
                        tm = p.op("tensor", "matmul", waits=(mw if f == 0 else []),
                                  out=bank_t[:],
                                  lhsT=wsel[:, f * 512 + gi * 128: f * 512 + gi * 128 + 128],
                                  rhs=xs[:, f * 1024 + toff:f * 1024 + toff + 512],
                                  start=(f == 0), stop=(f == 7))
                    pool.set_writer(bk, tm)
                    te = evac(dst, bank_t[:], [tm] + ww(dname), bank=bk)
                    setw(dname, te, keep=(tt == 1))
                bank_t, wts, bk = pool.acquire()
                tm = None
                for sub in range(4):
                    for f in range(8):
                        tm = p.op("tensor", "matmul",
                                  waits=(wts + [t_xloads[tt]] if (sub == 0 and f == 0) else []),
                                  out=bank_t[:, sub * 8:(sub + 1) * 8],
                                  lhsT=xs[:, f * 1024 + toff + sub * 128: f * 1024 + toff + sub * 128 + 128],
                                  rhs=wab_b[:, f * 8:(f + 1) * 8],
                                  start=(f == 0), stop=(f == 7))
                pool.set_writer(bk, tm)
                abw = ww("ab_tm")
                for sub in range(4):
                    sidx = tt * 4 + sub
                    te = evac(ab_tm[:, sidx:64:8], bank_t[:, sub * 8:(sub + 1) * 8],
                              [tm] + abw, bank=bk, eng="vector")
                    setw("ab_tm", te, keep=(sidx > 0))

            for g in range(8 if B_PH >= 2 else 0):
                rawb = qkraw[g] if g < 4 else vraw[g - 4]
                rname = f"qkraw{g}" if g < 4 else f"vraw{g - 4}"
                for tt in range(2):
                    toff = tt * 512
                    bank_t, wts, bk = pool.acquire()
                    tm = None
                    for tap in range(4):
                        tm = p.op("tensor", "matmul", waits=(wts + rw(rname) if tap == 0 else []),
                                  out=bank_t[:], lhsT=diag[g * 4 + tap][:],
                                  rhs=rawb[:, tap + toff: tap + toff + 512],
                                  start=(tap == 0), stop=(tap == 3))
                    pool.set_writer(bk, tm)
                    if g < 4:
                        dst, dname = (qn, "qn") if g < 2 else (kn, "kn")
                        gi = g % 2
                        te = p.op("scalar", "activation", waits=[tm] + ww(f"{dname}{gi}"),
                                  out=dst[gi][:, toff:toff + 512], in_=bank_t[:], func=AF.Silu)
                        setw(f"{dname}{gi}", te, keep=(tt == 1))
                    else:
                        gi = g - 4
                        te = p.op("scalar", "activation", waits=[tm] + ww(f"vpost{gi}"),
                                  out=vpost[gi][:, toff:toff + 512], in_=bank_t[:], func=AF.Silu)
                        setw(f"vpost{gi}", te, keep=(tt == 1))
                    pool.add_reader(bk, te)
                    addr(rname, te)

            for g in range(4 if B_PH >= 3 else 0):
                buf, bname = (qn[g], f"qn{g}") if g < 2 else (kn[g - 2], f"kn{g - 2}")
                for tt in range(2):
                    toff = tt * 512
                    t_sq = p.op("scalar", "activation", waits=rw(bname) + ww("sq_t"),
                                out=sq_t[:], in_=buf[:, toff:toff + 512], func=AF.Square)
                    setw("sq_t", t_sq)
                    bank_t, wts, bk = pool.acquire()
                    tm = p.op("tensor", "matmul", waits=wts + [t_sq], out=bank_t[0:2, :],
                              lhsT=blk2[:], rhs=sq_t[:], start=True, stop=True)
                    pool.set_writer(bk, tm)
                    addr("sq_t", tm)
                    t_eps = p.op("vector", "tensor_scalar", waits=[tm] + ww("ssqe"),
                                 out=ssqe[:], in0=bank_t[0:2, :], scalar1=1e-6,
                                 scalar2=None, op0=ALU.add)
                    pool.add_reader(bk, t_eps)
                    setw("ssqe", t_eps)
                    t_ln = p.op("scalar", "activation", waits=[t_eps] + ww("lr2"),
                                out=lr2[:], in_=ssqe[:], func=AF.Ln)
                    setw("lr2", t_ln)
                    addr("ssqe", t_ln)
                    t_r = p.op("scalar", "activation", waits=[t_ln] + ww("rr2"),
                               out=rr2[:], in_=lr2[:], func=AF.Exp,
                               scale=csts[0:2, 0:1],
                               bias=(csts[0:2, 1:2] if g < 2 else 0.0))
                    setw("rr2", t_r)
                    addr("lr2", t_r)
                    bank_t2, wts2, bk2 = pool.acquire()
                    tm2 = p.op("tensor", "matmul", waits=wts2 + [t_r], out=bank_t2[:],
                               lhsT=blk2T[:], rhs=rr2[:], start=True, stop=True)
                    pool.set_writer(bk2, tm2)
                    addr("rr2", tm2)
                    t_fin = p.op("vector", "tensor_tensor", waits=[tm2] + rw(bname),
                                 out=buf[:, toff:toff + 512], in0=buf[:, toff:toff + 512],
                                 in1=bank_t2[:], op=ALU.mult)
                    pool.add_reader(bk2, t_fin)
                    addw(bname, t_fin)

            ktw = ww("k_tm")
            vtw = ww("v_tm")
            for c in range(BCH if B_PH >= 4 else 0):
                for pr in range(2):
                    bank_t, wts, bk = pool.acquire(BF16)
                    tt1 = p.op("tensor", "transpose", waits=wts + rw(f"kn{pr}"),
                               out=bank_t[:, 0:128], in_=kn[pr][:, c * 128:(c + 1) * 128],
                               identity=identb[:])
                    pool.set_writer(bk, tt1)
                    addr(f"kn{pr}", tt1)
                    dsl = k_tm[:, (c * 2 + pr) * 128:(c * 2 + pr) * 128 + 128]
                    te = evac(dsl, bank_t[:, 0:128], [tt1] + ktw, bank=bk)
                    setw("k_tm", te, keep=(c > 0 or pr > 0))
                for g in range(4):
                    bank_t, wts, bk = pool.acquire(BF16)
                    tt1 = p.op("tensor", "transpose", waits=wts + rw(f"vpost{g}"),
                               out=bank_t[:, 0:128], in_=vpost[g][:, c * 128:(c + 1) * 128],
                               identity=identb[:])
                    pool.set_writer(bk, tt1)
                    addr(f"vpost{g}", tt1)
                    dsl = v_tm[:, (c * 4 + g) * 128:(c * 4 + g) * 128 + 128]
                    te = evac(dsl, bank_t[:, 0:128], [tt1] + vtw, bank=bk)
                    setw("v_tm", te, keep=(c > 0 or g > 0))

            if B_PH < 5:
                continue
            t1 = p.op("vector", "tensor_tensor", waits=rw("ab_tm") + rw("dtb_bc") + ww("za"),
                      out=za[:], in0=ab_tm[:, 0:32], in1=dtb_bc[:], op=ALU.add)
            setw("za", t1)
            addr("ab_tm", t1)
            t2 = p.op("scalar", "activation", waits=[t1] + ww("zt1"), out=zt1[:], in_=za[:], func=AF.Abs)
            setw("zt1", t2)
            t3 = p.op("vector", "tensor_scalar", waits=[t2], out=zt1[:], in0=zt1[:],
                      scalar1=-1.0, scalar2=None, op0=ALU.mult)
            setw("zt1", t3)
            t4 = p.op("scalar", "activation", waits=[t3] + ww("zt2"), out=zt2[:], in_=zt1[:], func=AF.Exp)
            setw("zt2", t4)
            addr("zt1", t4)
            t5 = p.op("vector", "tensor_scalar", waits=[t4], out=zt2[:], in0=zt2[:],
                      scalar1=1.0, scalar2=None, op0=ALU.add)
            setw("zt2", t5)
            t6 = p.op("scalar", "activation", waits=[t5], out=zt2[:], in_=zt2[:], func=AF.Ln)
            setw("zt2", t6)
            t7 = p.op("scalar", "activation", waits=[t1] + ww("zt3"), out=zt3[:], in_=za[:], func=AF.Relu)
            setw("zt3", t7)
            addr("za", t7)
            t8 = p.op("vector", "tensor_tensor", waits=[t6, t7], out=zt3[:], in0=zt3[:],
                      in1=zt2[:], op=ALU.add)
            setw("zt3", t8)
            addr("zt2", t8)
            t9 = p.op("vector", "tensor_tensor", waits=[t8] + rw("gm_bc") + ww("g_part"),
                      out=g_part[:], in0=zt3[:], in1=gm_bc[:], op=ALU.mult)
            setw("g_part", t9)
            addr("zt3", t9)
            tb1 = p.op("scalar", "activation", waits=rw("ab_tm") + ww("beta_p"),
                       out=beta_p[:], in_=ab_tm[:, 32:64], func=AF.Sigmoid)
            addr("ab_tm", tb1)
            tb2 = p.op("vector", "tensor_scalar", waits=[tb1], out=beta_p[:], in0=beta_p[:],
                       scalar1=2.0, scalar2=None, op0=ALU.mult)
            setw("beta_p", tb2)
            tb3 = p.op("scalar", "activation", waits=[tb2] + ww("lnb_p"), out=lnb_p[:],
                       in_=beta_p[:], func=AF.Ln)
            setw("lnb_p", tb3)
            addr("beta_p", tb3)

            if B_PH < 6:
                continue
            bank_t, wts, bk = pool.acquire()
            tm = p.op("tensor", "matmul", waits=wts + rw("g_part"), out=bank_t[:, 0:32],
                      lhsT=triu[:], rhs=g_part[:], start=True, stop=True)
            pool.set_writer(bk, tm)
            addr("g_part", tm)
            t_c = evac(c_all[:], bank_t[:, 0:32], [tm] + ww("c_all"), bank=bk)
            setw("c_all", t_c)
            t_cb = p.op("vector", "tensor_tensor", waits=rw("c_all") + rw("lnb_p") + ww("cb_all"),
                        out=cb_all[:], in0=c_all[:], in1=lnb_p[:], op=ALU.add)
            setw("cb_all", t_cb)
            addr("c_all", t_cb)
            addr("lnb_p", t_cb)
            t_nc = p.op("vector", "tensor_scalar", waits=rw("c_all") + ww("negc_all"),
                        out=negc_all[:], in0=c_all[:], scalar1=-1.0, scalar2=None, op0=ALU.mult)
            setw("negc_all", t_nc)
            addr("c_all", t_nc)
            t_A = p.op("scalar", "activation", waits=rw("c_all") + ww("A_all"),
                       out=A_all[:], in_=c_all[:], func=AF.Exp)
            setw("A_all", t_A)
            addr("c_all", t_A)
            t_nbA = p.op("vector", "scalar_tensor_tensor", waits=[t_A] + rw("beta_p") + ww("negbA"),
                         out=negbA[:], in0=A_all[:], scalar=-1.0, in1=beta_p[:],
                         op0=ALU.mult, op1=ALU.mult)
            setw("negbA", t_nbA)
            addr("A_all", t_nbA)
            addr("beta_p", t_nbA)
            t_c127 = p.dma("gpsimd", c127[:], c_all[127:128, :], waits=rw("c_all") + ww("c127"))
            setw("c127", t_c127)
            addr("c_all", t_c127)
            t_A127 = p.op("scalar", "activation", waits=[t_c127] + ww("A127"), out=A127[:],
                          in_=c127[:], func=AF.Exp)
            setw("A127", t_A127)
            addr("c127", t_A127)
            bank_t, wts, bk = pool.acquire()
            tm = p.op("tensor", "matmul", waits=wts + [t_A127], out=bank_t[:, 0:32],
                      lhsT=ones1[:], rhs=A127[:], start=True, stop=True)
            pool.set_writer(bk, tm)
            addr("A127", tm)
            t_gam = evac(gam_all[:], bank_t[:, 0:32], [tm] + ww("gam_all"), bank=bk)
            setw("gam_all", t_gam)
            bank_t, wts, bk = pool.acquire()
            tm = p.op("tensor", "matmul", waits=wts + rw("c127"), out=bank_t[:, 0:32],
                      lhsT=ones1[:], rhs=c127[:], start=True, stop=True)
            pool.set_writer(bk, tm)
            addr("c127", tm)
            t_da = p.op("vector", "scalar_tensor_tensor", waits=[tm] + rw("c_all") + ww("darg"),
                        out=darg[:], in0=c_all[:], scalar=-1.0, in1=bank_t[:, 0:32],
                        op0=ALU.mult, op1=ALU.add)
            pool.add_reader(bk, t_da)
            setw("darg", t_da)
            addr("c_all", t_da)
            t_dec = p.op("scalar", "activation", waits=[t_da] + ww("dec_all"), out=dec_all[:],
                         in_=darg[:], func=AF.Exp)
            setw("dec_all", t_dec)
            addr("darg", t_dec)
            for idx, (nm, srcb, srcn) in enumerate((("c", c_all, "c_all"),
                                                    ("negc", negc_all, "negc_all"),
                                                    ("cb", cb_all, "cb_all"),
                                                    ("A", A_all, "A_all"))):
                bank_t, wts, bk = pool.acquire()
                ttr = p.op("tensor", "transpose", waits=wts + rw(srcn),
                           out=bank_t[0:32, 0:128], in_=srcb[:], identity=identf[:])
                pool.set_writer(bk, ttr)
                addr(srcn, ttr)
                te = evac(rowst[:], bank_t[0:32, 0:128], [ttr] + ww("rowst"), bank=bk)
                setw("rowst", te)
                t_st = p.dma("gpsimd", rows_dram[idx:idx + 1, :].rearrange("o (a b) -> (o a) b", a=32),
                             rowst[:], waits=[te] + ww("rows_dram" + nm))
                addr("rowst", t_st)
                setw("rows_dram" + nm, t_st)

            rows_view = rows_dram.rearrange("a (h c t) -> c a h t", h=4, c=8)
            for c in range(min(BCH, B_BCHL) if B_PH >= 7 else 0):
                ssl = c % 2
                t_stg = p.dma("gpsimd", stg[ssl][0:1, :].rearrange("o (a h t) -> o a h t", a=4, h=4),
                              rows_view[c][None],
                              waits=ww(f"stg{ssl}") + [t for nm2 in ("c", "negc", "cb", "A")
                                                      for t in rw("rows_dram" + nm2)])
                setw(f"stg{ssl}", t_stg)
                for nm2 in ("c", "negc", "cb", "A"):
                    addr("rows_dram" + nm2, t_stg)

                def stgs(arr_idx, head):
                    return stg[ssl][0:1, arr_idx * 512 + head * 128: arr_idx * 512 + head * 128 + 128]

                for pr in range(2):
                    colh = [(pr * 2 + hh) * 8 + c for hh in range(2)]
                    kpair, kname = kn[pr], f"kn{pr}"
                    qpair, qname = qn[pr], f"qn{pr}"
                    spr, sname = Sbf[pr], f"Sbf{pr}"
                    tsl = slice(c * 128, (c + 1) * 128)
                    hb_t, hb, mm_done = [], [], []
                    for hh in range(2):
                        rsl = slice(hh * 64, (hh + 1) * 64)
                        bt, wts, bk = pool.acquire()
                        mm1 = p.op("tensor", "matmul",
                                   waits=wts + rw(kname) + rw(qname) + rw(sname),
                                   out=bt[:, 0:128], lhsT=kpair[rsl, tsl], rhs=kpair[rsl, tsl],
                                   start=True, stop=True)
                        mm2 = p.op("tensor", "matmul", out=bt[:, 128:256],
                                   lhsT=kpair[rsl, tsl], rhs=qpair[rsl, tsl],
                                   start=True, stop=True)
                        mm3 = p.op("tensor", "matmul", out=bt[:, 256:384],
                                   lhsT=kpair[rsl, tsl], rhs=spr[rsl, :],
                                   start=True, stop=True)
                        pool.set_writer(bk, mm3)
                        addr(kname, mm3)
                        addr(qname, mm3)
                        addr(sname, mm3)
                        hb_t.append(bt)
                        hb.append(bk)
                        mm_done.append(mm3)
                    bkr_t, wts, bkr = pool.acquire()
                    r1 = p.op("tensor", "matmul", waits=wts + rw(f"stg{ssl}"),
                              out=bkr_t[:, 0:256], lhsT=identb[:], rhs=maskL2[:],
                              start=True, stop=False)
                    for hh in range(2):
                        r1 = p.op("tensor", "matmul", out=bkr_t[:, hh * 128:(hh + 1) * 128],
                                  lhsT=ones1[:], rhs=stgs(1, pr * 2 + hh),
                                  start=False, stop=(hh == 1))
                    r2 = p.op("tensor", "matmul", out=bkr_t[:, 256:512], lhsT=identb[:],
                              rhs=maskU2[:], start=True, stop=False)
                    for hh in range(2):
                        r2 = p.op("tensor", "matmul", out=bkr_t[:, 256 + hh * 128:256 + (hh + 1) * 128],
                                  lhsT=ones1[:], rhs=stgs(2, pr * 2 + hh),
                                  start=False, stop=(hh == 1))
                    pool.set_writer(bkr, r2)
                    addr(f"stg{ssl}", r2)
                    bk3_t, wts, bk3 = pool.acquire()
                    r3 = p.op("tensor", "matmul", waits=wts + rw(f"stg{ssl}"),
                              out=bk3_t[:, 0:256], lhsT=identb[:], rhs=maskUi2[:],
                              start=True, stop=False)
                    for hh in range(2):
                        r3 = p.op("tensor", "matmul", out=bk3_t[:, hh * 128:(hh + 1) * 128],
                                  lhsT=ones1[:], rhs=stgs(0, pr * 2 + hh),
                                  start=False, stop=(hh == 1))
                    ab1 = p.op("tensor", "matmul", out=bk3_t[:, 256:384], lhsT=sel_lo[:],
                               rhs=stgs(3, pr * 2), start=True, stop=False)
                    ab2 = p.op("tensor", "matmul", out=bk3_t[:, 256:384], lhsT=sel_hi[:],
                               rhs=stgs(3, pr * 2 + 1), start=False, stop=True)
                    pool.set_writer(bk3, ab2)
                    addr(f"stg{ssl}", ab2)
                    ew = rw("cb_all") + rw("negc_all")
                    e1 = p.op("scalar", "activation", waits=ew + [r2] + ww("expL"),
                              out=expL[:, 0:128], in_=bkr_t[:, 0:128], func=AF.Exp,
                              bias=cb_all[:, colh[0]:colh[0] + 1])
                    e1b = p.op("scalar", "activation", waits=[e1], out=expL[:, 128:256],
                               in_=bkr_t[:, 128:256], func=AF.Exp,
                               bias=cb_all[:, colh[1]:colh[1] + 1])
                    e2 = p.op("scalar", "activation", waits=[e1b] + ww("expU"),
                              out=expU[:, 0:128], in_=bkr_t[:, 256:384], func=AF.Exp,
                              bias=negc_all[:, colh[0]:colh[0] + 1])
                    e2b = p.op("scalar", "activation", waits=[e2], out=expU[:, 128:256],
                               in_=bkr_t[:, 384:512], func=AF.Exp,
                               bias=negc_all[:, colh[1]:colh[1] + 1])
                    e3 = p.op("scalar", "activation", waits=[ab2] + ww("expP"),
                              out=expP[:, 0:128], in_=bk3_t[:, 0:128], func=AF.Exp,
                              bias=negc_all[:, colh[0]:colh[0] + 1])
                    e3b = p.op("scalar", "activation", waits=[e3], out=expP[:, 128:256],
                               in_=bk3_t[:, 128:256], func=AF.Exp,
                               bias=negc_all[:, colh[1]:colh[1] + 1])
                    setw("expL", e1b)
                    setw("expU", e2b)
                    setw("expP", e3b)
                    addr("cb_all", e1b)
                    addr("negc_all", e3b)
                    pool.add_reader(bkr, e2b)
                    pool.add_reader(bk3, e3b)
                    tbv = None
                    for hh in range(2):
                        hsl = slice(hh * 128, (hh + 1) * 128)
                        tL = p.op("vector", "scalar_tensor_tensor",
                                  waits=[mm_done[hh], e1b] + (ww("Lb0") if hh == 0 else []),
                                  out=Lb[0][:, hsl], in0=hb_t[hh][:, 0:128], scalar=-1.0,
                                  in1=expL[:, hsl], op0=ALU.mult, op1=ALU.mult)
                        setw("Lb0", tL, keep=(hh == 1))
                        tU = p.op("vector", "scalar_tensor_tensor",
                                  waits=[e2b] + (ww("Ub0") if hh == 0 else []),
                                  out=Ub[0][:, hsl], in0=hb_t[hh][:, 0:128], scalar=-1.0,
                                  in1=expU[:, hsl], op0=ALU.mult, op1=ALU.mult)
                        setw("Ub0", tU, keep=(hh == 1))
                        tP = p.op("vector", "tensor_tensor",
                                  waits=[e3b] + (ww("PT") if hh == 0 else []),
                                  out=PT[:, hsl], in0=hb_t[hh][:, 128:256], in1=expP[:, hsl],
                                  op=ALU.mult)
                        setw("PT", tP, keep=(hh == 1))
                        vb = (c * 4 + pr * 2 + hh) * 128
                        tbv = p.op("vector", "tensor_scalar",
                                   waits=rw("v_tm") + rw("beta_p") + (ww("bV") if hh == 0 else []),
                                   out=bV[:, hsl], in0=v_tm[:, vb:vb + 128],
                                   scalar1=beta_p[:, colh[hh]:colh[hh] + 1], scalar2=None,
                                   op0=ALU.mult)
                        setw("bV", tbv, keep=(hh == 1))
                        td = p.op("vector", "scalar_tensor_tensor",
                                  waits=[tbv] + rw("negbA") + (ww("Db0") if hh == 0 else []),
                                  out=Db[0][:, hsl], in0=hb_t[hh][:, 256:384],
                                  scalar=negbA[:, colh[hh]:colh[hh] + 1], in1=bV[:, hsl],
                                  op0=ALU.mult, op1=ALU.add)
                        setw("Db0", td, keep=(hh == 1))
                        pool.add_reader(hb[hh], td)
                    for t_ in rw("Lb0"):
                        addr("expL", t_)
                    addr("v_tm", tbv)
                    addr("beta_p", tbv)
                    addr("negbA", td)
                    tok_U = rw("Ub0")
                    tok_L = rw("Lb0")
                    tok_P = rw("PT")
                    tok_D = rw("Db0")
                    for t_ in tok_U:
                        addr("expU", t_)
                    for t_ in tok_P:
                        addr("expP", t_)
                    if B_SUB < 3:
                        continue
                    cur, nxt = 0, 1
                    for j in range(7):
                        bkd_t, wts, bkd = pool.acquire()
                        md = p.op("tensor", "matmul", waits=wts + tok_D + tok_U,
                                  out=bkd_t[:, 0:256], lhsT=identb[:], rhs=Db[cur][:],
                                  start=True, stop=False)
                        md = p.op("tensor", "matmul", out=bkd_t[:, 0:128],
                                  lhsT=Ub[cur][:, 0:128], rhs=Db[cur][:, 0:128],
                                  start=False, stop=False)
                        md = p.op("tensor", "matmul", out=bkd_t[:, 128:256],
                                  lhsT=Ub[cur][:, 128:256], rhs=Db[cur][:, 128:256],
                                  start=False, stop=True)
                        pool.set_writer(bkd, md)
                        addr(f"Db{cur}", md)
                        addr(f"Ub{cur}", md)
                        te = evac(Db[nxt][:], bkd_t[:, 0:256], [md] + ww(f"Db{nxt}"), bank=bkd)
                        setw(f"Db{nxt}", te)
                        tok_D = [te]
                        if j < 6:
                            bkq_t, wts2, bkq = pool.acquire()
                            q1 = p.op("tensor", "matmul", waits=wts2 + tok_U + tok_L,
                                      out=bkq_t[:, 0:128], lhsT=Lb[cur][:, 0:128],
                                      rhs=Ub[cur][:, 0:128], start=True, stop=True)
                            q2 = p.op("tensor", "matmul", out=bkq_t[:, 128:256],
                                      lhsT=Lb[cur][:, 128:256], rhs=Ub[cur][:, 128:256],
                                      start=True, stop=True)
                            q3 = p.op("tensor", "matmul", out=bkq_t[:, 256:384],
                                      lhsT=Ub[cur][:, 0:128], rhs=Lb[cur][:, 0:128],
                                      start=True, stop=True)
                            q4 = p.op("tensor", "matmul", out=bkq_t[:, 384:512],
                                      lhsT=Ub[cur][:, 128:256], rhs=Lb[cur][:, 128:256],
                                      start=True, stop=True)
                            pool.set_writer(bkq, q4)
                            addr(f"Ub{cur}", q4)
                            addr(f"Lb{cur}", q4)
                            teU = evac(Ub[nxt][:], bkq_t[:, 0:256], [q4] + ww(f"Ub{nxt}"), bank=bkq, eng="scalar")
                            teL = evac(Lb[nxt][:], bkq_t[:, 256:512], [q4] + ww(f"Lb{nxt}"), bank=bkq, eng="scalar")
                            setw(f"Ub{nxt}", teU)
                            setw(f"Lb{nxt}", teL)
                            tok_U, tok_L = [teU], [teL]
                        cur, nxt = nxt, cur
                    if B_SUB < 4:
                        continue
                    bka_t, wts, bka = pool.acquire()
                    a1 = p.op("tensor", "matmul", waits=wts + rw(f"stg{ssl}"),
                              out=bka_t[:, 0:128], lhsT=sel_lo[:],
                              rhs=stgs(3, pr * 2), start=True, stop=False)
                    a2 = p.op("tensor", "matmul", out=bka_t[:, 0:128], lhsT=sel_hi[:],
                              rhs=stgs(3, pr * 2 + 1), start=False, stop=True)
                    pool.set_writer(bka, a2)
                    addr(f"stg{ssl}", a2)
                    tqs = p.op("vector", "tensor_tensor", waits=[a2] + rw(qname) + ww("Qs"),
                               out=Qs[:], in0=qpair[:, tsl], in1=bka_t[:, 0:128], op=ALU.mult)
                    pool.add_reader(bka, tqs)
                    setw("Qs", tqs)
                    addr(qname, tqs)
                    ob_t, ob_bk, ob_tok = [], [], []
                    for hh in range(2):
                        hsl = slice(hh * 128, (hh + 1) * 128)
                        rsl = slice(hh * 64, (hh + 1) * 64)
                        bt, wts, bk = pool.acquire()
                        oo = p.op("tensor", "matmul",
                                  waits=wts + tok_D + tok_P + [tqs] + rw(sname),
                                  out=bt[:, 0:128], lhsT=Db[cur][:, hsl], rhs=PT[:, hsl],
                                  start=True, stop=False)
                        oo = p.op("tensor", "matmul", out=bt[:, 0:128],
                                  lhsT=Sbf[pr][rsl, :], rhs=Qs[rsl, :], start=False, stop=True)
                        pool.set_writer(bk, oo)
                        addr("PT", oo)
                        addr("Qs", oo)
                        addr(sname, oo)
                        addr(f"Db{cur}", oo)
                        ob_t.append(bt)
                        ob_bk.append(bk)
                        ob_tok.append(oo)
                    oview = o_blk[:].rearrange("p (hh t) -> p hh t", hh=4)
                    for hh in range(2):
                        te_o = p.op("vector", "tensor_copy",
                                    waits=[ob_tok[hh]] + (ww("o_blk") if (c == 0 and pr == 0 and hh == 0) else []),
                                    out=oview[:, pr * 2 + hh, c * 128:(c + 1) * 128],
                                    in_=ob_t[hh][:, 0:128])
                        pool.add_reader(ob_bk[hh], te_o)
                        setw("o_blk", te_o, keep=not (c == 0 and pr == 0 and hh == 0))
                    if B_SUB < 5:
                        continue
                    kt0 = (c * 2 + pr) * 128
                    tk0 = p.op("vector", "tensor_scalar", waits=rw("k_tm") + rw("dec_all") + ww("Kp"),
                               out=Kp[:, 0:64], in0=k_tm[:, kt0:kt0 + 64],
                               scalar1=dec_all[:, colh[0]:colh[0] + 1], scalar2=None, op0=ALU.mult)
                    tk1 = p.op("vector", "tensor_scalar", waits=[tk0],
                               out=Kp[:, 64:128], in0=k_tm[:, kt0 + 64:kt0 + 128],
                               scalar1=dec_all[:, colh[1]:colh[1] + 1], scalar2=None, op0=ALU.mult)
                    setw("Kp", tk1)
                    addr("k_tm", tk1)
                    addr("dec_all", tk1)
                    bkst_t, wts, bkst = pool.acquire()
                    sm0 = p.op("tensor", "matmul", waits=wts + [tk1] + tok_D,
                               out=bkst_t[0:64, 0:128], lhsT=Kp[:, 0:64], rhs=Db[cur][:, 0:128],
                               start=True, stop=True)
                    sm1 = p.op("tensor", "matmul", out=bkst_t[64:128, 0:128],
                               lhsT=Kp[:, 64:128], rhs=Db[cur][:, 128:256],
                               start=True, stop=True, tile_position=(0, 64))
                    pool.set_writer(bkst, sm1)
                    addr("Kp", sm1)
                    addr(f"Db{cur}", sm1)
                    su0 = p.op("vector", "scalar_tensor_tensor",
                               waits=[sm1] + ww(f"Sf{pr}") + rw("gam_all"),
                               out=Sf[pr][0:64, :], in0=Sf[pr][0:64, :],
                               scalar=gam_all[0:64, colh[0]:colh[0] + 1], in1=bkst_t[0:64, 0:128],
                               op0=ALU.mult, op1=ALU.add)
                    su1 = p.op("vector", "scalar_tensor_tensor", waits=[su0],
                               out=Sf[pr][64:128, :], in0=Sf[pr][64:128, :],
                               scalar=gam_all[64:128, colh[1]:colh[1] + 1], in1=bkst_t[64:128, 0:128],
                               op0=ALU.mult, op1=ALU.add)
                    setw(f"Sf{pr}", su1)
                    addr("gam_all", su1)
                    pool.add_reader(bkst, su1)
                    tsb = p.op("vector", "tensor_copy", waits=[su1] + ww(f"Sbf{pr}"),
                               out=Sbf[pr][:], in_=Sf[pr][:])
                    setw(f"Sbf{pr}", tsb)
                    addr(f"Sf{pr}", tsb)

            for h in range(4 if B_PH >= 8 else 0):
                ho = h * 1024
                t_sq = p.op("scalar", "activation", waits=rw("o_blk") + ww("sqb"),
                            out=sqb[:], in_=o_blk[:, ho:ho + 1024], func=AF.Square)
                setw("sqb", t_sq)
                addr("o_blk", t_sq)
                t_sil = p.op("scalar", "activation", waits=rw(f"ggraw{h}") + ww("silb"),
                             out=silb[:], in_=ggraw[h][:, 0:1024], func=AF.Silu)
                setw("silb", t_sil)
                addr(f"ggraw{h}", t_sil)
                for half in range(2):
                    hsl = slice(half * 512, (half + 1) * 512)
                    bank_t, wts, bk = pool.acquire()
                    n1 = p.op("tensor", "matmul", waits=wts + [t_sq], out=bank_t[0:1, 0:512],
                              lhsT=ones_c[:], rhs=sqb[:, hsl],
                              start=True, stop=True)
                    pool.set_writer(bk, n1)
                    t_ss = p.op("vector", "tensor_scalar", waits=[n1] + ww("ssn"),
                                out=ssn[:], in0=bank_t[0:1, 0:512],
                                scalar1=1.0 / 128, scalar2=1e-5, op0=ALU.mult, op1=ALU.add)
                    pool.add_reader(bk, t_ss)
                    setw("ssn", t_ss)
                    t_ln = p.op("scalar", "activation", waits=[t_ss] + ww("lnn"),
                                out=lnn[:], in_=ssn[:], func=AF.Ln)
                    setw("lnn", t_ln)
                    addr("ssn", t_ln)
                    t_r = p.op("scalar", "activation", waits=[t_ln] + ww("rsn"),
                               out=rsn[:], in_=lnn[:], func=AF.Exp, scale=csts[0:1, 0:1])
                    setw("rsn", t_r)
                    addr("lnn", t_r)
                    bank_t2, wts2, bk2 = pool.acquire()
                    bb = p.op("tensor", "matmul", waits=wts2 + [t_r], out=bank_t2[:],
                              lhsT=ones1[:], rhs=rsn[0:1, :], start=True, stop=True)
                    pool.set_writer(bk2, bb)
                    addr("rsn", bb)
                    t_y1 = p.op("vector", "scalar_tensor_tensor",
                                waits=[bb] + rw("o_blk") + ww("ytmp"),
                                out=ytmp[:], in0=o_blk[:, ho + half * 512:ho + (half + 1) * 512],
                                scalar=normw_s[:], in1=bank_t2[:], op0=ALU.mult, op1=ALU.mult)
                    pool.add_reader(bk2, t_y1)
                    setw("ytmp", t_y1)
                    addr("o_blk", t_y1)
                    t_y2 = p.op("vector", "tensor_tensor", waits=[t_y1, t_sil],
                                out=ytmp[:], in0=ytmp[:], in1=silb[:, hsl], op=ALU.mult)
                    setw("ytmp", t_y2)
                    addr("silb", t_y2)
                    t_st = p.dma("gpsimd",
                                 yout[h * 128:(h + 1) * 128, t0 + half * 512:t0 + (half + 1) * 512],
                                 ytmp[:], waits=[t_y2])
                    addr("ytmp", t_st)

            if bi < NBLK - 1:
                for i in range(4):
                    t_h = p.op("vector", "tensor_copy", waits=ww(f"qkraw{i}"),
                               out=qkraw[i][:, 0:3], in_=qkraw[i][:, BLOCK:BLOCK + 3])
                    setw(f"qkraw{i}", t_h)
                    t_h2 = p.op("vector", "tensor_copy", waits=ww(f"vraw{i}"),
                                out=vraw[i][:, 0:3], in_=vraw[i][:, BLOCK:BLOCK + 3])
                    setw(f"vraw{i}", t_h2)

        if B_DUMP:
            dumps_b = [("qn0", qn[0]), ("qn1", qn[1]), ("kn0", kn[0]), ("kn1", kn[1]),
                       ("v_tm", v_tm), ("k_tm", k_tm), ("o_blk", o_blk),
                       ("qkraw0", qkraw[0]), ("ggraw0", ggraw[0])]
            off = 0
            for nm, buf in dumps_b:
                w = buf.shape[1]
                p.dma("gpsimd", dbgb[:, off:off + w], buf[:, 0:w], waits=ww(nm) + rw(nm))
                off += w
            dumps_f = [("ab_tm", ab_tm, 64), ("g_part", g_part, 32), ("beta_p", beta_p, 32),
                       ("c_all", c_all, 32), ("A_all", A_all, 32), ("dec_all", dec_all, 32),
                       ("gam_all", gam_all, 32), ("negbA", negbA, 32)]
            off = 0
            for nm, buf, w in dumps_f:
                p.dma("gpsimd", dbgf[:, off:off + w], buf[:, 0:w], waits=ww(nm) + rw(nm))
                off += w
        p.waits("gpsimd", p.barrier_tokens())
        p.finalize()
    return nc


_launches = {}
_last_res = None


def _get(name, builder):
    if name not in _launches:
        _launches[name] = builder()
    return _launches[name]


def run_A(x0):
    nc = _get("A", build_launch_A)
    in_maps = []
    for c in range(NCORE):
        b, q = divmod(c, 4)
        in_maps.append({"x": f32(x0[b, q * Q:(q + 1) * Q])})
    res = run_bass_kernel_spmd(nc, in_maps, core_ids=list(range(NCORE)))
    xn = np.empty((B, T, D), np.float32)
    xnb = np.empty((B, T, D), BF)
    for c in range(NCORE):
        b, q = divmod(c, 4)
        xn[b, q * Q:(q + 1) * Q] = res.results[c]["xn"]
        xnb[b, q * Q:(q + 1) * Q] = res.results[c]["xnb"]
    return xn, xnb




def _consts_B():
    eye = np.eye(128, dtype=np.float32)
    r = np.arange(128)
    mL = np.where(r[None, :] < r[:, None], 0.0, NEG).astype(np.float32)
    mU = np.where(r[:, None] < r[None, :], 0.0, NEG).astype(np.float32)
    mUi = np.where(r[:, None] <= r[None, :], 0.0, NEG).astype(np.float32)
    csts = np.zeros((128, 4), np.float32)
    csts[:, 0] = -0.5
    csts[:, 1] = np.log(0.125)
    return {
        "identb": cbf(eye), "identf": f32(eye),
        "triu": f32(np.where(r[:, None] <= r[None, :], 1.0, 0.0)),
        "maskL2": cbf(np.concatenate([mL, mL], 1)),
        "maskU2": cbf(np.concatenate([mU, mU], 1)),
        "maskUi2": cbf(np.concatenate([mUi, mUi], 1)),
        "ones1": f32(np.ones((1, 128))),
        "sel_lo": f32((r[None, :] < 64).astype(np.float32)),
        "sel_hi": f32((r[None, :] >= 64).astype(np.float32)),
        "blk2": cbf(np.stack([(r < 64), (r >= 64)], 1).astype(np.float32)),
        "blk2T": f32(np.stack([(r < 64), (r >= 64)], 0).astype(np.float32)[:2]),
        "ones_c": cbf(np.ones((128, 1))),
        "csts": csts,
    }


def run_B(xn0b, params):
    nc = _get("B", build_launch_B)
    consts = _consts_B()
    xnb_fm = [np.ascontiguousarray(xn0b[b].T) for b in range(B)]
    in_maps = []
    for c in range(NCORE):
        b = c // 4
        hg = (c % 4) * 4
        wq, wk = params["L0_attn_w_q"], params["L0_attn_w_k"]
        im = {
            "xnb": xnb_fm[b],
            "wqk": f32(np.concatenate([wq[:, hg * 64:(hg + 4) * 64],
                                       wk[:, hg * 64:(hg + 4) * 64]], 1)),
            "wv4": f32(params["L0_attn_w_v"][:, hg * 128:(hg + 4) * 128]),
            "wg4": f32(params["L0_attn_w_g"][:, hg * 128:(hg + 4) * 128]),
            "wab": f32(np.concatenate([params["L0_attn_w_a"][:, hg:hg + 4],
                                       params["L0_attn_w_b"][:, hg:hg + 4]], 1)),
            "convqk": f32(np.concatenate([params["L0_attn_conv_q"][hg * 64:(hg + 4) * 64],
                                          params["L0_attn_conv_k"][hg * 64:(hg + 4) * 64]], 0)),
            "convv": f32(params["L0_attn_conv_v"][hg * 128:(hg + 4) * 128]),
            "alogb": f32(np.broadcast_to(params["L0_attn_A_log"][hg:hg + 4], (128, 4))),
            "dtbb": f32(np.broadcast_to(params["L0_attn_dt_bias"][hg:hg + 4], (128, 4))),
            "normw": f32(params["L0_attn_norm_w"][:, None]),
        }
        im.update(consts)
        in_maps.append(im)
    res = run_bass_kernel_spmd(nc, in_maps, core_ids=list(range(NCORE)))
    global _last_res
    _last_res = res
    y = np.empty((B, T, H * DV), BF)
    for c in range(NCORE):
        b = c // 4
        hg = (c % 4) * 4
        y[b, :, hg * 128:(hg + 4) * 128] = res.results[c]["y"].T
    return y





def build_launch_C():
    nc = new_nc()
    ap = lambda n, s, d, k: nc.dram_tensor(n, s, d, kind=k).ap()
    x0q = ap("x0q", [Q, D], F32, "ExternalInput")
    xn0q = ap("xn0q", [Q, D], F32, "ExternalInput")
    xnfm = ap("xnfm", [D, Q], BF16, "ExternalInput")
    yfm = ap("yfm", [2048, Q], BF16, "ExternalInput")
    wout = ap("wout", [2048, D], F32, "ExternalInput")
    hwT = ap("hwT", [D, 32], F32, "ExternalInput")
    keysT = ap("keysT", [32, 256], F32, "ExternalInput")
    vdfm = ap("vdfm", [D, 8192], F32, "ExternalInput")
    vuall = ap("vuall", [8192, D], F32, "ExternalInput")
    scaleb = ap("scaleb", [128, 1], F32, "ExternalInput")
    normw1 = ap("normw1", [1, D], F32, "ExternalInput")
    identb_i = ap("identb", [128, 128], BF16, "ExternalInput")
    ones1_i = ap("ones1", [1, 128], F32, "ExternalInput")
    csts_i = ap("csts", [128, 4], F32, "ExternalInput")
    selg_i = ap("selg", [128, 32 * 128], BF16, "ExternalInput")
    x1qo = ap("x1q", [Q, D], F32, "ExternalOutput")
    xn1qo = ap("xn1q", [Q, D], F32, "ExternalOutput")
    xn1qbo = ap("xn1qb", [Q, D], BF16, "ExternalOutput")
    z1qbo = ap("z1qb", [Q, D], BF16, "ExternalOutput")
    vd_bf = nc.dram_tensor("vd_bf", [D, 8192], BF16).ap()
    vu_bf = nc.dram_tensor("vu_bf", [8192, D], BF16).ap()

    N5 = Q // 512
    with ExitStack() as ctx:
        E = ctx.enter_context
        p = Prog(nc, ctx)
        pool = PsumPool(nc, ctx)
        sb = lambda n, s, d: E(nc.sbuf_tensor(n, s, d))
        identb = sb("identb_s", [128, 128], BF16)
        ones1 = sb("ones1_s", [1, 128], F32)
        csts = sb("csts_s", [128, 4], F32)
        scale_c = sb("scale_c", [128, 1], F32)
        selg = sb("selg_s", [128, 32 * 128], BF16)
        woutb = sb("woutb", [128, 16 * 1024], BF16)
        hwTb = sb("hwTb", [128, 8 * 32], BF16)
        keysb = sb("keysb", [32, 256], BF16)
        nw_bc = sb("nw_bc", [128, D], F32)
        wst = [sb(f"wst{i}", [128, 512], F32) for i in range(2)]
        cst = [sb(f"cst{i}", [128, 512], BF16) for i in range(2)]
        yqt = [sb(f"yqt{i}", [128, 2048], BF16) for i in range(2)]
        xnf = sb("xnf", [128, 8 * Q], BF16)
        vd_s = [sb(f"vd{i}", [128, 1024], BF16) for i in range(2)]
        vu_s = sb("vu_s", [128, 8 * 1024], BF16)
        hfm = sb("hfm", [32, Q], BF16)
        sim = sb("sim", [128, 256], F32)
        simw = sb("simw", [128, 256], F32)
        mc = sb("mc", [128, 4], F32)
        dm4 = sb("dm4", [128, 4], F32)
        dm4t = sb("dm4t", [128, 4], F32)
        ge4 = sb("ge4", [128, 256], F32)
        den = sb("den", [128, 1], F32)
        comb = sb("comb", [128, 256], BF16)
        cfb = sb("cfb", [128, 4 * 256], BF16)
        projw = sb("projw", [128, 8 * 512], BF16)
        silp = sb("silp", [128, 512], BF16)
        acc = sb("acc", [128, 4096], F32)
        x0t = sb("x0t", [128, 1024], F32)
        xnt = sb("xnt", [128, 1024], F32)
        x1t = sb("x1t", [128, 1024], F32)
        xn1t = sb("xn1t", [128, 1024], F32)
        xn1bt = sb("xn1bt", [128, 1024], BF16)
        z1bt = sb("z1bt", [128, 1024], BF16)
        ssq = sb("ssq", [128, 1], F32)
        ssum = sb("ssum", [128, 1], F32)
        mu = sb("mu", [128, 1], F32)
        var = sb("var", [128, 1], F32)
        sstd = sb("sstd", [128, 1], F32)
        rstd = sb("rstd", [128, 1], F32)
        trb = sb("trb", [128, 1024], BF16)

        st_w, st_r = {}, {}

        def _merge(dct, tok):
            sem, val = tok
            key = id(sem)
            if key not in dct or dct[key][1] < val:
                dct[key] = (sem, val)

        def setw(name, tok, keep=False):
            if not keep or name not in st_w:
                st_w[name] = {}
            if not keep:
                st_r[name] = {}
            _merge(st_w[name], tok)

        def addr(name, tok):
            _merge(st_r.setdefault(name, {}), tok)

        def rw(name):
            return list(st_w.get(name, {}).values())

        def ww(name):
            return list(st_r.get(name, {}).values()) + rw(name)

        tc = []
        for apx, buf in ((identb_i, identb), (ones1_i, ones1), (csts_i, csts),
                         (scaleb, scale_c), (selg_i, selg)):
            tc.append(p.dma("sync", buf[:], apx[:, :]))
        t_k = p.dma("sync", wst[0][0:32, 0:256], keysT[:, :])
        setw("wst0", t_k)
        t_kc = p.op("vector", "tensor_copy", waits=[t_k], out=keysb[:], in_=wst[0][0:32, 0:256])
        addr("wst0", t_kc)
        tc.append(t_kc)
        for f in range(16):
            for hf in range(2):
                sl = (f * 2 + hf) % 2
                t_l = p.dma("sync", wst[sl][:], wout[f * 128:(f + 1) * 128, hf * 512:(hf + 1) * 512],
                            waits=ww(f"wst{sl}"))
                setw(f"wst{sl}", t_l)
                t_c = p.op("vector", "tensor_copy", waits=[t_l],
                           out=woutb[:, f * 1024 + hf * 512:f * 1024 + hf * 512 + 512],
                           in_=wst[sl][:])
                addr(f"wst{sl}", t_c)
                tc.append(t_c)
        for f in range(8):
            sl = f % 2
            t_l = p.dma("sync", wst[sl][:, 0:32], hwT[f * 128:(f + 1) * 128, :], waits=ww(f"wst{sl}"))
            setw(f"wst{sl}", t_l)
            t_c = p.op("vector", "tensor_copy", waits=[t_l], out=hwTb[:, f * 32:(f + 1) * 32],
                       in_=wst[sl][:, 0:32])
            addr(f"wst{sl}", t_c)
            tc.append(t_c)
        t_x = p.dma("sync", xnf[:].rearrange("p (f t) -> p f t", f=8),
                    xnfm.rearrange("(f p) t -> p f t", p=128))
        tc += [t_x]
        setw("xnf", t_x)
        for half in range(2):
            t_nw = p.dma("sync", wst[0][0:1, :], normw1[:, half * 512:(half + 1) * 512],
                         waits=ww("wst0"))
            setw("wst0", t_nw)
            bank_t, wts, bk = pool.acquire()
            tm = p.op("tensor", "matmul", waits=wts + [t_nw],
                      out=bank_t[:], lhsT=ones1[:], rhs=wst[0][0:1, 0:512],
                      start=True, stop=True)
            pool.set_writer(bk, tm)
            te = p.op("vector", "tensor_copy", waits=[tm], out=nw_bc[:, half * 512:(half + 1) * 512],
                      in_=bank_t[:])
            pool.add_reader(bk, te)
            addr("wst0", te)
            setw("nw_bc", te, keep=(half == 1))
        for (src_ap, dst_ap, nrow, ncol, nm) in ((vdfm, vd_bf, D, 8192, "vdc"),
                                                 (vuall, vu_bf, 8192, D, "vuc")):
            for f in range(nrow // 128):
                for cc in range(ncol // 512):
                    sl = (f * (ncol // 512) + cc) % 2
                    t_l = p.dma("sync", wst[sl][:],
                                src_ap[f * 128:(f + 1) * 128, cc * 512:(cc + 1) * 512],
                                waits=ww(f"wst{sl}"))
                    setw(f"wst{sl}", t_l)
                    t_c = p.op("vector", "tensor_copy", waits=[t_l] + ww(f"cst{sl}"),
                               out=cst[sl][:], in_=wst[sl][:])
                    setw(f"cst{sl}", t_c)
                    addr(f"wst{sl}", t_c)
                    t_s = p.dma("gpsimd", dst_ap[f * 128:(f + 1) * 128, cc * 512:(cc + 1) * 512],
                                cst[sl][:], waits=[t_c])
                    addr(f"cst{sl}", t_s)
                    setw(nm, t_s, keep=not (f == 0 and cc == 0))
        w_init = list(tc) + rw("nw_bc")

        for tt in range(N5):
            bank_t, wts, bk = pool.acquire()
            tm = None
            for f in range(8):
                tm = p.op("tensor", "matmul",
                          waits=(wts + w_init if (tt == 0 and f == 0) else (wts if f == 0 else [])),
                          out=bank_t[0:32, :], lhsT=hwTb[:, f * 32:(f + 1) * 32],
                          rhs=xnf[:, f * Q + tt * 512: f * Q + tt * 512 + 512],
                          start=(f == 0), stop=(f == 7))
            pool.set_writer(bk, tm)
            addr("xnf", tm)
            te = p.op("scalar", "copy", waits=[tm] + ww("hfm"), out=hfm[0:32, tt * 512:(tt + 1) * 512],
                      in_=bank_t[0:32, :])
            pool.add_reader(bk, te)
            setw("hfm", te, keep=(tt > 0))

        for t5 in range(N5):
            q0 = t5 * 512
            for sub in range(4):
                t0 = q0 + sub * 128
                ysl = (t5 * 4 + sub) % 2
                t_yld = p.dma("sync", yqt[ysl][:].rearrange("p (f t) -> p f t", f=16),
                              yfm.rearrange("(f p) t -> p f t", p=128)[:, :, t0:t0 + 128],
                              waits=ww(f"yqt{ysl}"))
                setw(f"yqt{ysl}", t_yld)
                for half in range(2):
                    bank_t, wts, bk = pool.acquire()
                    tm = None
                    for f in range(16):
                        tm = p.op("tensor", "matmul", waits=(wts + [t_yld] if f == 0 else []),
                                  out=bank_t[:], lhsT=yqt[ysl][:, f * 128:(f + 1) * 128],
                                  rhs=woutb[:, f * 1024 + half * 512: f * 1024 + half * 512 + 512],
                                  start=(f == 0), stop=(f == 15))
                    pool.set_writer(bk, tm)
                    addr(f"yqt{ysl}", tm)
                    te = p.op("vector", "tensor_copy",
                              waits=[tm] + (ww("acc") if (sub == 0 and half == 0) else []),
                              out=acc[:, sub * 1024 + half * 512: sub * 1024 + half * 512 + 512],
                              in_=bank_t[:])
                    pool.add_reader(bk, te)
                    setw("acc", te, keep=not (sub == 0 and half == 0))
            for sub in range(4):
                t0 = q0 + sub * 128
                bank_t, wts, bk = pool.acquire()
                tm = p.op("tensor", "matmul", waits=wts + rw("hfm") + [t_kc], out=bank_t[:, 0:256],
                          lhsT=hfm[0:32, t0:t0 + 128], rhs=keysb[:], start=True, stop=True)
                pool.set_writer(bk, tm)
                addr("hfm", tm)
                t_s = p.op("vector", "tensor_copy", waits=[tm] + ww("sim"), out=sim[:],
                           in_=bank_t[:, 0:256])
                pool.add_reader(bk, t_s)
                setw("sim", t_s)
                t_wk = p.op("vector", "tensor_copy", waits=[t_s] + ww("simw"), out=simw[:], in_=sim[:])
                setw("simw", t_wk)
                tprev = t_wk
                for r4 in range(4):
                    t_m = p.op("vector", "reduce_max", waits=[tprev] + (ww("mc") if r4 == 0 else []),
                               out=mc[:, r4:r4 + 1], in_=simw[:], axis=mybir.AxisListType.X)
                    setw("mc", t_m, keep=(r4 > 0))
                    if r4 < 3:
                        t_g = p.op("vector", "tensor_scalar", waits=[t_m] + ww("ge4"), out=ge4[:],
                                   in0=simw[:], scalar1=mc[:, r4:r4 + 1], scalar2=None, op0=ALU.is_ge)
                        setw("ge4", t_g)
                        tprev = p.op("vector", "scalar_tensor_tensor", waits=[t_g], out=simw[:],
                                     in0=ge4[:], scalar=NEG, in1=simw[:], op0=ALU.mult, op1=ALU.add)
                        setw("simw", tprev, keep=True)
                        addr("ge4", tprev)
                t_dm = p.op("vector", "tensor_scalar", waits=rw("mc") + ww("dm4"), out=dm4[:],
                            in0=mc[:], scalar1=mc[:, 0:1], scalar2=None, op0=ALU.subtract)
                setw("dm4", t_dm)
                addr("mc", t_dm)
                t_de = p.op("scalar", "activation", waits=[t_dm], out=dm4[:], in_=dm4[:], func=AF.Exp)
                setw("dm4", t_de)
                t_ds = p.op("vector", "tensor_scalar", waits=[t_de] + ww("den") + ww("dm4t"),
                            out=dm4t[:], in0=dm4[:], scalar1=1.0, scalar2=None,
                            op0=ALU.mult, op1=ALU.add, accum_out=den[:])
                setw("den", t_ds)
                setw("dm4t", t_ds)
                addr("dm4", t_ds)
                t_di = p.op("vector", "reciprocal", waits=[t_ds], out=den[:], in_=den[:])
                setw("den", t_di)
                t_sel = p.op("vector", "tensor_scalar", waits=rw("sim") + rw("mc") + ww("ge4"),
                             out=ge4[:], in0=sim[:], scalar1=mc[:, 3:4], scalar2=None, op0=ALU.is_ge)
                setw("ge4", t_sel)
                addr("sim", t_sel)
                addr("mc", t_sel)
                t_ea = p.op("vector", "tensor_scalar", waits=rw("sim") + rw("mc") + rw("simw"),
                            out=simw[:], in0=sim[:], scalar1=mc[:, 0:1], scalar2=None,
                            op0=ALU.subtract)
                setw("simw", t_ea)
                addr("sim", t_ea)
                t_ex = p.op("scalar", "activation", waits=[t_ea], out=simw[:], in_=simw[:], func=AF.Exp)
                setw("simw", t_ex)
                t_cb = p.op("vector", "scalar_tensor_tensor", waits=[t_ex, t_di, t_sel] + ww("comb"),
                            out=comb[:], in0=simw[:], scalar=den[:], in1=ge4[:],
                            op0=ALU.mult, op1=ALU.mult)
                setw("comb", t_cb)
                addr("simw", t_cb)
                addr("den", t_cb)
                addr("ge4", t_cb)
                for half in range(2):
                    bank_t, wts, bk = pool.acquire(BF16)
                    ttp = p.op("tensor", "transpose", waits=wts + [t_cb],
                               out=bank_t[:, 0:128], in_=comb[:, half * 128:(half + 1) * 128],
                               identity=identb[:])
                    pool.set_writer(bk, ttp)
                    addr("comb", ttp)
                    te = p.op("vector", "tensor_copy",
                              waits=[ttp] + (ww("cfb") if (sub == 0 and half == 0) else []),
                              out=cfb[:, sub * 256 + half * 128: sub * 256 + half * 128 + 128],
                              in_=bank_t[:, 0:128])
                    pool.add_reader(bk, te)
                    setw("cfb", te, keep=not (sub == 0 and half == 0))
            for gb in range(8):
                for gi in range(8):
                    g = gb * 8 + gi
                    sl = g % 2
                    t_vd = p.dma("sync", vd_s[sl][:].rearrange("p (f c) -> p f c", f=8),
                                 vd_bf.rearrange("(f p) c -> p f c", p=128)[:, :, g * 128:(g + 1) * 128],
                                 waits=ww(f"vd{sl}") + (rw("vdc") if t5 == 0 and gb == 0 and gi < 2 else []))
                    setw(f"vd{sl}", t_vd)
                    t_vu = p.dma("sync", vu_s[:, gi * 1024:(gi + 1) * 1024],
                                 vu_bf[g * 128:(g + 1) * 128, :],
                                 waits=ww(f"vuq{gi}") + (rw("vuc") if t5 == 0 and gb == 0 else []))
                    setw(f"vuq{gi}", t_vu)
                    bank_t, wts, bk = pool.acquire()
                    tm = None
                    for f in range(8):
                        tm = p.op("tensor", "matmul", waits=(wts + [t_vd] if f == 0 else []),
                                  out=bank_t[:], lhsT=vd_s[sl][:, f * 128:(f + 1) * 128],
                                  rhs=xnf[:, f * Q + q0: f * Q + q0 + 512],
                                  start=(f == 0), stop=(f == 7))
                    pool.set_writer(bk, tm)
                    addr(f"vd{sl}", tm)
                    t_si = p.op("scalar", "activation", waits=[tm] + ww("silp"),
                                out=silp[:], in_=bank_t[:], func=AF.Silu)
                    pool.add_reader(bk, t_si)
                    setw("silp", t_si)
                    bank_t2, wts2, bk2 = pool.acquire()
                    tw = None
                    for sub in range(4):
                        tw = p.op("tensor", "matmul",
                                  waits=(wts2 + rw("cfb") if sub == 0 else []),
                                  out=bank_t2[:, sub * 128:(sub + 1) * 128],
                                  lhsT=selg[:, (g % 32) * 128:(g % 32) * 128 + 128],
                                  rhs=cfb[:, sub * 256 + (g // 32) * 128: sub * 256 + (g // 32) * 128 + 128],
                                  start=True, stop=True)
                    pool.set_writer(bk2, tw)
                    addr("cfb", tw)
                    t_pw = p.op("vector", "tensor_tensor", waits=[tw, t_si] + ww(f"pw{gi}"),
                                out=projw[:, gi * 512:(gi + 1) * 512], in0=silp[:],
                                in1=bank_t2[:], op=ALU.mult)
                    pool.add_reader(bk2, t_pw)
                    setw(f"pw{gi}", t_pw)
                    addr("silp", t_pw)
                for sub in range(4):
                    for half in range(2):
                        bank_t, wts, bk = pool.acquire()
                        tm = None
                        for gi in range(8):
                            tm = p.op("tensor", "matmul",
                                      waits=(wts + rw(f"pw{gi}") + rw(f"vuq{gi}") if gi == 0 else
                                             rw(f"pw{gi}") + rw(f"vuq{gi}")),
                                      out=bank_t[:],
                                      lhsT=projw[:, gi * 512 + sub * 128: gi * 512 + sub * 128 + 128],
                                      rhs=vu_s[:, gi * 1024 + half * 512: gi * 1024 + half * 512 + 512],
                                      start=(gi == 0), stop=(gi == 7))
                        pool.set_writer(bk, tm)
                        for gi in range(8):
                            addr(f"pw{gi}", tm)
                            addr(f"vuq{gi}", tm)
                        t_ac = p.op("vector", "scalar_tensor_tensor", waits=[tm] + ww("acc"),
                                    out=acc[:, sub * 1024 + half * 512: sub * 1024 + half * 512 + 512],
                                    in0=bank_t[:], scalar=scale_c[:],
                                    in1=acc[:, sub * 1024 + half * 512: sub * 1024 + half * 512 + 512],
                                    op0=ALU.mult, op1=ALU.add)
                        pool.add_reader(bk, t_ac)
                        setw("acc", t_ac, keep=True)
            for sub in range(4):
                t0 = q0 + sub * 128
                asl = slice(sub * 1024, (sub + 1) * 1024)
                t_l0 = p.dma("sync", x0t[:], x0q[t0:t0 + 128, :], waits=ww("x0t"))
                setw("x0t", t_l0)
                t_ln = p.dma("sync", xnt[:], xn0q[t0:t0 + 128, :], waits=ww("xnt"))
                setw("xnt", t_ln)
                t_a1 = p.op("vector", "tensor_tensor", waits=[t_l0] + rw("acc") + ww("x1t"),
                            out=x1t[:], in0=acc[:, asl], in1=x0t[:], op=ALU.add)
                setw("x1t", t_a1)
                addr("acc", t_a1)
                addr("x0t", t_a1)
                t_a2 = p.op("vector", "tensor_tensor", waits=[t_ln, t_a1],
                            out=x1t[:], in0=x1t[:], in1=xnt[:], op=ALU.add)
                setw("x1t", t_a2)
                addr("xnt", t_a2)
                t_s1 = p.dma("gpsimd", x1qo[t0:t0 + 128, :], x1t[:], waits=[t_a2])
                addr("x1t", t_s1)
                t_sq = p.op("scalar", "activation", waits=[t_a2] + ww("trb"), out=trb[:],
                            in_=x1t[:], func=AF.Square, accum_out=ssq[:])
                setw("trb", t_sq)
                setw("ssq", t_sq)
                t_sm = p.op("vector", "tensor_scalar", waits=[t_a2] + ww("z1bt") + ww("ssum"),
                            out=z1bt[:], in0=x1t[:], scalar1=1.0, scalar2=None,
                            op0=ALU.mult, op1=ALU.add, accum_out=ssum[:])
                setw("ssum", t_sm)
                setw("z1bt", t_sm)
                t_mu = p.op("vector", "tensor_scalar", waits=[t_sm] + ww("mu"), out=mu[:],
                            in0=ssum[:], scalar1=1.0 / D, scalar2=None, op0=ALU.mult)
                setw("mu", t_mu)
                addr("ssum", t_mu)
                t_v1 = p.op("vector", "scalar_tensor_tensor", waits=[t_mu] + ww("var"), out=var[:],
                            in0=mu[:], scalar=-1.0, in1=mu[:], op0=ALU.mult, op1=ALU.mult)
                setw("var", t_v1)
                t_v2 = p.op("vector", "scalar_tensor_tensor", waits=[t_v1, t_sq], out=var[:],
                            in0=ssq[:], scalar=1.0 / D, in1=var[:], op0=ALU.mult, op1=ALU.add)
                setw("var", t_v2)
                addr("ssq", t_v2)
                t_v3 = p.op("vector", "tensor_scalar", waits=[t_v2], out=var[:], in0=var[:],
                            scalar1=1e-5, scalar2=None, op0=ALU.add)
                setw("var", t_v3)
                t_sd = p.op("scalar", "activation", waits=[t_v3] + ww("sstd"), out=sstd[:],
                            in_=var[:], func=AF.Sqrt)
                setw("sstd", t_sd)
                addr("var", t_sd)
                t_rs = p.op("vector", "reciprocal", waits=[t_sd] + ww("rstd"), out=rstd[:], in_=sstd[:])
                setw("rstd", t_rs)
                addr("sstd", t_rs)
                t_xn1 = p.op("vector", "tensor_scalar", waits=[t_rs] + rw("x1t") + ww("xn1t"),
                             out=xn1t[:], in0=x1t[:], scalar1=mu[:], scalar2=rstd[:],
                             op0=ALU.subtract, op1=ALU.mult)
                setw("xn1t", t_xn1)
                addr("x1t", t_xn1)
                addr("mu", t_xn1)
                addr("rstd", t_xn1)
                t_cb1 = p.op("vector", "tensor_copy", waits=[t_xn1] + ww("xn1bt"), out=xn1bt[:],
                             in_=xn1t[:])
                setw("xn1bt", t_cb1)
                t_s2 = p.dma("gpsimd", xn1qo[t0:t0 + 128, :], xn1t[:], waits=[t_xn1])
                addr("xn1t", t_s2)
                t_s3 = p.dma("gpsimd", xn1qbo[t0:t0 + 128, :], xn1bt[:], waits=[t_cb1])
                addr("xn1bt", t_s3)
                t_sq2 = p.op("scalar", "activation", waits=[t_xn1] + rw("trb") + ww("ssq"),
                             out=trb[:], in_=xn1t[:], func=AF.Square, accum_out=ssq[:])
                setw("trb", t_sq2)
                setw("ssq", t_sq2)
                t_vv = p.op("vector", "tensor_scalar", waits=[t_sq2] + ww("var"), out=var[:],
                            in0=ssq[:], scalar1=1.0 / D, scalar2=1.1920929e-07,
                            op0=ALU.mult, op1=ALU.add)
                setw("var", t_vv)
                addr("ssq", t_vv)
                t_sd2 = p.op("scalar", "activation", waits=[t_vv] + ww("sstd"), out=sstd[:],
                             in_=var[:], func=AF.Sqrt)
                setw("sstd", t_sd2)
                addr("var", t_sd2)
                t_rs2 = p.op("vector", "reciprocal", waits=[t_sd2] + ww("rstd"), out=rstd[:], in_=sstd[:])
                setw("rstd", t_rs2)
                addr("sstd", t_rs2)
                t_z1 = p.op("vector", "tensor_scalar", waits=[t_rs2] + rw("xn1t") + ww("x1t"),
                            out=x1t[:], in0=xn1t[:], scalar1=rstd[:], scalar2=None, op0=ALU.mult)
                setw("x1t", t_z1)
                addr("xn1t", t_z1)
                addr("rstd", t_z1)
                t_z2 = p.op("vector", "tensor_tensor", waits=[t_z1] + rw("nw_bc") + rw("z1bt"),
                            out=z1bt[:], in0=x1t[:], in1=nw_bc[:], op=ALU.mult)
                setw("z1bt", t_z2)
                addr("x1t", t_z2)
                addr("nw_bc", t_z2)
                t_s4 = p.dma("gpsimd", z1qbo[t0:t0 + 128, :], z1bt[:], waits=[t_z2])
                addr("z1bt", t_s4)
        p.waits("gpsimd", p.barrier_tokens())
        p.finalize()
    return nc


def run_C(x0, xn0, xn0b, yb, params):
    nc = _get("C", build_launch_C)
    r = np.arange(128)
    eye = np.eye(128, dtype=np.float32)
    selg = np.zeros((128, 32 * 128), np.float32)
    for s in range(32):
        for m in range(128):
            selg[s * 4 + m // 32, s * 128 + m] = 1.0
    csts = np.zeros((128, 4), np.float32)
    csts[:, 0] = -0.5
    consts = {
        "identb": cbf(eye), "ones1": f32(np.ones((1, 128))), "csts": csts,
        "selg": cbf(selg),
    }
    vdfm = f32(np.transpose(params["L0_ffn_values_down"], (1, 0, 2)).reshape(D, 8192))
    vuall = f32(params["L0_ffn_values_up"].reshape(8192, D))
    in_maps = []
    for c in range(NCORE):
        b, q = divmod(c, 4)
        tsl = slice(q * Q, (q + 1) * Q)
        im = {
            "x0q": f32(x0[b, tsl]),
            "xn0q": f32(xn0[b, tsl]),
            "xnfm": np.ascontiguousarray(xn0b[b, tsl].T),
            "yfm": np.ascontiguousarray(yb[b, tsl].T),
            "wout": f32(params["L0_attn_w_out"]),
            "hwT": f32(params["L0_ffn_hasher_w"].T),
            "keysT": f32(params["L0_ffn_keys"].T),
            "vdfm": vdfm,
            "vuall": vuall,
            "scaleb": f32(np.broadcast_to(params["L0_ffn_scale"].reshape(1, 1), (128, 1))),
            "normw1": f32(params["L1_attn_norm_w"][None, :]),
        }
        im.update(consts)
        in_maps.append(im)
    res = run_bass_kernel_spmd(nc, in_maps, core_ids=list(range(NCORE)))
    global _last_res
    _last_res = res
    outs = {}
    for nm, dt in (("x1q", np.float32), ("xn1q", np.float32), ("xn1qb", BF), ("z1qb", BF)):
        a = np.empty((B, T, D), dt)
        for c in range(NCORE):
            b, q = divmod(c, 4)
            a[b, q * Q:(q + 1) * Q] = res.results[c][nm]
        outs[nm] = a
    return outs





def build_launch_D():
    nc = new_nc()
    ap = lambda n, s, d, k: nc.dram_tensor(n, s, d, kind=k).ap()
    z1fm = ap("z1fm", [D, T], BF16, "ExternalInput")
    wq4 = ap("wq4", [D, 256], F32, "ExternalInput")
    wk4 = ap("wk4", [D, 256], F32, "ExternalInput")
    wv4 = ap("wv4", [D, 256], F32, "ExternalInput")
    wg4h = ap("wg4h", [D, 4], F32, "ExternalInput")
    identb_i = ap("identb", [128, 128], BF16, "ExternalInput")
    ones164_i = ap("ones164", [1, 64], F32, "ExternalInput")
    maskd4_i = ap("maskd4", [128, 2048], BF16, "ExternalInput")
    csts_i = ap("csts", [128, 4], F32, "ExternalInput")
    ogo = ap("og", [256, T], BF16, "ExternalOutput")
    gates_dram = nc.dram_tensor("gates_dram", [4, T], F32).ap()

    NT5 = T // 512
    with ExitStack() as ctx:
        E = ctx.enter_context
        p = Prog(nc, ctx)
        pool = PsumPool(nc, ctx)
        sb = lambda n, s, d: E(nc.sbuf_tensor(n, s, d))
        identb = sb("identb_s", [128, 128], BF16)
        ones164 = sb("ones164_s", [1, 64], F32)
        maskd4 = sb("maskd4_s", [128, 2048], BF16)
        csts = sb("csts_s", [128, 4], F32)
        wqb = sb("wqb", [128, 8 * 256], BF16)
        wkb = sb("wkb", [128, 8 * 256], BF16)
        wvb = sb("wvb", [128, 8 * 256], BF16)
        wgb = sb("wgb", [128, 8 * 4], BF16)
        wst = [sb(f"wst{i}", [128, 256], F32) for i in range(2)]
        z1s = [sb(f"z1s{i}", [128, 4096], BF16) for i in range(2)]
        Qf = sb("Qf", [128, T], BF16)
        Kf = sb("Kf", [128, T], BF16)
        vaug = sb("vaug", [128, 64 * 130], BF16)
        Pt = [sb(f"Pt{i}", [128, 1024], BF16) for i in range(2)]
        osb = sb("osb", [64, 512], F32)
        grow = [sb(f"grow{i}", [1, 512], F32) for i in range(2)]
        glr = sb("glr", [1, 512], F32)
        linv = sb("linv", [1, 512], F32)
        ogt = sb("ogt", [64, 512], BF16)

        pend_epi = [None]

        def emit_epi(st):
            acc_te, acc_bke, tme, t_sge, gsle, q0e, hhe, pre = st
            t_o = p.op("scalar", "copy", waits=[tme] + ww("osb"), out=osb[:],
                       in_=acc_te[0:64, :])
            setw("osb", t_o)
            t_li = p.op("vector", "reciprocal", waits=[tme] + ww("linv"), out=linv[:],
                        in_=acc_te[64:65, :])
            setw("linv", t_li)
            pool.add_reader(acc_bke, t_o)
            pool.add_reader(acc_bke, t_li)
            pool.unpin(acc_bke)
            t_gl = p.op("vector", "tensor_tensor", waits=[t_li, t_sge] + ww("glr"),
                        out=glr[:], in0=grow[gsle][:], in1=linv[:], op=ALU.mult)
            setw("glr", t_gl)
            addr(f"grow{gsle}", t_gl)
            addr("linv", t_gl)
            bb_t, bwts, bb_bk = pool.acquire()
            t_bb = p.op("tensor", "matmul", waits=bwts + [t_gl], out=bb_t[0:64, :],
                        lhsT=ones164[:], rhs=glr[:], start=True, stop=True)
            pool.set_writer(bb_bk, t_bb)
            addr("glr", t_bb)
            t_og = p.op("vector", "tensor_tensor", waits=[t_bb, t_o] + ww("ogt"),
                        out=ogt[:], in0=osb[:], in1=bb_t[0:64, :], op=ALU.mult)
            pool.add_reader(bb_bk, t_og)
            setw("ogt", t_og)
            addr("osb", t_og)
            t_st = p.dma("gpsimd", ogo[(pre * 2 + hhe) * 64:(pre * 2 + hhe) * 64 + 64, q0e:q0e + 512],
                         ogt[:], waits=[t_og])
            addr("ogt", t_st)
        gt2 = sb("gt2", [2, 512], F32)

        st_w, st_r = {}, {}

        def _merge(dct, tok):
            sem, val = tok
            key = id(sem)
            if key not in dct or dct[key][1] < val:
                dct[key] = (sem, val)

        def setw(name, tok, keep=False):
            if not keep or name not in st_w:
                st_w[name] = {}
            if not keep:
                st_r[name] = {}
            _merge(st_w[name], tok)

        def addr(name, tok):
            _merge(st_r.setdefault(name, {}), tok)

        def rw(name):
            return list(st_w.get(name, {}).values())

        def ww(name):
            return list(st_r.get(name, {}).values()) + rw(name)

        tc = []
        for apx, buf in ((identb_i, identb), (ones164_i, ones164), (maskd4_i, maskd4),
                         (csts_i, csts)):
            tc.append(p.dma("sync", buf[:], apx[:, :]))
        for (wsrc, wdst, ncol) in ((wq4, wqb, 256), (wk4, wkb, 256), (wv4, wvb, 256),
                                   (wg4h, wgb, 4)):
            for f in range(8):
                sl = f % 2
                t_l = p.dma("sync", wst[sl][:, 0:ncol], wsrc[f * 128:(f + 1) * 128, :],
                            waits=ww(f"wst{sl}"))
                setw(f"wst{sl}", t_l)
                t_c = p.op("vector", "tensor_copy", waits=[t_l],
                           out=wdst[:, f * ncol:(f + 1) * ncol], in_=wst[sl][:, 0:ncol])
                addr(f"wst{sl}", t_c)
                tc.append(t_c)
        w_init = list(tc)

        for pr in range(2):
            t_ms = p.op("vector", "memset", vaug[:], 1.0, waits=ww("vaug"))
            setw("vaug", t_ms)
            for tt in range(NT5):
                q0 = tt * 512
                sl = tt % 2
                t_z = p.dma("sync", z1s[sl][:].rearrange("p (f t) -> p f t", f=8),
                            z1fm.rearrange("(f p) t -> p f t", p=128)[:, :, q0:q0 + 512],
                            waits=ww(f"z1s{sl}"))
                setw(f"z1s{sl}", t_z)
                for (wb, dst, dname) in ((wqb, Qf, "Qf"), (wkb, Kf, "Kf")):
                    bank_t, wts, bk = pool.acquire()
                    tm = None
                    for f in range(8):
                        tm = p.op("tensor", "matmul",
                                  waits=(wts + [t_z] + (w_init if (pr == 0 and tt == 0 and f == 0) else []) if f == 0 else []),
                                  out=bank_t[:],
                                  lhsT=wb[:, f * 256 + pr * 128: f * 256 + pr * 128 + 128],
                                  rhs=z1s[sl][:, f * 512:(f + 1) * 512],
                                  start=(f == 0), stop=(f == 7))
                    pool.set_writer(bk, tm)
                    addr(f"z1s{sl}", tm)
                    te = p.op("scalar", "copy", waits=[tm] + (ww(dname) if tt == 0 else []),
                              out=dst[:, q0:q0 + 512], in_=bank_t[:])
                    pool.add_reader(bk, te)
                    setw(dname, te, keep=(tt > 0))
                for sub in range(4):
                    st_i = tt * 4 + sub
                    bank_t, wts, bk = pool.acquire()
                    tm = None
                    for f in range(8):
                        tm = p.op("tensor", "matmul", waits=(wts + [t_z] if f == 0 else []),
                                  out=bank_t[:, 0:128],
                                  lhsT=z1s[sl][:, f * 512 + sub * 128: f * 512 + sub * 128 + 128],
                                  rhs=wvb[:, f * 256 + pr * 128: f * 256 + pr * 128 + 128],
                                  start=(f == 0), stop=(f == 7))
                    pool.set_writer(bk, tm)
                    te = p.op("vector", "tensor_copy", waits=[tm] + rw("vaug"),
                              out=vaug[:, st_i * 130:st_i * 130 + 130].rearrange(
                                  "p (h c) -> p h c", h=2)[:, :, 0:64],
                              in_=bank_t[:, 0:128].rearrange("p (h c) -> p h c", h=2))
                    pool.add_reader(bk, te)
                    setw("vaug", te, keep=True)
                bank_t, wts, bk = pool.acquire()
                tm = None
                for f in range(8):
                    tm = p.op("tensor", "matmul", waits=(wts + [t_z] if f == 0 else []),
                              out=bank_t[0:2, :],
                              lhsT=wgb[:, f * 4 + pr * 2: f * 4 + pr * 2 + 2],
                              rhs=z1s[sl][:, f * 512:(f + 1) * 512],
                              start=(f == 0), stop=(f == 7))
                pool.set_writer(bk, tm)
                te = p.op("vector", "tensor_copy", waits=[tm] + ww("gt2"), out=gt2[:],
                          in_=bank_t[0:2, :])
                pool.add_reader(bk, te)
                setw("gt2", te)
                t_gs = p.dma("gpsimd", gates_dram[pr * 2:pr * 2 + 2, q0:q0 + 512], gt2[:],
                             waits=[te] + (ww("gates_dram") if tt == 0 else []))
                addr("gt2", t_gs)
                setw("gates_dram", t_gs, keep=(tt > 0))
            for hh in range(2):
                rsl = slice(hh * 64, (hh + 1) * 64)
                for qt in range(NT5):
                    q0 = qt * 512
                    gsl = qt % 2
                    t_g = p.dma("sync", grow[gsl][:], gates_dram[pr * 2 + hh:pr * 2 + hh + 1, q0:q0 + 512],
                                waits=ww(f"grow{gsl}") + rw("gates_dram"))
                    setw(f"grow{gsl}", t_g)
                    addr("gates_dram", t_g)
                    t_sg = p.op("scalar", "activation", waits=[t_g], out=grow[gsl][:], in_=grow[gsl][:],
                                func=AF.Sigmoid)
                    setw(f"grow{gsl}", t_sg)
                    acc_t, awts, acc_bk = pool.acquire(pin=True)
                    ng = (q0 // 128 + 4 + 1) // 2
                    first_mm = [True]

                    def emit_scores(gidx):
                        kb0 = gidx * 2
                        sc_t, swts, sc_bk = pool.acquire()
                        sc_t2, swts2, sc_bk2 = pool.acquire()
                        banks = [(sc_t, sc_bk, swts), (sc_t2, sc_bk2, swts2)]
                        exp_waits = []
                        for half in range(2):
                            kb = kb0 + half
                            if kb * 128 >= q0 + 512:
                                continue
                            bt, bkx, bwts = banks[half]
                            tm = p.op("tensor", "matmul",
                                      waits=bwts + (rw("Kf") + rw("Qf") if (gidx == 0 and qt == 0) else []),
                                      out=bt[:], lhsT=Kf[rsl, kb * 128:(kb + 1) * 128],
                                      rhs=Qf[rsl, q0:q0 + 512], start=True,
                                      stop=(kb * 128 + 128 <= q0))
                            if gidx == 0:
                                addr("Kf", tm)
                                addr("Qf", tm)
                            if kb * 128 + 128 > q0:
                                m = kb - q0 // 128
                                tm = p.op("tensor", "matmul", out=bt[:],
                                          lhsT=identb[:], rhs=maskd4[:, m * 512:(m + 1) * 512],
                                          start=False, stop=True)
                            pool.set_writer(bkx, tm)
                            exp_waits.append(tm)
                        return banks, exp_waits

                    def emit_exp(gidx, banks, exp_waits):
                        nm = f"Pt{gidx % 2}"
                        for half in range(2):
                            kb = gidx * 2 + half
                            if kb * 128 >= q0 + 512:
                                continue
                            bt, bkx, _ = banks[half]
                            t_e = p.op("scalar", "activation",
                                       waits=(exp_waits + ww(nm) if half == 0 else []),
                                       out=Pt[gidx % 2][:, half * 512:(half + 1) * 512], in_=bt[:],
                                       func=AF.Exp, scale=csts[:, 0:1])
                            pool.add_reader(bkx, t_e)
                            setw(nm, t_e, keep=(half == 1))

                    def emit_av(gidx):
                        nm = f"Pt{gidx % 2}"
                        tm = None
                        for half in range(2):
                            kb = gidx * 2 + half
                            if kb * 128 >= q0 + 512:
                                continue
                            tm = p.op("tensor", "matmul",
                                      waits=(awts if first_mm[0] else []) + rw(nm) + (rw("vaug") if (qt == 0 and gidx == 0) else []),
                                      out=acc_t[0:65, :],
                                      lhsT=vaug[:, kb * 130 + hh * 65: kb * 130 + hh * 65 + 65],
                                      rhs=Pt[gidx % 2][:, half * 512:(half + 1) * 512],
                                      start=first_mm[0], stop=(kb == q0 // 128 + 3))
                            addr(nm, tm)
                            addr("vaug", tm)
                            first_mm[0] = False
                        return tm

                    pend = None
                    tm = None
                    for gidx in range(ng):
                        banks, exp_waits = emit_scores(gidx)
                        emit_exp(gidx, banks, exp_waits)
                        if gidx == 0 and pend_epi[0] is not None:
                            emit_epi(pend_epi[0])
                            pend_epi[0] = None
                        if pend is not None:
                            tm = emit_av(pend)
                        pend = gidx
                    tm = emit_av(pend)
                    pool.set_writer(acc_bk, tm)
                    pend_epi[0] = (acc_t, acc_bk, tm, t_sg, gsl, q0, hh, pr)
                if pend_epi[0] is not None:
                    emit_epi(pend_epi[0])
                    pend_epi[0] = None
        p.waits("gpsimd", p.barrier_tokens())
        p.finalize()
    return nc


def run_D(z1b, params):
    nc = _get("D", build_launch_D)
    eye = np.eye(128, dtype=np.float32)
    r = np.arange(128)
    maskd = np.zeros((128, 2048), np.float32)
    for m in range(4):
        for j in range(4):
            blk = maskd[:, m * 512 + j * 128: m * 512 + (j + 1) * 128]
            if j < m:
                blk[:] = NEG
            elif j == m:
                blk[:] = np.where(r[:, None] > r[None, :], NEG, 0.0)
    csts = np.zeros((128, 4), np.float32)
    csts[:, 0] = 0.125
    consts = {
        "identb": cbf(eye), "ones164": f32(np.ones((1, 64))),
        "maskd4": cbf(maskd), "csts": csts,
    }
    z1fm = [np.ascontiguousarray(z1b[b].T) for b in range(B)]
    in_maps = []
    for c in range(NCORE):
        b = c // 4
        hg = (c % 4) * 4
        im = {
            "z1fm": z1fm[b],
            "wq4": f32(params["L1_attn_Wq"][:, hg * 64:(hg + 4) * 64]),
            "wk4": f32(params["L1_attn_Wkv"][:, hg * 64:(hg + 4) * 64]),
            "wv4": f32(params["L1_attn_Wkv"][:, 1024 + hg * 64:1024 + (hg + 4) * 64]),
            "wg4h": f32(params["L1_attn_Wg"][:, hg:hg + 4]),
        }
        im.update(consts)
        in_maps.append(im)
    res = run_bass_kernel_spmd(nc, in_maps, core_ids=list(range(NCORE)))
    global _last_res
    _last_res = res
    og = np.empty((B, T, 1024), BF)
    for c in range(NCORE):
        b = c // 4
        hg = (c % 4) * 4
        og[b, :, hg * 64:(hg + 4) * 64] = res.results[c]["og"].T
    return og





def build_launch_E():
    nc = new_nc()
    ap = lambda n, s, d, k: nc.dram_tensor(n, s, d, kind=k).ap()
    x1q = ap("x1q", [Q, D], F32, "ExternalInput")
    xnfm = ap("xnfm", [D, Q], BF16, "ExternalInput")
    ofm = ap("ofm", [D, Q], BF16, "ExternalInput")
    wout1 = ap("wout1", [D, D], F32, "ExternalInput")
    sgwT = ap("sgwT", [D, 2048], F32, "ExternalInput")
    suwT = ap("suwT", [D, 2048], F32, "ExternalInput")
    sdwT = ap("sdwT", [2048, D], F32, "ExternalInput")
    gwT = ap("gwT", [D, 4096], F32, "ExternalInput")
    uwT = ap("uwT", [D, 4096], F32, "ExternalInput")
    dwT = ap("dwT", [4096, D], F32, "ExternalInput")
    rwT = ap("rwT", [D, 16], F32, "ExternalInput")
    cbias = ap("cbias", [128, 16], F32, "ExternalInput")
    identb_i = ap("identb", [128, 128], BF16, "ExternalInput")
    sel16_i = ap("sel16", [16, 16 * 128], BF16, "ExternalInput")
    x2qo = ap("x2q", [Q, D], F32, "ExternalOutput")
    sg_bf = nc.dram_tensor("sg_bf", [D, 2048], BF16).ap()
    su_bf = nc.dram_tensor("su_bf", [D, 2048], BF16).ap()
    gw_bf = nc.dram_tensor("gw_bf", [D, 4096], BF16).ap()
    uw_bf = nc.dram_tensor("uw_bf", [D, 4096], BF16).ap()
    dw_bf = nc.dram_tensor("dw_bf", [4096, D], BF16).ap()

    N5 = Q // 512
    with ExitStack() as ctx:
        E = ctx.enter_context
        p = Prog(nc, ctx)
        pool = PsumPool(nc, ctx)
        sb = lambda n, s, d: E(nc.sbuf_tensor(n, s, d))
        identb = sb("identb_s", [128, 128], BF16)
        sel16 = sb("sel16_s", [16, 16 * 128], BF16)
        woutb = sb("woutb", [128, 8 * 1024], BF16)
        rwTb = sb("rwTb", [128, 8 * 16], BF16)
        cb_bc = sb("cb_bc", [128, 16], F32)
        sdb = sb("sdb", [128, 16 * 1024], BF16)
        wst = [sb(f"wst{i}", [128, 512], F32) for i in range(2)]
        cst = [sb(f"cst{i}", [128, 512], BF16) for i in range(2)]
        xnf = sb("xnf", [128, 8 * Q], BF16)
        ot = [sb(f"ot{i}", [128, 1024], BF16) for i in range(2)]
        wsg = [sb(f"wsg{i}", [128, 1024], BF16) for i in range(2)]
        wsu = [sb(f"wsu{i}", [128, 1024], BF16) for i in range(2)]
        wge = [sb(f"wge{i}", [128, 2048], BF16) for i in range(2)]
        wue = [sb(f"wue{i}", [128, 2048], BF16) for i in range(2)]
        wde = [sb(f"wde{i}", [128, 2 * 1024], BF16) for i in range(4)]
        hsh = sb("hsh", [128, 16 * 512], BF16)
        hwe = sb("hwe", [128, 4 * 2 * 512], BF16)
        silp = sb("silp", [128, 512], BF16)
        hxe = sb("hxe", [128, 512], BF16)
        acc = sb("acc", [128, 4096], F32)
        x1t = sb("x1t", [128, 1024], F32)
        x2t = sb("x2t", [128, 1024], F32)
        logt = sb("logt", [128, 16], F32)
        sbt = sb("sbt", [128, 16], F32)
        scor = sb("scor", [128, 16], F32)
        gsc = sb("gsc", [128, 4], F32)
        gm = sb("gm", [128, 1], F32)
        gmask = sb("gmask", [128, 4], F32)
        smask = sb("smask", [128, 16], F32)
        mskd = sb("mskd", [128, 16], F32)
        m4r = sb("m4r", [128, 1], F32)
        selr = sb("selr", [128, 16], F32)
        wr = sb("wr", [128, 16], F32)
        dsum = sb("dsum", [128, 1], F32)
        comb = sb("comb", [128, 16], BF16)
        combf = sb("combf", [16, 512], BF16)
        tr16 = sb("tr16", [128, 16], F32)

        st_w, st_r = {}, {}

        def _merge(dct, tok):
            sem, val = tok
            key = id(sem)
            if key not in dct or dct[key][1] < val:
                dct[key] = (sem, val)

        def setw(name, tok, keep=False):
            if not keep or name not in st_w:
                st_w[name] = {}
            if not keep:
                st_r[name] = {}
            _merge(st_w[name], tok)

        def addr(name, tok):
            _merge(st_r.setdefault(name, {}), tok)

        def rw(name):
            return list(st_w.get(name, {}).values())

        def ww(name):
            return list(st_r.get(name, {}).values()) + rw(name)

        tc = []
        for apx, buf in ((identb_i, identb), (sel16_i, sel16), (cbias, cb_bc)):
            tc.append(p.dma("sync", buf[:], apx[:, :]))
        for f in range(8):
            for hf in range(2):
                sl = (f * 2 + hf) % 2
                t_l = p.dma("sync", wst[sl][:], wout1[f * 128:(f + 1) * 128, hf * 512:(hf + 1) * 512],
                            waits=ww(f"wst{sl}"))
                setw(f"wst{sl}", t_l)
                t_c = p.op("vector", "tensor_copy", waits=[t_l],
                           out=woutb[:, f * 1024 + hf * 512: f * 1024 + hf * 512 + 512],
                           in_=wst[sl][:])
                addr(f"wst{sl}", t_c)
                tc.append(t_c)
        for f in range(8):
            sl = f % 2
            t_l = p.dma("sync", wst[sl][:, 0:16], rwT[f * 128:(f + 1) * 128, :], waits=ww(f"wst{sl}"))
            setw(f"wst{sl}", t_l)
            t_c = p.op("vector", "tensor_copy", waits=[t_l], out=rwTb[:, f * 16:(f + 1) * 16],
                       in_=wst[sl][:, 0:16])
            addr(f"wst{sl}", t_c)
            tc.append(t_c)
        for f in range(16):
            for hf in range(2):
                sl = (f * 2 + hf) % 2
                t_l = p.dma("sync", wst[sl][:], sdwT[f * 128:(f + 1) * 128, hf * 512:(hf + 1) * 512],
                            waits=ww(f"wst{sl}"))
                setw(f"wst{sl}", t_l)
                t_c = p.op("vector", "tensor_copy", waits=[t_l],
                           out=sdb[:, f * 1024 + hf * 512: f * 1024 + hf * 512 + 512],
                           in_=wst[sl][:])
                addr(f"wst{sl}", t_c)
                tc.append(t_c)
        t_x = p.dma("sync", xnf[:].rearrange("p (f t) -> p f t", f=8),
                    xnfm.rearrange("(f p) t -> p f t", p=128))
        tc.append(t_x)
        setw("xnf", t_x)
        for (src_ap, dst_ap, nrow, ncol, nm) in ((sgwT, sg_bf, D, 2048, "sgc"),
                                                 (suwT, su_bf, D, 2048, "suc"),
                                                 (gwT, gw_bf, D, 4096, "gwc"),
                                                 (uwT, uw_bf, D, 4096, "uwc"),
                                                 (dwT, dw_bf, 4096, D, "dwc")):
            for f in range(nrow // 128):
                for cc in range(ncol // 512):
                    sl = (f * (ncol // 512) + cc) % 2
                    t_l = p.dma("sync", wst[sl][:],
                                src_ap[f * 128:(f + 1) * 128, cc * 512:(cc + 1) * 512],
                                waits=ww(f"wst{sl}"))
                    setw(f"wst{sl}", t_l)
                    t_c = p.op("vector", "tensor_copy", waits=[t_l] + ww(f"cst{sl}"),
                               out=cst[sl][:], in_=wst[sl][:])
                    setw(f"cst{sl}", t_c)
                    addr(f"wst{sl}", t_c)
                    t_s = p.dma("gpsimd", dst_ap[f * 128:(f + 1) * 128, cc * 512:(cc + 1) * 512],
                                cst[sl][:], waits=[t_c])
                    addr(f"cst{sl}", t_s)
                    setw(nm, t_s, keep=not (f == 0 and cc == 0))
        w_init = list(tc)

        for t5 in range(N5):
            q0 = t5 * 512
            for sub in range(4):
                t0 = q0 + sub * 128
                osl = (t5 * 4 + sub) % 2
                t_old = p.dma("sync", ot[osl][:].rearrange("p (f t) -> p f t", f=8),
                              ofm.rearrange("(f p) t -> p f t", p=128)[:, :, t0:t0 + 128],
                              waits=ww(f"ot{osl}"))
                setw(f"ot{osl}", t_old)
                for half in range(2):
                    bank_t, wts, bk = pool.acquire()
                    tm = None
                    for f in range(8):
                        tm = p.op("tensor", "matmul",
                                  waits=(wts + [t_old] + (w_init if (t5 == 0 and sub == 0 and half == 0 and f == 0) else []) if f == 0 else []),
                                  out=bank_t[:], lhsT=ot[osl][:, f * 128:(f + 1) * 128],
                                  rhs=woutb[:, f * 1024 + half * 512: f * 1024 + half * 512 + 512],
                                  start=(f == 0), stop=(f == 7))
                    pool.set_writer(bk, tm)
                    addr(f"ot{osl}", tm)
                    te = p.op("vector", "tensor_copy",
                              waits=[tm] + (ww("acc") if (sub == 0 and half == 0) else []),
                              out=acc[:, sub * 1024 + half * 512: sub * 1024 + half * 512 + 512],
                              in_=bank_t[:])
                    pool.add_reader(bk, te)
                    setw("acc", te, keep=not (sub == 0 and half == 0))
            for sub in range(4):
                t0 = q0 + sub * 128
                bank_t, wts, bk = pool.acquire()
                tm = None
                for f in range(8):
                    tm = p.op("tensor", "matmul", waits=(wts + rw("xnf") if f == 0 else []),
                              out=bank_t[:, 0:16],
                              lhsT=xnf[:, f * Q + t0: f * Q + t0 + 128],
                              rhs=rwTb[:, f * 16:(f + 1) * 16],
                              start=(f == 0), stop=(f == 7))
                pool.set_writer(bk, tm)
                addr("xnf", tm)
                t_cp = p.op("vector", "tensor_copy", waits=[tm] + ww("logt"), out=logt[:],
                            in_=bank_t[:, 0:16])
                pool.add_reader(bk, t_cp)
                setw("logt", t_cp)
                t_sc = p.op("scalar", "activation", waits=[t_cp] + ww("scor"), out=scor[:],
                            in_=logt[:], func=AF.Sigmoid)
                setw("scor", t_sc)
                addr("logt", t_sc)
                t_sb = p.op("vector", "tensor_tensor", waits=[t_sc, tc[2]] + ww("sbt"),
                            out=sbt[:], in0=scor[:], in1=cb_bc[:], op=ALU.add)
                setw("sbt", t_sb)
                tprev = t_sb
                for g4 in range(4):
                    gs = slice(g4 * 4, g4 * 4 + 4)
                    t_m1 = p.op("vector", "reduce_max", waits=[tprev] + (ww("gm") if g4 == 0 else []),
                                out=gm[:], in_=sbt[:, gs], axis=mybir.AxisListType.X)
                    setw("gm", t_m1)
                    t_g1 = p.op("vector", "tensor_scalar", waits=[t_m1] + (ww("mskd") if g4 == 0 else []),
                                out=mskd[:, gs], in0=sbt[:, gs], scalar1=gm[:], scalar2=None,
                                op0=ALU.is_ge)
                    setw("mskd", t_g1, keep=(g4 > 0))
                    t_g2 = p.op("vector", "scalar_tensor_tensor", waits=[t_g1],
                                out=mskd[:, gs], in0=mskd[:, gs], scalar=NEG, in1=sbt[:, gs],
                                op0=ALU.mult, op1=ALU.add)
                    setw("mskd", t_g2, keep=True)
                    t_m2 = p.op("vector", "reduce_max", waits=[t_g2] + (ww("gsc") if g4 == 0 else []),
                                out=gsc[:, g4:g4 + 1], in_=mskd[:, gs], axis=mybir.AxisListType.X)
                    setw("gsc", t_m2, keep=(g4 > 0))
                    t_m3 = p.op("vector", "tensor_tensor", waits=[t_m2, t_m1],
                                out=gsc[:, g4:g4 + 1], in0=gsc[:, g4:g4 + 1], in1=gm[:], op=ALU.add)
                    setw("gsc", t_m3, keep=True)
                    addr("gm", t_m3)
                    tprev = t_m3
                t_t1 = p.op("vector", "reduce_max", waits=rw("gsc") + ww("gm"), out=gm[:],
                            in_=gsc[:], axis=mybir.AxisListType.X)
                setw("gm", t_t1)
                t_t2 = p.op("vector", "tensor_scalar", waits=[t_t1] + ww("gmask"), out=gmask[:],
                            in0=gsc[:], scalar1=gm[:], scalar2=None, op0=ALU.is_ge)
                setw("gmask", t_t2)
                t_t3 = p.op("vector", "scalar_tensor_tensor", waits=[t_t2] + ww("tr16"),
                            out=tr16[:, 0:4], in0=gmask[:], scalar=NEG, in1=gsc[:],
                            op0=ALU.mult, op1=ALU.add)
                setw("tr16", t_t3)
                addr("gsc", t_t3)
                addr("gmask", t_t3)
                t_t4 = p.op("vector", "reduce_max", waits=[t_t3] + ww("gm"), out=gm[:],
                            in_=tr16[:, 0:4], axis=mybir.AxisListType.X)
                setw("gm", t_t4)
                addr("tr16", t_t4)
                t_t5x = p.op("vector", "tensor_scalar", waits=[t_t4] + rw("gsc") + ww("gmask"),
                             out=gmask[:], in0=gsc[:], scalar1=gm[:], scalar2=None, op0=ALU.is_ge)
                setw("gmask", t_t5x)
                addr("gm", t_t5x)
                addr("gsc", t_t5x)
                t_sm = None
                for j in range(4):
                    t_sm = p.op("vector", "tensor_copy",
                                waits=[t_t5x] + (ww("smask") if j == 0 else []),
                                out=smask[:].rearrange("p (g j) -> p g j", g=4)[:, :, j],
                                in_=gmask[:])
                    setw("smask", t_sm, keep=(j > 0))
                t_mk = p.op("vector", "tensor_scalar", waits=[t_sm] + ww("mskd"),
                            out=mskd[:], in0=smask[:], scalar1=1.0e30, scalar2=-1.0e30,
                            op0=ALU.mult, op1=ALU.add)
                setw("mskd", t_mk)
                addr("smask", t_mk)
                t_mk2 = p.op("vector", "tensor_tensor", waits=[t_mk] + rw("sbt"), out=mskd[:],
                             in0=mskd[:], in1=sbt[:], op=ALU.add)
                addr("sbt", t_mk2)
                setw("mskd", t_mk2)
                tprev = t_mk2
                for r4 in range(3):
                    t_m = p.op("vector", "reduce_max", waits=[tprev] + ww("m4r"), out=m4r[:],
                               in_=mskd[:], axis=mybir.AxisListType.X)
                    setw("m4r", t_m)
                    t_g = p.op("vector", "tensor_scalar", waits=[t_m] + ww("selr"), out=selr[:],
                               in0=mskd[:], scalar1=m4r[:], scalar2=None, op0=ALU.is_ge)
                    setw("selr", t_g)
                    tprev = p.op("vector", "scalar_tensor_tensor", waits=[t_g], out=mskd[:],
                                 in0=selr[:], scalar=NEG, in1=mskd[:], op0=ALU.mult, op1=ALU.add)
                    setw("mskd", tprev, keep=True)
                    addr("selr", tprev)
                t_m4 = p.op("vector", "reduce_max", waits=[tprev] + ww("m4r"), out=m4r[:],
                            in_=mskd[:], axis=mybir.AxisListType.X)
                setw("m4r", t_m4)
                addr("mskd", t_m4)
                t_mo = p.op("vector", "tensor_scalar",
                            waits=[t_m4] + rw("smask") + ww("tr16"),
                            out=tr16[:], in0=smask[:], scalar1=1.0e30, scalar2=-1.0e30,
                            op0=ALU.mult, op1=ALU.add)
                setw("tr16", t_mo)
                addr("smask", t_mo)
                t_mo2 = p.op("vector", "tensor_tensor", waits=[t_mo] + rw("sbt"), out=tr16[:],
                             in0=tr16[:], in1=sbt[:], op=ALU.add)
                addr("sbt", t_mo2)
                setw("tr16", t_mo2)
                t_se = p.op("vector", "tensor_scalar", waits=[t_mo2] + rw("m4r") + ww("selr"),
                            out=selr[:], in0=tr16[:], scalar1=m4r[:], scalar2=None, op0=ALU.is_ge)
                setw("selr", t_se)
                addr("tr16", t_se)
                addr("m4r", t_se)
                t_wr = p.op("vector", "tensor_tensor", waits=[t_se] + rw("scor") + ww("wr"),
                            out=wr[:], in0=scor[:], in1=selr[:], op=ALU.mult)
                setw("wr", t_wr)
                addr("scor", t_wr)
                addr("selr", t_wr)
                t_dsm = p.op("vector", "tensor_scalar", waits=[t_wr] + ww("dsum") + ww("tr16"),
                             out=tr16[:], in0=wr[:], scalar1=1.0, scalar2=None,
                             op0=ALU.mult, op1=ALU.add, accum_out=dsum[:])
                setw("dsum", t_dsm)
                setw("tr16", t_dsm)
                t_dse = p.op("vector", "tensor_scalar", waits=[t_dsm], out=dsum[:], in0=dsum[:],
                             scalar1=1e-20, scalar2=None, op0=ALU.add)
                setw("dsum", t_dse)
                t_dsi = p.op("vector", "reciprocal", waits=[t_dse], out=dsum[:], in_=dsum[:])
                setw("dsum", t_dsi)
                t_cmb = p.op("vector", "tensor_scalar", waits=[t_dsi] + rw("wr") + ww("comb"),
                             out=comb[:], in0=wr[:], scalar1=dsum[:], scalar2=None, op0=ALU.mult)
                setw("comb", t_cmb)
                addr("wr", t_cmb)
                addr("dsum", t_cmb)
                bank_t, wts, bk = pool.acquire(BF16)
                ttp = p.op("tensor", "transpose", waits=wts + [t_cmb], out=bank_t[0:16, 0:128],
                           in_=comb[:], identity=identb[:])
                pool.set_writer(bk, ttp)
                addr("comb", ttp)
                te = p.op("vector", "tensor_copy",
                          waits=[ttp] + (ww("combf") if sub == 0 else []),
                          out=combf[0:16, sub * 128:(sub + 1) * 128],
                          in_=bank_t[0:16, 0:128])
                pool.add_reader(bk, te)
                setw("combf", te, keep=(sub > 0))
            for hc in range(16):
                sl = hc % 2
                for (wsrc, wbuf, wnm, cnm) in ((sg_bf, wsg, "wsg", "sgc"), (su_bf, wsu, "wsu", "suc")):
                    t_l = p.dma("sync", wbuf[sl][:].rearrange("p (f c) -> p f c", f=8),
                                wsrc.rearrange("(f p) c -> p f c", p=128)[:, :, hc * 128:(hc + 1) * 128],
                                waits=ww(f"{wnm}{sl}") + (rw(cnm) if t5 == 0 else []))
                    setw(f"{wnm}{sl}", t_l)
                bank_g, wts_g, bk_g = pool.acquire()
                tm_g = None
                for f in range(8):
                    tm_g = p.op("tensor", "matmul",
                                waits=(wts_g + rw(f"wsg{sl}") + rw("xnf") if f == 0 else []),
                                out=bank_g[:], lhsT=wsg[sl][:, f * 128:(f + 1) * 128],
                                rhs=xnf[:, f * Q + q0: f * Q + q0 + 512],
                                start=(f == 0), stop=(f == 7))
                pool.set_writer(bk_g, tm_g)
                addr(f"wsg{sl}", tm_g)
                bank_u, wts_u, bk_u = pool.acquire()
                tm_u = None
                for f in range(8):
                    tm_u = p.op("tensor", "matmul",
                                waits=(wts_u + rw(f"wsu{sl}") if f == 0 else []),
                                out=bank_u[:], lhsT=wsu[sl][:, f * 128:(f + 1) * 128],
                                rhs=xnf[:, f * Q + q0: f * Q + q0 + 512],
                                start=(f == 0), stop=(f == 7))
                pool.set_writer(bk_u, tm_u)
                addr(f"wsu{sl}", tm_u)
                t_si = p.op("scalar", "activation", waits=[tm_g] + ww("silp"), out=silp[:],
                            in_=bank_g[:], func=AF.Silu)
                pool.add_reader(bk_g, t_si)
                setw("silp", t_si)
                t_h = p.op("vector", "tensor_tensor", waits=[t_si, tm_u] + ww(f"hsh{hc}"),
                           out=hsh[:, hc * 512:(hc + 1) * 512], in0=silp[:], in1=bank_u[:],
                           op=ALU.mult)
                pool.add_reader(bk_u, t_h)
                setw(f"hsh{hc}", t_h)
                addr("silp", t_h)
            for sub in range(4):
                for half in range(2):
                    bank_t, wts, bk = pool.acquire()
                    tm = None
                    for hc in range(16):
                        tm = p.op("tensor", "matmul",
                                  waits=(wts + rw(f"hsh{hc}") if hc == 0 else rw(f"hsh{hc}")),
                                  out=bank_t[:],
                                  lhsT=hsh[:, hc * 512 + sub * 128: hc * 512 + sub * 128 + 128],
                                  rhs=sdb[:, hc * 1024 + half * 512: hc * 1024 + half * 512 + 512],
                                  start=(hc == 0), stop=(hc == 15))
                        addr(f"hsh{hc}", tm)
                    pool.set_writer(bk, tm)
                    t_ac = p.op("vector", "tensor_tensor", waits=[tm] + ww("acc"),
                                out=acc[:, sub * 1024 + half * 512: sub * 1024 + half * 512 + 512],
                                in0=bank_t[:],
                                in1=acc[:, sub * 1024 + half * 512: sub * 1024 + half * 512 + 512],
                                op=ALU.add)
                    pool.add_reader(bk, t_ac)
                    setw("acc", t_ac, keep=True)
            for eg in range(4):
                for ei in range(4):
                    e = eg * 4 + ei
                    sl = e % 2
                    t_lg = p.dma("sync", wge[sl][:].rearrange("p (f c) -> p f c", f=8),
                                 gw_bf.rearrange("(f p) c -> p f c", p=128)[:, :, e * 256:(e + 1) * 256],
                                 waits=ww(f"wge{sl}") + (rw("gwc") if t5 == 0 else []))
                    setw(f"wge{sl}", t_lg)
                    t_lu = p.dma("sync", wue[sl][:].rearrange("p (f c) -> p f c", f=8),
                                 uw_bf.rearrange("(f p) c -> p f c", p=128)[:, :, e * 256:(e + 1) * 256],
                                 waits=ww(f"wue{sl}") + (rw("uwc") if t5 == 0 else []))
                    setw(f"wue{sl}", t_lu)
                    bank_w, wts_w, bk_w = pool.acquire()
                    t_wb = p.op("tensor", "matmul", waits=wts_w + rw("combf"),
                                out=bank_w[:], lhsT=sel16[:, e * 128:(e + 1) * 128],
                                rhs=combf[0:16, :], start=True, stop=True)
                    pool.set_writer(bk_w, t_wb)
                    addr("combf", t_wb)
                    for c2 in range(2):
                        bank_g, wts_g, bk_g = pool.acquire()
                        tm_g = None
                        for f in range(8):
                            tm_g = p.op("tensor", "matmul",
                                        waits=(wts_g + [t_lg] if f == 0 else []),
                                        out=bank_g[:],
                                        lhsT=wge[sl][:, f * 256 + c2 * 128: f * 256 + c2 * 128 + 128],
                                        rhs=xnf[:, f * Q + q0: f * Q + q0 + 512],
                                        start=(f == 0), stop=(f == 7))
                        pool.set_writer(bk_g, tm_g)
                        addr(f"wge{sl}", tm_g)
                        bank_u, wts_u, bk_u = pool.acquire()
                        tm_u = None
                        for f in range(8):
                            tm_u = p.op("tensor", "matmul",
                                        waits=(wts_u + [t_lu] if f == 0 else []),
                                        out=bank_u[:],
                                        lhsT=wue[sl][:, f * 256 + c2 * 128: f * 256 + c2 * 128 + 128],
                                        rhs=xnf[:, f * Q + q0: f * Q + q0 + 512],
                                        start=(f == 0), stop=(f == 7))
                        pool.set_writer(bk_u, tm_u)
                        addr(f"wue{sl}", tm_u)
                        t_si = p.op("scalar", "activation", waits=[tm_g] + ww("silp"), out=silp[:],
                                    in_=bank_g[:], func=AF.Silu)
                        pool.add_reader(bk_g, t_si)
                        setw("silp", t_si)
                        t_hx = p.op("vector", "tensor_tensor", waits=[t_si, tm_u] + ww("hxe"),
                                    out=hxe[:], in0=silp[:], in1=bank_u[:], op=ALU.mult)
                        pool.add_reader(bk_u, t_hx)
                        setw("hxe", t_hx)
                        addr("silp", t_hx)
                        t_hw = p.op("vector", "tensor_tensor",
                                    waits=[t_hx, t_wb] + ww(f"hwe_{ei}_{c2}"),
                                    out=hwe[:, (ei * 2 + c2) * 512:(ei * 2 + c2 + 1) * 512],
                                    in0=hxe[:], in1=bank_w[:], op=ALU.mult)
                        pool.add_reader(bk_w, t_hw)
                        setw(f"hwe_{ei}_{c2}", t_hw)
                        addr("hxe", t_hw)
                    t_ld = p.dma("sync", wde[ei][:, 0:1024],
                                 dw_bf[e * 256:e * 256 + 128, :],
                                 waits=ww(f"wde{ei}") + (rw("dwc") if t5 == 0 else []))
                    setw(f"wde{ei}", t_ld)
                    t_ld2 = p.dma("sync", wde[ei][:, 1024:2048],
                                  dw_bf[e * 256 + 128:e * 256 + 256, :], waits=[])
                    setw(f"wde{ei}", t_ld2, keep=True)
                for sub in range(4):
                    for half in range(2):
                        bank_t, wts, bk = pool.acquire()
                        tm = None
                        nmm = 0
                        for ei in range(4):
                            for c2 in range(2):
                                tm = p.op("tensor", "matmul",
                                          waits=((wts if nmm == 0 else []) + rw(f"hwe_{ei}_{c2}") +
                                                 rw(f"wde{ei}")),
                                          out=bank_t[:],
                                          lhsT=hwe[:, (ei * 2 + c2) * 512 + sub * 128:
                                                   (ei * 2 + c2) * 512 + sub * 128 + 128],
                                          rhs=wde[ei][:, c2 * 1024 + half * 512:
                                                      c2 * 1024 + half * 512 + 512],
                                          start=(nmm == 0), stop=(nmm == 7))
                                addr(f"hwe_{ei}_{c2}", tm)
                                addr(f"wde{ei}", tm)
                                nmm += 1
                        pool.set_writer(bk, tm)
                        t_ac = p.op("vector", "tensor_tensor", waits=[tm] + ww("acc"),
                                    out=acc[:, sub * 1024 + half * 512: sub * 1024 + half * 512 + 512],
                                    in0=bank_t[:],
                                    in1=acc[:, sub * 1024 + half * 512: sub * 1024 + half * 512 + 512],
                                    op=ALU.add)
                        pool.add_reader(bk, t_ac)
                        setw("acc", t_ac, keep=True)
            for sub in range(4):
                t0 = q0 + sub * 128
                t_l1 = p.dma("sync", x1t[:], x1q[t0:t0 + 128, :], waits=ww("x1t"))
                setw("x1t", t_l1)
                t_x2 = p.op("vector", "tensor_tensor", waits=[t_l1] + rw("acc") + ww("x2t"),
                            out=x2t[:], in0=acc[:, sub * 1024:(sub + 1) * 1024], in1=x1t[:],
                            op=ALU.add)
                setw("x2t", t_x2)
                addr("acc", t_x2)
                addr("x1t", t_x2)
                t_st = p.dma("gpsimd", x2qo[t0:t0 + 128, :], x2t[:], waits=[t_x2])
                addr("x2t", t_st)
        p.waits("gpsimd", p.barrier_tokens())
        p.finalize()
    return nc


def run_E(x1, xn1b, og, params):
    nc = _get("E", build_launch_E)
    eye = np.eye(128, dtype=np.float32)
    sel16 = np.zeros((16, 16 * 128), np.float32)
    for e in range(16):
        sel16[e, e * 128:(e + 1) * 128] = 1.0
    consts = {"identb": cbf(eye), "sel16": cbf(sel16)}
    gwT = f32(np.transpose(params["L1_ffn_gate_w"], (1, 0, 2)).reshape(D, 4096))
    uwT = f32(np.transpose(params["L1_ffn_up_w"], (1, 0, 2)).reshape(D, 4096))
    dwT = f32(params["L1_ffn_down_w"].reshape(4096, D))
    in_maps = []
    for c in range(NCORE):
        b, q = divmod(c, 4)
        tsl = slice(q * Q, (q + 1) * Q)
        im = {
            "x1q": f32(x1[b, tsl]),
            "xnfm": np.ascontiguousarray(xn1b[b, tsl].T),
            "ofm": np.ascontiguousarray(og[b, tsl].T),
            "wout1": f32(params["L1_attn_Wout"]),
            "sgwT": f32(params["L1_ffn_shared_gate_w"].T),
            "suwT": f32(params["L1_ffn_shared_up_w"].T),
            "sdwT": f32(params["L1_ffn_shared_down_w"].T),
            "gwT": gwT, "uwT": uwT, "dwT": dwT,
            "rwT": f32(params["L1_ffn_router_w"].T),
            "cbias": f32(np.broadcast_to(params["L1_ffn_correction_bias"][None, :], (128, 16))),
        }
        im.update(consts)
        in_maps.append(im)
    res = run_bass_kernel_spmd(nc, in_maps, core_ids=list(range(NCORE)))
    global _last_res
    _last_res = res
    x2 = np.empty((B, T, D), np.float32)
    for c in range(NCORE):
        b, q = divmod(c, 4)
        x2[b, q * Q:(q + 1) * Q] = res.results[c]["x2q"]
    return x2


def kernel(**inputs):
    params = inputs
    x0 = inputs["L0_attn_x"]
    xn0, xn0b = run_A(x0)
    yb = run_B(xn0b, params)
    c_outs = run_C(x0, xn0, xn0b, yb, params)
    og = run_D(c_outs["z1qb"], params)
    x2 = run_E(c_outs["x1q"], c_outs["xn1qb"], og, params)
    return x2


if __name__ == "__main__":
    pass
